# version 20
# baseline (speedup 1.0000x reference)
import numpy as np
import ml_dtypes
import concourse.bass as bass
import concourse.mybir as mybir
from concourse.bass_utils import run_bass_kernel_spmd

F32 = mybir.dt.float32
BF16 = mybir.dt.bfloat16
AF = mybir.ActivationFunctionType
ALU = mybir.AluOpType
AX = mybir.AxisListType

T = 4096
D = 1024
NT = T // 128
FC = D // 128
GAMMA0 = 1.0 / 32.0
DEBUG_C = False
DEBUG_G = 0
D_PAR = True
NDIR = 5
D_W1DB = True
DEBUG_D = False
NEG_BIG = -1.0e30

ENGINES = ("sync", "tensor", "scalar", "vector", "gpsimd")


class Prog:

    def __init__(self, nc):
        self.nc = nc
        self.ops = []
        self.lw = {}
        self.rd = {}
        self.gcount = {}
        self.gsem = {}
        self.esem = {}
        self.epoch = 0
        self._new_epoch()

    def _new_epoch(self):
        self.epoch += 1
        for e in ENGINES:
            self.esem[e] = self.nc.alloc_semaphore(f"E{e[:2]}{self.epoch}")

    def _gsem(self, g):
        if g not in self.gsem:
            self.gsem[g] = self.nc.alloc_semaphore(f"G{len(self.gsem)}")
        return self.gsem[g]

    def add(self, engine, emit, reads=(), writes=(), dma=None):
        deps = set()
        gwait = {}

        def dep_on(j):
            dj = self.ops[j]
            if dj["dma"] is not None:
                g = dj["dma"]
                gwait[g] = max(gwait.get(g, 0), self.gcount[g])
            else:
                deps.add(j)

        for k in reads:
            for j in self.lw.get(k, {}).values():
                dep_on(j)
        for k in writes:
            for j in self.lw.get(k, {}).values():
                dj = self.ops[j]
                if dma is not None and dj["dma"] == dma:
                    continue
                dep_on(j)
            for j in self.rd.get(k, ()):
                dep_on(j)
        i = len(self.ops)
        if dma is not None:
            self.gcount[dma] = self.gcount.get(dma, 0) + 1
            self._gsem(dma)
        op = {"engine": engine, "emit": emit, "deps": deps, "dma": dma,
              "esem": self.esem[engine], "inc": False, "msv": None,
              "gwait": gwait}
        self.ops.append(op)
        for k in reads:
            self.rd.setdefault(k, []).append(i)
        for k in writes:
            tag = "dma" if dma is not None else engine
            self.lw.setdefault(k, {})[tag] = i
            self.rd[k] = []
        return i

    def region_switch(self, old_key, new_key):
        lw = self.lw.setdefault(new_key, {})
        for tag, j in self.lw.get(old_key, {}).items():
            if tag not in lw or lw[tag] < j:
                lw[tag] = j
        self.rd.setdefault(new_key, []).extend(self.rd.get(old_key, ()))

    def barrier(self):
        last = {}
        for i, op in enumerate(self.ops):
            if op["dma"] is None and op["emit"] is not None:
                last[op["engine"]] = i
        groups = dict(self.gcount)
        for e in ENGINES:
            deps = {i for en, i in last.items() if en != e}
            self.ops.append({"engine": e, "emit": None, "deps": deps,
                             "dma": None, "esem": self.esem[e], "inc": False,
                             "msv": None, "gwait": dict(groups)})
        self.lw = {}
        self.rd = {}
        self._new_epoch()

    def finalize_and_emit(self):
        for op in self.ops:
            for j in op["deps"]:
                dj = self.ops[j]
                if dj["engine"] != op["engine"]:
                    dj["inc"] = True
        counters = {}
        for op in self.ops:
            if op["inc"]:
                s = op["esem"]
                counters[s] = counters.get(s, 0) + 1
                op["msv"] = counters[s]
        for g, c in self.gcount.items():
            assert 16 * c < 60000, f"dma group {g} count {c} too high"
        for s, c in counters.items():
            assert c < 60000, "milestone sem overflow"
        ops = self.ops
        gsem = self.gsem

        eseq = {}
        ecount = {e: 0 for e in ENGINES}
        for i, op in enumerate(ops):
            if op["emit"] is not None:
                ecount[op["engine"]] += 1
                eseq[i] = ecount[op["engine"]]

        with self.nc.Block() as blk:
            def mk_body(ename):
                def body(eng):
                    water = {}
                    drained_upto = 0
                    emitted = 0
                    for op in ops:
                        if op["engine"] != ename:
                            continue
                        need_drain = any(
                            ops[j]["engine"] == ename and ops[j]["emit"] is not None
                            and eseq[j] > drained_upto
                            for j in op["deps"])
                        if need_drain and ename != "tensor":
                            eng.drain()
                            drained_upto = emitted
                        for j in sorted(op["deps"]):
                            dj = ops[j]
                            if dj["engine"] == ename:
                                continue
                            if not dj["inc"]:
                                continue
                            sem, val = dj["esem"], dj["msv"]
                            if water.get(sem.num, 0) < val:
                                eng.wait_ge(sem, val)
                                water[sem.num] = val
                        for g, cnt in sorted(op["gwait"].items()):
                            sem, val = gsem[g], 16 * cnt
                            if water.get(sem.num, 0) < val:
                                eng.wait_ge(sem, val)
                                water[sem.num] = val
                        if op["emit"] is None:
                            continue
                        inst = op["emit"](eng)
                        emitted += 1
                        if op["dma"] is not None:
                            inst.then_inc(gsem[op["dma"]], 16)
                        elif op["inc"]:
                            inst.then_inc(op["esem"], 1)
                return body
            for ename in ENGINES:
                getattr(blk, ename)(mk_body(ename))



def emit_ln_tile(p, env, x_hbm, tt, xin, xn_t, dst_writes, emit_to):
    nc = env["nc"]
    kx = xin.name
    p.add("sync", lambda e, a=x_hbm, b=xin: e.dma_start(
        b[:], a.ap()[tt * 128:(tt + 1) * 128, :]), reads=[x_hbm.name], writes=[kx], dma=kx)
    sq, st = env["sq"], env["stats"]
    ks = st.name
    p.add("vector", lambda e: e.tensor_reduce(st[:, 0:1], xin[:], AX.X, ALU.add),
          reads=[kx], writes=[ks + ":sum"])
    p.add("scalar", lambda e: e.activation(sq[:], xin[:], AF.Square),
          reads=[kx], writes=[sq.name])
    p.add("vector", lambda e: e.tensor_reduce(st[:, 1:2], sq[:], AX.X, ALU.add),
          reads=[sq.name], writes=[ks + ":sq"])
    p.add("vector", lambda e: e.tensor_scalar(st[:, 2:3], st[:, 0:1], 1.0 / D, None, ALU.mult),
          reads=[ks + ":sum"], writes=[ks + ":m"])
    p.add("vector", lambda e: e.tensor_scalar(st[:, 3:4], st[:, 1:2], 1.0 / D, None, ALU.mult),
          reads=[ks + ":sq"], writes=[ks + ":e"])
    p.add("vector", lambda e: e.tensor_tensor(st[:, 4:5], st[:, 2:3], st[:, 2:3], ALU.mult),
          reads=[ks + ":m"], writes=[ks + ":m2"])
    p.add("vector", lambda e: e.tensor_sub(st[:, 5:6], st[:, 3:4], st[:, 4:5]),
          reads=[ks + ":e", ks + ":m2"], writes=[ks + ":v"])
    eps = env["eps_col"]
    p.add("scalar", lambda e: e.activation(st[:, 6:7], st[:, 5:6], AF.Sqrt, bias=eps[:, 0:1]),
          reads=[ks + ":v", eps.name], writes=[ks + ":sd"])
    p.add("vector", lambda e: e.reciprocal(st[:, 7:8], st[:, 6:7]),
          reads=[ks + ":sd"], writes=[ks + ":r"])
    p.add("vector", lambda e: e.tensor_scalar(xn_t[:], xin[:], st[:, 2:3], st[:, 7:8],
                                              ALU.subtract, ALU.mult),
          reads=[kx, ks + ":m", ks + ":r"], writes=[xn_t.name])
    emit_to(p)


def emit_transpose_to(p, env, src_ap, src_key, dst_ap, dst_key, pt, evac_engine="vector"):
    ident = env["ident_bf"]
    p.add("tensor", lambda e: e.transpose(pt[:, 0:128], src_ap, ident[:]),
          reads=[src_key, ident.name], writes=[pt.name])
    p.add(evac_engine, lambda e: (e.tensor_copy(dst_ap, pt[:, 0:128])
                                  if evac_engine != "scalar" else e.copy(dst_ap, pt[:, 0:128])),
          reads=[pt.name], writes=[dst_key])


def emit_load_cast_w(p, env, w_hbm, rows0, rows1, cols0, cols1, dst, dst_off,
                     stage_pair, tag):
    W = cols1 - cols0
    n = (rows1 - rows0) // 128
    for i in range(n):
        stg = stage_pair[i % 2]
        p.add("sync", lambda e, i=i, stg=stg: e.dma_start(
            stg[:, 0:W], w_hbm.ap()[rows0 + i * 128: rows0 + (i + 1) * 128, cols0:cols1]),
            reads=[w_hbm.name], writes=[stg.name], dma=stg.name)
        p.add("vector", lambda e, i=i, stg=stg: e.tensor_copy(
            dst[:, dst_off + i * W: dst_off + (i + 1) * W], stg[:, 0:W]),
            reads=[stg.name], writes=[dst.name + tag])


def emit_hilo_row(p, env, src_hbm, n, row_f, row_hi, row_lo, row_tmp):
    p.add("sync", lambda e: e.dma_start(row_f[:, 0:n], src_hbm.ap()[0:1, 0:n]),
          reads=[src_hbm.name], writes=[row_f.name], dma=row_f.name)
    p.add("vector", lambda e: e.tensor_copy(row_hi[:, 0:n], row_f[:, 0:n]),
          reads=[row_f.name], writes=[row_hi.name])
    p.add("vector", lambda e: e.tensor_copy(row_tmp[:, 0:n], row_hi[:, 0:n]),
          reads=[row_hi.name], writes=[row_tmp.name])
    p.add("vector", lambda e: e.tensor_sub(row_tmp[:, 0:n], row_f[:, 0:n], row_tmp[:, 0:n]),
          reads=[row_f.name, row_tmp.name], writes=[row_tmp.name])
    p.add("vector", lambda e: e.tensor_copy(row_lo[:, 0:n], row_tmp[:, 0:n]),
          reads=[row_tmp.name], writes=[row_lo.name])



SBUF_TOP = 229344

_ESIZE = {F32: 4, BF16: 2}


class BAlloc:

    def __init__(self, nc, base, size):
        self.nc, self.base, self.size = nc, base, size
        self.off = 0

    def alloc_sbuf_tensor(self, name, shape, dtype):
        nb = (shape[-1] * _ESIZE[dtype] + 31) & ~31
        t = self.nc.alloc_sbuf_tensor_at(name, shape, dtype, offset=self.base + self.off)
        self.off += nb
        assert self.off <= self.size, f"arena overflow at {name}: {self.off}/{self.size}"
        return t

    def alloc_sbuf_tensor_at(self, name, shape, dtype, offset):
        return self.nc.alloc_sbuf_tensor_at(name, shape, dtype, offset=offset)

    def suballoc(self, nbytes):
        off = self.base + self.off
        self.off += (nbytes + 31) & ~31
        assert self.off <= self.size
        return off

    def reset(self):
        self.off = 0


def sbuf_arena(al, name, nbytes):
    return al.suballoc(nbytes)


def build_block_a(nc, p, al, Xi, Xo, WqT, WkT, Wv, env):
    sq = al.alloc_sbuf_tensor("a_sq", [128, 1024], F32)
    stats = al.alloc_sbuf_tensor("a_st", [128, 8], F32)
    xn_t2 = [al.alloc_sbuf_tensor(f"a_xnt{i}", [128, 1024], BF16) for i in range(2)]
    xn_fm = al.alloc_sbuf_tensor("a_xnfm", [128, FC * T], BF16)
    P_bf = al.alloc_sbuf_tensor("a_P", [128, 8192], BF16)
    V_tm = al.alloc_sbuf_tensor("a_V", [128, NT * 1024], BF16)
    hh_bf = al.alloc_sbuf_tensor("a_hh", [128, 1024], BF16)
    at_sb = [al.alloc_sbuf_tensor(f"a_at{i}", [128, 128], BF16) for i in range(2)]
    sm = al.alloc_sbuf_tensor("a_sm", [128, 8], F32)
    r1 = sbuf_arena(al, "a_reg1", 16384)
    r2 = sbuf_arena(al, "a_reg2", 16384)
    r3 = sbuf_arena(al, "a_reg3", 8192)
    wq_bf = al.alloc_sbuf_tensor_at("a_wq", [128, 8192], BF16, offset=r1)
    wv_bf = al.alloc_sbuf_tensor_at("a_wv", [128, 8192], BF16, offset=r1)
    s_row = al.alloc_sbuf_tensor_at("a_srow", [128, T], F32, offset=r1)
    wk_bf = al.alloc_sbuf_tensor_at("a_wk", [128, 8192], BF16, offset=r2)
    a_bf = al.alloc_sbuf_tensor_at("a_abf", [128, T], BF16, offset=r2)
    xnew = [al.alloc_sbuf_tensor_at(f"a_xn{i}", [128, 1024], F32, offset=r2 + 8192 + 4096 * i)
            for i in range(2)]
    wstage = [al.alloc_sbuf_tensor_at(f"a_wst{i}", [128, 1024], F32, offset=r3 + 4096 * i)
              for i in range(2)]
    xold = [al.alloc_sbuf_tensor_at(f"a_xo{i}", [128, 1024], F32, offset=r3 + 4096 * i)
            for i in range(2)]
    xin = [al.alloc_sbuf_tensor_at(f"a_xin{i}", [128, 1024], F32, offset=r3 + 4096 * i)
           for i in range(2)]

    q4 = env["psum_q"]
    pA, pB, pS = q4[0], q4[1], [q4[2], q4[3]]
    pT = env["psum_t"]
    pY = env["psum_o"]

    env = dict(env)
    env["sq"] = sq
    env["stats"] = stats

    for tt in range(NT):
        xn_t = xn_t2[0]
        def consume(p, tt=tt, xn_t=xn_t):
            for c in range(FC):
                emit_transpose_to(
                    p, env, xn_t[:, c * 128:(c + 1) * 128], xn_t.name,
                    xn_fm[:, c * T + tt * 128: c * T + (tt + 1) * 128],
                    f"{xn_fm.name}:{tt}",
                    pT[c % 2], evac_engine=("vector" if c % 2 == 0 else "scalar"))
        emit_ln_tile(p, env, Xi, tt, xin[tt % 2], xn_t, None, consume)

    p.region_switch(xin[0].name, wstage[0].name)
    p.region_switch(xin[1].name, wstage[1].name)
    emit_load_cast_w(p, env, WqT, 0, 1024, 0, 1024, wq_bf, 0, wstage, "")
    emit_load_cast_w(p, env, WkT, 0, 1024, 0, 1024, wk_bf, 0, wstage, "")
    for mq in range(8):
        for half in range(2):
            ps = pA if half == 0 else pB
            for dc in range(8):
                p.add("tensor", lambda e, mq=mq, half=half, dc=dc, ps=ps: e.matmul(
                    ps[:, 0:512],
                    wq_bf[:, dc * 1024 + mq * 128: dc * 1024 + (mq + 1) * 128],
                    wk_bf[:, dc * 1024 + half * 512: dc * 1024 + half * 512 + 512],
                    start=(dc == 0), stop=(dc == 7)),
                    reads=[wq_bf.name, wk_bf.name], writes=[ps.name])
            p.add("scalar", lambda e, mq=mq, half=half, ps=ps: e.copy(
                P_bf[:, mq * 1024 + half * 512: mq * 1024 + half * 512 + 512], ps[:, 0:512]),
                reads=[ps.name], writes=[P_bf.name])

    p.region_switch(wq_bf.name, wv_bf.name)
    emit_load_cast_w(p, env, Wv, 0, 1024, 0, 1024, wv_bf, 0, wstage, "")
    for tt in range(NT):
        for half in range(2):
            ps = pA if half == 0 else pB
            for fc in range(FC):
                p.add("tensor", lambda e, tt=tt, half=half, fc=fc, ps=ps: e.matmul(
                    ps[:, 0:512],
                    xn_fm[:, fc * T + tt * 128: fc * T + (tt + 1) * 128],
                    wv_bf[:, fc * 1024 + half * 512: fc * 1024 + half * 512 + 512],
                    start=(fc == 0), stop=(fc == 7)),
                    reads=[f"{xn_fm.name}:{tt}", wv_bf.name], writes=[ps.name])
            p.add("scalar", lambda e, tt=tt, half=half, ps=ps: e.copy(
                V_tm[:, tt * 1024 + half * 512: tt * 1024 + half * 512 + 512], ps[:, 0:512]),
                reads=[ps.name], writes=[f"{V_tm.name}:{tt}"])

    p.region_switch(wv_bf.name, s_row.name)
    p.region_switch(wk_bf.name, a_bf.name)
    p.region_switch(wk_bf.name, xnew[0].name)
    p.region_switch(wk_bf.name, xnew[1].name)
    p.region_switch(wstage[0].name, xold[0].name)
    p.region_switch(wstage[1].name, xold[1].name)
    for qt in range(NT):
        kxo = xold[qt % 2].name
        p.add("gpsimd", lambda e, qt=qt, b=xold[qt % 2]: e.dma_start(
            b[:], Xi.ap()[qt * 128:(qt + 1) * 128, :]), reads=[Xi.name], writes=[kxo], dma=kxo)
        for m in range(8):
            ps = pA if m < 4 else pB
            col = (m % 4) * 128
            for c in range(8):
                p.add("tensor", lambda e, m=m, c=c, ps=ps, col=col, qt=qt: e.matmul(
                    ps[:, col:col + 128],
                    P_bf[:, c * 1024 + m * 128: c * 1024 + (m + 1) * 128],
                    xn_fm[:, c * T + qt * 128: c * T + (qt + 1) * 128],
                    start=(c == 0), stop=(c == 7)),
                    reads=[P_bf.name, f"{xn_fm.name}:{qt}"], writes=[ps.name])
        p.add("vector", lambda e: e.tensor_copy(hh_bf[:, 0:512], pA[:, 0:512]),
              reads=[pA.name], writes=[hh_bf.name])
        p.add("vector", lambda e: e.tensor_copy(hh_bf[:, 512:1024], pB[:, 0:512]),
              reads=[pB.name], writes=[hh_bf.name])
        ng = (qt + 4) // 4
        for g in range(ng):
            ps = pS[g % 2]
            gk = [f"{xn_fm.name}:{g * 4 + j}" for j in range(4)]
            for m in range(8):
                p.add("tensor", lambda e, g=g, m=m, ps=ps: e.matmul(
                    ps[:, 0:512],
                    hh_bf[:, m * 128:(m + 1) * 128],
                    xn_fm[:, m * T + g * 512: m * T + g * 512 + 512],
                    start=(m == 0), stop=(m == 7)),
                    reads=[hh_bf.name] + gk, writes=[ps.name])
            if g == ng - 1:
                mask4 = env["mask4"]
                v = qt % 4
                p.add("vector", lambda e, ps=ps, ng=ng, v=v: e.tensor_tensor(
                    s_row[:, (ng - 1) * 512: ng * 512], ps[:, 0:512],
                    mask4[:, v * 512:(v + 1) * 512], ALU.add),
                    reads=[ps.name, mask4.name], writes=[s_row.name])
            else:
                eng = "vector" if g % 2 == 0 else "scalar"
                p.add(eng, lambda e, g=g, ps=ps, eng=eng: (
                    e.tensor_copy(s_row[:, g * 512:(g + 1) * 512], ps[:, 0:512])
                    if eng == "vector" else e.copy(s_row[:, g * 512:(g + 1) * 512], ps[:, 0:512])),
                    reads=[ps.name], writes=[s_row.name])
        L = ng * 512
        p.add("vector", lambda e, L=L: e.tensor_reduce(
            sm[:, 0:1], s_row[:, 0:L], AX.X, ALU.max, negate=True),
            reads=[s_row.name], writes=[sm.name + ":nm"])
        p.add("vector", lambda e: e.tensor_scalar(sm[:, 1:2], sm[:, 0:1], GAMMA0, None, ALU.mult),
              reads=[sm.name + ":nm"], writes=[sm.name + ":bm"])
        p.add("scalar", lambda e, L=L: e.activation(
            a_bf[:, 0:L], s_row[:, 0:L], AF.Exp, bias=sm[:, 1:2], scale=GAMMA0,
            accum_out=sm[:, 2:3]),
            reads=[s_row.name, sm.name + ":bm"], writes=[a_bf.name, sm.name + ":l"])
        p.add("vector", lambda e: e.reciprocal(sm[:, 3:4], sm[:, 2:3]),
              reads=[sm.name + ":l"], writes=[sm.name + ":il"])
        def emit_at(kb):
            pt = pT[kb % 2]
            ab = at_sb[kb % 2]
            p.add("tensor", lambda e, kb=kb, pt=pt: e.transpose(
                pt[:, 0:128], a_bf[:, kb * 128:(kb + 1) * 128], env["ident_bf"][:]),
                reads=[a_bf.name], writes=[pt.name])
            eng = "vector" if kb % 2 == 0 else "scalar"
            p.add(eng, lambda e, pt=pt, ab=ab, eng=eng: (
                e.tensor_copy(ab[:], pt[:, 0:128]) if eng == "vector"
                else e.copy(ab[:], pt[:, 0:128])),
                reads=[pt.name], writes=[ab.name])
        PIPELINE_Y_A = True
        if PIPELINE_Y_A:
            emit_at(0)
        for kb in range(qt + 1):
            if PIPELINE_Y_A:
                if kb + 1 <= qt:
                    emit_at(kb + 1)
            else:
                emit_at(kb)
            ab = at_sb[kb % 2]
            for half in range(2):
                p.add("tensor", lambda e, kb=kb, half=half, ab=ab, qt=qt: e.matmul(
                    pY[half][:, 0:512], ab[:],
                    V_tm[:, kb * 1024 + half * 512: kb * 1024 + half * 512 + 512],
                    start=(kb == 0), stop=(kb == qt)),
                    reads=[ab.name, f"{V_tm.name}:{kb}"], writes=[pY[half].name])
        kxn = xnew[qt % 2].name
        for half in range(2):
            p.add("vector", lambda e, qt=qt, half=half: e.scalar_tensor_tensor(
                xnew[qt % 2][:, half * 512:half * 512 + 512], pY[half][:, 0:512],
                sm[:, 3:4], xold[qt % 2][:, half * 512:half * 512 + 512],
                ALU.mult, ALU.add),
                reads=[pY[half].name, sm.name + ":il", kxo], writes=[kxn])
        p.add("gpsimd", lambda e, qt=qt: e.dma_start(
            Xo.ap()[qt * 128:(qt + 1) * 128, :], xnew[qt % 2][:]),
            reads=[kxn], writes=[Xo.name], dma="store_" + Xo.name)



def emit_load_cast(p, env, mk_ap, hbm_key, n, W, dst, dst_off, stage_pair, tag=""):
    for i in range(n):
        stg = stage_pair[i % len(stage_pair)]
        ap_i = mk_ap(i)
        p.add("sync", lambda e, ap_i=ap_i, stg=stg: e.dma_start(stg[:, 0:W], ap_i),
              reads=[hbm_key], writes=[stg.name], dma=stg.name)
        p.add("vector", lambda e, i=i, stg=stg: e.tensor_copy(
            dst[:, dst_off + i * W: dst_off + (i + 1) * W], stg[:, 0:W]),
            reads=[stg.name], writes=[dst.name + tag])


def build_block_b(nc, p, al, Xi, Xo, Wr, Wgs, Wus, Wds, Wgr, Wur, Wdr, env):
    xin = [al.alloc_sbuf_tensor(f"b_xin{i}", [128, 1024], F32) for i in range(2)]
    sq = al.alloc_sbuf_tensor("b_sq", [128, 1024], F32)
    stats = al.alloc_sbuf_tensor("b_st", [128, 8], F32)
    xn_t2 = [al.alloc_sbuf_tensor(f"b_xnt{i}", [128, 1024], BF16) for i in range(2)]
    xn_fm = al.alloc_sbuf_tensor("b_xnfm", [128, FC * 2048], BF16)
    acc = al.alloc_sbuf_tensor("b_acc", [128, 16 * 1024], F32)
    wstage = [al.alloc_sbuf_tensor(f"b_wst{i}", [128, 1024], F32) for i in range(2)]
    wg2 = [al.alloc_sbuf_tensor(f"b_wg{i}", [128, 4096], BF16) for i in range(2)]
    wu2 = [al.alloc_sbuf_tensor(f"b_wu{i}", [128, 4096], BF16) for i in range(2)]
    wd2 = [al.alloc_sbuf_tensor(f"b_wd{i}", [128, 4096], BF16) for i in range(2)]
    rw = al.alloc_sbuf_tensor("b_rw", [128, 128], BF16)
    hsb2 = [al.alloc_sbuf_tensor(f"b_h{i}", [128, 2048], BF16) for i in range(2)]
    ssb2 = [al.alloc_sbuf_tensor(f"b_s{i}", [128, 512], BF16) for i in range(2)]
    usb2 = [al.alloc_sbuf_tensor(f"b_u{i}", [128, 512], BF16) for i in range(2)]
    eb = al.alloc_sbuf_tensor("b_e", [128, 256], F32)
    pn = al.alloc_sbuf_tensor("b_pn", [128, 256], F32)
    comb = al.alloc_sbuf_tensor("b_comb", [128, 256], F32)
    t1 = al.alloc_sbuf_tensor("b_t1", [128, 256], F32)
    t2 = al.alloc_sbuf_tensor("b_t2", [128, 256], F32)
    st2 = al.alloc_sbuf_tensor("b_st2", [128, 112], F32)

    q4 = env["psum_q"]
    pR, pG, pU = q4[0], q4[1], q4[2]
    pO = env["psum_o"]
    pT = env["psum_t"]

    env = dict(env)
    env["sq"] = sq
    env["stats"] = stats

    emit_load_cast(p, env, lambda i: Wr.ap()[i * 128:(i + 1) * 128, :],
                   Wr.name, 8, 16, rw, 0, wstage)

    for h in range(2):
        tb = h * 16
        for tl in range(16):
            tt = tb + tl
            xn_t = xn_t2[tl % 2]
            def consume(p, tl=tl, xn_t=xn_t):
                for c in range(FC):
                    emit_transpose_to(
                        p, env, xn_t[:, c * 128:(c + 1) * 128], xn_t.name,
                        xn_fm[:, c * 2048 + tl * 128: c * 2048 + (tl + 1) * 128],
                        f"{xn_fm.name}:{tl}", pT[c % 2],
                        evac_engine=("vector" if c % 2 == 0 else "scalar"))
            emit_ln_tile(p, env, Xi, tt, xin[tl % 2], xn_t, None, consume)
            p.add("gpsimd", lambda e, tt=tt, tl=tl: e.dma_start(
                acc[:, tl * 1024:(tl + 1) * 1024], Xi.ap()[tt * 128:(tt + 1) * 128, :]),
                reads=[Xi.name], writes=[f"b_acc:{tl}"], dma=f"b_accld{tl % 4}")
        for tl in range(16):
            for fc in range(FC):
                p.add("tensor", lambda e, tl=tl, fc=fc: e.matmul(
                    pR[:, tl * 16:(tl + 1) * 16],
                    xn_fm[:, fc * 2048 + tl * 128: fc * 2048 + (tl + 1) * 128],
                    rw[:, fc * 16:(fc + 1) * 16],
                    start=(fc == 0), stop=(fc == 7)),
                    reads=[f"{xn_fm.name}:{tl}", rw.name], writes=[pR.name])
        p.add("scalar", lambda e: e.activation(eb[:], pR[:, 0:256], AF.Exp),
              reads=[pR.name], writes=[eb.name])
        e3 = lambda t: t[:].rearrange("p (a b) -> p a b", b=16)
        bc = lambda t, c0: t[:, c0:c0 + 16].unsqueeze(2).broadcast_to([128, 16, 16])
        p.add("vector", lambda e: e.tensor_reduce(
            st2[:, 0:16].rearrange("p (a b) -> p a b", b=1), e3(eb), AX.X, ALU.add),
            reads=[eb.name], writes=[st2.name + ":lr"])
        p.add("vector", lambda e: e.reciprocal(st2[:, 16:32], st2[:, 0:16]),
              reads=[st2.name + ":lr"], writes=[st2.name + ":ilr"])
        p.add("vector", lambda e: e.tensor_tensor(e3(pn), e3(eb), bc(st2, 16), ALU.mult),
              reads=[eb.name, st2.name + ":ilr"], writes=[pn.name])
        p.add("vector", lambda e: e.tensor_reduce(
            st2[:, 32:48].rearrange("p (a b) -> p a b", b=1), e3(pn), AX.X, ALU.max),
            reads=[pn.name], writes=[st2.name + ":m1"])
        p.add("vector", lambda e: e.tensor_tensor(e3(t1), e3(pn), bc(st2, 32), ALU.is_ge),
              reads=[pn.name, st2.name + ":m1"], writes=[t1.name])
        p.add("vector", lambda e: e.scalar_tensor_tensor(
            e3(t2), e3(t1), NEG_BIG, e3(pn), ALU.mult, ALU.add),
            reads=[t1.name, pn.name], writes=[t2.name])
        p.add("vector", lambda e: e.tensor_reduce(
            st2[:, 48:64].rearrange("p (a b) -> p a b", b=1), e3(t2), AX.X, ALU.max),
            reads=[t2.name], writes=[st2.name + ":m2"])
        p.add("vector", lambda e: e.tensor_tensor(st2[:, 64:80], st2[:, 32:48],
                                                  st2[:, 48:64], ALU.add),
              reads=[st2.name + ":m1", st2.name + ":m2"], writes=[st2.name + ":dn"])
        p.add("vector", lambda e: e.tensor_scalar(st2[:, 80:96], st2[:, 64:80], 1e-9,
                                                  None, ALU.add),
              reads=[st2.name + ":dn"], writes=[st2.name + ":dn2"])
        p.add("vector", lambda e: e.reciprocal(st2[:, 96:112], st2[:, 80:96]),
              reads=[st2.name + ":dn2"], writes=[st2.name + ":idn"])
        p.add("vector", lambda e: e.tensor_tensor(e3(t1), e3(pn), bc(st2, 48), ALU.is_ge),
              reads=[pn.name, st2.name + ":m2"], writes=[t1.name])
        p.add("vector", lambda e: e.tensor_tensor(e3(t2), e3(pn), e3(t1), ALU.mult),
              reads=[pn.name, t1.name], writes=[t2.name])
        p.add("vector", lambda e: e.tensor_tensor(e3(comb), e3(t2), bc(st2, 96), ALU.mult),
              reads=[t2.name, st2.name + ":idn"], writes=[comb.name])

        experts = [("s", n, 4, 512) for n in range(2)] + \
                  [("r", r, 2, 256) for r in range(16)]

        def load_expert(xi):
            kind, ei, nmc, cw = experts[xi]
            wg, wu, wd = wg2[xi % 2], wu2[xi % 2], wd2[xi % 2]
            if kind == "s":
                WG, WU, WD = Wgs, Wus, Wds
            else:
                WG, WU, WD = Wgr, Wur, Wdr
            emit_load_cast(p, env, lambda i, ei=ei, WG=WG: WG.ap()[ei, i * 128:(i + 1) * 128, :],
                           WG.name, 8, cw, wg, 0, wstage, tag=":g")
            emit_load_cast(p, env, lambda i, ei=ei, WU=WU: WU.ap()[ei, i * 128:(i + 1) * 128, :],
                           WU.name, 8, cw, wu, 0, wstage, tag=":u")
            emit_load_cast(p, env, lambda i, ei=ei, WD=WD: WD.ap()[ei, i * 128:(i + 1) * 128, :],
                           WD.name, nmc, 1024, wd, 0, wstage, tag=":d")

        if h == 0:
            load_expert(0)
        for xi, (kind, ei, nmc, cw) in enumerate(experts):
            if xi + 1 < len(experts):
                load_expert(xi + 1)
            elif h == 0:
                load_expert(0)
            wg, wu, wd = wg2[xi % 2], wu2[xi % 2], wd2[xi % 2]
            it = [0]

            def gu_h(tg):
                hsb = hsb2[tg % 2]
                for mc in range(nmc):
                    pGt = q4[1] if it[0] % 2 == 0 else q4[0]
                    pUt = q4[2] if it[0] % 2 == 0 else q4[3]
                    ssb = ssb2[it[0] % 2]
                    usb = usb2[it[0] % 2]
                    it[0] += 1
                    for fc in range(FC):
                        p.add("tensor", lambda e, tg=tg, mc=mc, fc=fc, cw=cw, wg=wg, pGt=pGt: e.matmul(
                            pGt[:, 0:512],
                            wg[:, fc * cw + mc * 128: fc * cw + (mc + 1) * 128],
                            xn_fm[:, fc * 2048 + tg * 512: fc * 2048 + tg * 512 + 512],
                            start=(fc == 0), stop=(fc == 7)),
                            reads=[wg.name + ":g"] + [f"{xn_fm.name}:{tg * 4 + j}" for j in range(4)],
                            writes=[pGt.name])
                    for fc in range(FC):
                        p.add("tensor", lambda e, tg=tg, mc=mc, fc=fc, cw=cw, wu=wu, pUt=pUt: e.matmul(
                            pUt[:, 0:512],
                            wu[:, fc * cw + mc * 128: fc * cw + (mc + 1) * 128],
                            xn_fm[:, fc * 2048 + tg * 512: fc * 2048 + tg * 512 + 512],
                            start=(fc == 0), stop=(fc == 7)),
                            reads=[wu.name + ":u"] + [f"{xn_fm.name}:{tg * 4 + j}" for j in range(4)],
                            writes=[pUt.name])
                    p.add("scalar", lambda e, pGt=pGt, ssb=ssb: e.activation(
                        ssb[:], pGt[:, 0:512], AF.Silu),
                        reads=[pGt.name], writes=[ssb.name])
                    p.add("vector", lambda e, pUt=pUt, usb=usb: e.tensor_copy(
                        usb[:], pUt[:, 0:512]),
                        reads=[pUt.name], writes=[usb.name])
                    p.add("vector", lambda e, mc=mc, ssb=ssb, usb=usb, hsb=hsb: e.tensor_tensor(
                        hsb[:, mc * 512:(mc + 1) * 512], ssb[:], usb[:], ALU.mult),
                        reads=[ssb.name, usb.name], writes=[f"{hsb.name}:{mc}"])

            def down(tg):
                hsb = hsb2[tg % 2]
                for tl2 in range(4):
                    tl = tg * 4 + tl2
                    for mc in range(nmc):
                        for hf in range(2):
                            p.add("tensor", lambda e, mc=mc, hf=hf, tl2=tl2, nmc=nmc, wd=wd, hsb=hsb: e.matmul(
                                pO[hf][:, 0:512],
                                hsb[:, mc * 512 + tl2 * 128: mc * 512 + (tl2 + 1) * 128],
                                wd[:, mc * 1024 + hf * 512: mc * 1024 + hf * 512 + 512],
                                start=(mc == 0), stop=(mc == nmc - 1)),
                                reads=[f"{hsb.name}:{mc}", wd.name + ":d"], writes=[pO[hf].name])
                    for hf in range(2):
                        csl = slice(tl * 1024 + hf * 512, tl * 1024 + hf * 512 + 512)
                        if kind == "s":
                            p.add("vector", lambda e, csl=csl, hf=hf: e.tensor_add(
                                acc[:, csl], acc[:, csl], pO[hf][:, 0:512]),
                                reads=[pO[hf].name, f"b_acc:{tl}"], writes=[f"b_acc:{tl}"])
                        else:
                            p.add("vector", lambda e, csl=csl, hf=hf, tl=tl, ei=ei: e.scalar_tensor_tensor(
                                acc[:, csl], pO[hf][:, 0:512],
                                comb[:, tl * 16 + ei: tl * 16 + ei + 1],
                                acc[:, csl], ALU.mult, ALU.add),
                                reads=[pO[hf].name, comb.name, f"b_acc:{tl}"],
                                writes=[f"b_acc:{tl}"])

            gu_h(0)
            for tg in range(4):
                if tg + 1 < 4:
                    gu_h(tg + 1)
                down(tg)
        for tl in range(16):
            tt = tb + tl
            p.add("gpsimd", lambda e, tt=tt, tl=tl: e.dma_start(
                Xo.ap()[tt * 128:(tt + 1) * 128, :], acc[:, tl * 1024:(tl + 1) * 1024]),
                reads=[f"b_acc:{tl}"], writes=[Xo.name], dma="b_store")



def build_block_c(nc, p, al, Xi, Xmid, Xo, WqT, WkT, WvT, WpT, Bq, Bk, Bv, Bp,
                  CosT, SinT, XnH, IlS, env):
    sq = al.alloc_sbuf_tensor("c_sq", [128, 1024], F32)
    stats = al.alloc_sbuf_tensor("c_st", [128, 8], F32)

    K_fm = al.alloc_sbuf_tensor("c_K", [128, 4 * T], BF16)
    V_tm = al.alloc_sbuf_tensor("c_V", [128, NT * 512], BF16)
    cos_sb = al.alloc_sbuf_tensor("c_cos", [128, 1024], BF16)
    sin_sb = al.alloc_sbuf_tensor("c_sin", [128, 1024], BF16)
    xnt_s = [al.alloc_sbuf_tensor(f"c_xns{i}", [128, 1024], BF16) for i in range(2)]
    xnfm_t2 = [al.alloc_sbuf_tensor(f"c_xnfm{i}", [128, 1024], BF16) for i in range(2)]
    wq_bf = al.alloc_sbuf_tensor("c_wq", [128, 4096], BF16)
    wp_bf = al.alloc_sbuf_tensor("c_wp", [128, 4096], BF16)
    kraw = al.alloc_sbuf_tensor("c_kraw", [128, 512], F32)
    krot2 = [al.alloc_sbuf_tensor(f"c_krot{i}", [128, 512], BF16) for i in range(2)]
    tmpa = al.alloc_sbuf_tensor("c_tma", [128, 256], F32)
    tmpb = al.alloc_sbuf_tensor("c_tmb", [128, 256], F32)
    abf = [al.alloc_sbuf_tensor(f"c_abf{i}", [128, T], BF16) for i in range(4)]
    at_sb = [al.alloc_sbuf_tensor(f"c_at{i}", [128, 512], BF16) for i in range(2)]
    qfm = al.alloc_sbuf_tensor("c_qfm", [128, 2048], BF16)
    yfm = al.alloc_sbuf_tensor("c_yfm", [128, 2048], F32)
    ybf = al.alloc_sbuf_tensor("c_ybf", [128, 2048], BF16)
    il_all = al.alloc_sbuf_tensor("c_il", [128, 32], F32)
    ilT = sq
    smx = al.alloc_sbuf_tensor("c_smx", [128, 160], F32)
    rhl = al.alloc_sbuf_tensor("c_rhl", [1, 6 * 512], BF16)
    bphl = al.alloc_sbuf_tensor("c_bp", [1, 2 * 1024], BF16)
    xnew = [al.alloc_sbuf_tensor(f"c_xn{i}", [128, 1024], F32) for i in range(2)]
    wstage = [al.alloc_sbuf_tensor("c_wst", [128, 1024], F32)]
    rows = sq
    rtmp = sq
    ar1 = sbuf_arena(al, "c_ar1", 8192)
    ar2 = sbuf_arena(al, "c_ar2", 16384)
    xin = [al.alloc_sbuf_tensor_at(f"c_xin{i}", [128, 1024], F32, offset=ar1 + 4096 * i)
           for i in range(2)]
    xold = [al.alloc_sbuf_tensor_at(f"c_xo{i}", [128, 1024], F32, offset=ar1 + 4096 * i)
            for i in range(2)]
    wk_bf = al.alloc_sbuf_tensor_at("c_wk", [128, 4096], BF16, offset=ar2)
    wv_bf = al.alloc_sbuf_tensor_at("c_wv", [128, 4096], BF16, offset=ar2 + 8192)
    s_row = al.alloc_sbuf_tensor_at("c_srow", [128, T], F32, offset=ar2)
    xn_t2 = [al.alloc_sbuf_tensor_at(f"c_xnt{i}", [128, 1024], BF16,
                                     offset=ar2 + 2048 * i) for i in range(2)]

    q4 = env["psum_q"]
    pP, pYH, pS = q4[0], q4[1], [q4[2], q4[3]]
    pT = env["psum_t"]
    pO = env["psum_o"]

    env = dict(env)
    env["sq"] = sq
    env["stats"] = stats
    ones_bf = env["ones_bf"]
    onesf = env["ones_f"]
    ident = env["ident_bf"]

    def hilo_into(src_hbm, c0, n, dsth, dstl, keyh, keyl):
        r0 = sq[0:1, 0:n]
        r1 = sq[0:1, 512:512 + n]
        p.add("sync", lambda e: e.dma_start(r0, src_hbm.ap()[0:1, c0:c0 + n]),
              reads=[src_hbm.name], writes=[sq.name], dma="c_rowld")
        p.add("vector", lambda e: e.tensor_copy(dsth, r0),
              reads=[sq.name], writes=[keyh])
        p.add("vector", lambda e: e.tensor_copy(r1, dsth),
              reads=[keyh], writes=[sq.name])
        p.add("vector", lambda e: e.tensor_sub(r1, r0, r1),
              reads=[sq.name], writes=[sq.name])
        p.add("vector", lambda e: e.tensor_copy(dstl, r1),
              reads=[sq.name], writes=[keyl])

    def hilo(src_hbm, c0, n, slot):
        assert n <= 512
        dsth = rhl[:, slot * 512: slot * 512 + n]
        dstl = rhl[:, (slot + 1) * 512: (slot + 1) * 512 + n]
        keyh = rhl.name + f":{slot}h"
        keyl = rhl.name + f":{slot}l"
        hilo_into(src_hbm, c0, n, dsth, dstl, keyh, keyl)
        return dsth, dstl, [keyh, keyl]

    for tt in range(NT):
        xn_t = xn_t2[tt % 2]
        def consume(p, tt=tt, xn_t=xn_t):
            p.add("gpsimd", lambda e, tt=tt, xn_t=xn_t: e.dma_start(
                XnH.ap()[tt * 128:(tt + 1) * 128, :], xn_t[:]),
                reads=[xn_t.name], writes=[XnH.name], dma="c_xnst")
        emit_ln_tile(p, env, Xi, tt, xin[tt % 2], xn_t, None, consume)

    for nmsb, hbm in ((cos_sb, CosT), (sin_sb, SinT)):
        p.add("sync", lambda e, nmsb=nmsb, hbm=hbm: e.dma_start(
            wstage[0][:].rearrange("p (a b) -> p a b", b=32),
            hbm.ap().rearrange("(a p) b -> p a b", p=128)),
            reads=[hbm.name], writes=[wstage[0].name], dma=wstage[0].name)
        p.add("vector", lambda e, nmsb=nmsb: e.tensor_copy(nmsb[:], wstage[0][:]),
              reads=[wstage[0].name], writes=[nmsb.name])

    def rope(src_f32, dst_bf, tile):
        va = lambda t: t[:].rearrange("p (h x) -> p h x", x=64)[:, :, 0:32]
        vb = lambda t: t[:].rearrange("p (h x) -> p h x", x=64)[:, :, 32:64]
        cosb = cos_sb[:, tile * 32:(tile + 1) * 32].unsqueeze(1).broadcast_to([128, 8, 32])
        sinb = sin_sb[:, tile * 32:(tile + 1) * 32].unsqueeze(1).broadcast_to([128, 8, 32])
        t3 = lambda t: t[:].rearrange("p (h x) -> p h x", x=32)
        ks, kd = src_f32.name, dst_bf.name
        p.add("vector", lambda e: e.tensor_tensor(t3(tmpa), va(src_f32), cosb, ALU.mult),
              reads=[ks, cos_sb.name], writes=[tmpa.name])
        p.add("vector", lambda e: e.tensor_tensor(t3(tmpb), vb(src_f32), sinb, ALU.mult),
              reads=[ks, sin_sb.name], writes=[tmpb.name])
        p.add("vector", lambda e: e.tensor_tensor(va(dst_bf), t3(tmpa), t3(tmpb), ALU.subtract),
              reads=[tmpa.name, tmpb.name], writes=[kd])
        p.add("vector", lambda e: e.tensor_tensor(t3(tmpa), va(src_f32), sinb, ALU.mult),
              reads=[ks, sin_sb.name], writes=[tmpa.name])
        p.add("vector", lambda e: e.tensor_tensor(t3(tmpb), vb(src_f32), cosb, ALU.mult),
              reads=[ks, cos_sb.name], writes=[tmpb.name])
        p.add("vector", lambda e: e.tensor_tensor(vb(dst_bf), t3(tmpa), t3(tmpb), ALU.add),
              reads=[tmpa.name, tmpb.name], writes=[kd])

    def xn_tile_to_fm(tile, stream_slot):
        xs = xnt_s[stream_slot]
        xnfm_t = xnfm_t2[stream_slot]
        p.add("sync", lambda e, tile=tile, xs=xs: e.dma_start(
            xs[:], XnH.ap()[tile * 128:(tile + 1) * 128, :]),
            reads=[XnH.name], writes=[xs.name], dma=xs.name)
        for c in range(FC):
            emit_transpose_to(p, env, xs[:, c * 128:(c + 1) * 128], xs.name,
                              xnfm_t[:, c * 128:(c + 1) * 128], xnfm_t.name,
                              pT[c % 2],
                              evac_engine=("vector" if c % 2 == 0 else "scalar"))
        return xnfm_t

    GQ = 8
    for H in range(2):
        c0 = H * 512
        if H == 0:
            p.region_switch(xin[0].name, xold[0].name)
            p.region_switch(xin[1].name, xold[1].name)
            p.region_switch(xn_t2[0].name, wk_bf.name)
            p.region_switch(xn_t2[1].name, wk_bf.name)
        else:
            p.region_switch(s_row.name, wk_bf.name)
            p.region_switch(s_row.name, wv_bf.name)
        emit_load_cast(p, env, lambda i, c0=c0: WkT.ap()[i * 128:(i + 1) * 128, c0:c0 + 512],
                       WkT.name, 8, 512, wk_bf, 0, wstage)
        emit_load_cast(p, env, lambda i: WvT.ap()[i * 128:(i + 1) * 128, c0:c0 + 512],
                       WvT.name, 8, 512, wv_bf, 0, wstage)
        bk_h, bk_l, bk_keys = hilo(Bk, c0, 512, 0)
        bv_h, bv_l, bv_keys = hilo(Bv, c0, 512, 2)
        for kt in range(NT):
            xnfm_t = xn_tile_to_fm(kt, kt % 2)
            krot = krot2[kt % 2]
            for fc in range(FC):
                p.add("tensor", lambda e, fc=fc, xnfm_t=xnfm_t: e.matmul(
                    pP[:, 0:512], xnfm_t[:, fc * 128:(fc + 1) * 128],
                    wk_bf[:, fc * 512:(fc + 1) * 512],
                    start=(fc == 0), stop=False),
                    reads=[xnfm_t.name, wk_bf.name], writes=[pP.name])
            p.add("tensor", lambda e: e.matmul(pP[:, 0:512], ones_bf[:], bk_h,
                                               start=False, stop=False),
                  reads=[ones_bf.name] + bk_keys, writes=[pP.name])
            p.add("tensor", lambda e: e.matmul(pP[:, 0:512], ones_bf[:], bk_l,
                                               start=False, stop=True),
                  reads=[ones_bf.name] + bk_keys, writes=[pP.name])
            p.add("vector", lambda e, kraw=kraw: e.tensor_copy(kraw[:], pP[:, 0:512]),
                  reads=[pP.name], writes=[kraw.name])
            rope(kraw, krot, kt)
            for c in range(4):
                emit_transpose_to(p, env, krot[:, c * 128:(c + 1) * 128], krot.name,
                                  K_fm[:, c * T + kt * 128: c * T + (kt + 1) * 128],
                                  f"{K_fm.name}:{kt}", pT[c % 2],
                                  evac_engine=("vector" if c % 2 == 0 else "scalar"))
            for fc in range(FC):
                p.add("tensor", lambda e, fc=fc, xnfm_t=xnfm_t: e.matmul(
                    pP[:, 0:512], xnfm_t[:, fc * 128:(fc + 1) * 128],
                    wv_bf[:, fc * 512:(fc + 1) * 512],
                    start=(fc == 0), stop=False),
                    reads=[xnfm_t.name, wv_bf.name], writes=[pP.name])
            p.add("tensor", lambda e: e.matmul(pP[:, 0:512], ones_bf[:], bv_h,
                                               start=False, stop=False),
                  reads=[ones_bf.name] + bv_keys, writes=[pP.name])
            p.add("tensor", lambda e: e.matmul(pP[:, 0:512], ones_bf[:], bv_l,
                                               start=False, stop=True),
                  reads=[ones_bf.name] + bv_keys, writes=[pP.name])
            p.add("scalar", lambda e, kt=kt: e.copy(
                V_tm[:, kt * 512:(kt + 1) * 512], pP[:, 0:512]),
                reads=[pP.name], writes=[f"{V_tm.name}:{kt}"])

        p.region_switch(wk_bf.name, s_row.name)
        p.region_switch(wv_bf.name, s_row.name)
        emit_load_cast(p, env, lambda i, c0=c0: WqT.ap()[i * 128:(i + 1) * 128, c0:c0 + 512],
                       WqT.name, 8, 512, wq_bf, 0, wstage)
        emit_load_cast(p, env, lambda i, c0=c0: WpT.ap()[c0 + i * 128: c0 + (i + 1) * 128, :],
                       WpT.name, 4, 1024, wp_bf, 0, wstage)
        bq_h, bq_l, bq_keys = hilo(Bq, c0, 512, 4)
        if H == 0:
            for hseg in range(2):
                hilo_into(Bp, hseg * 512, 512,
                          bphl[:, hseg * 512: hseg * 512 + 512],
                          bphl[:, 1024 + hseg * 512: 1024 + hseg * 512 + 512],
                          bphl.name + ":h", bphl.name + ":l")
        Xsrc = Xi if H == 0 else Xmid
        Xdst = Xmid if H == 0 else Xo
        for g in range(GQ):
            for qtl in range(4):
                qt = g * 4 + qtl
                xn_tile_to_fm(qt, qtl % 2)
                for fc in range(FC):
                    p.add("tensor", lambda e, fc=fc: e.matmul(
                        pP[:, 0:512], xnfm_t[:, fc * 128:(fc + 1) * 128],
                        wq_bf[:, fc * 512:(fc + 1) * 512],
                        start=(fc == 0), stop=False),
                        reads=[xnfm_t.name, wq_bf.name], writes=[pP.name])
                p.add("tensor", lambda e: e.matmul(pP[:, 0:512], ones_bf[:], bq_h,
                                                   start=False, stop=False),
                      reads=[ones_bf.name] + bq_keys, writes=[pP.name])
                p.add("tensor", lambda e: e.matmul(pP[:, 0:512], ones_bf[:], bq_l,
                                                   start=False, stop=True),
                      reads=[ones_bf.name] + bq_keys, writes=[pP.name])
                p.add("vector", lambda e, kraw=kraw: e.tensor_copy(kraw[:], pP[:, 0:512]),
                      reads=[pP.name], writes=[kraw.name])
                rope(kraw, krot, qt)
                for c in range(4):
                    emit_transpose_to(
                        p, env, krot[:, c * 128:(c + 1) * 128], krot.name,
                        qfm[:, qtl * 512 + c * 128: qtl * 512 + (c + 1) * 128],
                        f"{qfm.name}:{qtl}", pT[c % 2],
                        evac_engine=("vector" if c % 2 == 0 else "scalar"))
            ng = g + 1
            L = ng * 512
            mask4 = env["mask4"]
            slots = [pS[0], pS[1], pP, pO[0], pO[1]]
            mask4 = env["mask4"]

            def chain(hh, qtl):
                c = hh // 2
                pb = (hh % 2) * 64
                qt = g * 4 + qtl
                v = qt % 4
                sb = (qtl + 4 * (hh % 2)) * 20
                kgm = f"{smx.name}:{sb}:gm"
                klsp = f"{smx.name}:{sb}:lsp"
                kbm = f"{smx.name}:{sb}:bm"
                kls = f"{smx.name}:{sb}:ls"
                km = f"{smx.name}:{sb}:m"
                direct = ng <= NDIR
                for g2 in range(ng):
                    ps = slots[(qtl * ng + g2) % len(slots)] if direct else slots[g2 % 3]
                    kk = [f"{K_fm.name}:{4 * g2 + j}" for j in range(4)]
                    p.add("tensor", lambda e, g2=g2, qtl=qtl, c=c, pb=pb, ps=ps: e.matmul(
                        ps[:, 0:512],
                        qfm[pb:pb + 64, qtl * 512 + c * 128: qtl * 512 + (c + 1) * 128],
                        K_fm[pb:pb + 64, c * T + g2 * 512: c * T + g2 * 512 + 512],
                        start=True, stop=True),
                        reads=[f"{qfm.name}:{qtl}"] + kk, writes=[ps.name])
                    if direct:
                        if g2 == ng - 1:
                            p.add("vector", lambda e, ps=ps, v=v: e.tensor_tensor(
                                ps[:, 0:512], ps[:, 0:512],
                                mask4[:, v * 512:(v + 1) * 512], ALU.add),
                                reads=[ps.name, mask4.name], writes=[ps.name])
                        p.add("vector", lambda e, ps=ps, g2=g2, sb=sb: e.tensor_reduce(
                            smx[:, sb + g2: sb + g2 + 1], ps[:, 0:512], AX.X, ALU.max),
                            reads=[ps.name], writes=[kgm])
                    else:
                        if g2 == ng - 1:
                            p.add("vector", lambda e, ps=ps, g2=g2, v=v: e.tensor_tensor(
                                s_row[:, g2 * 512: g2 * 512 + 512], ps[:, 0:512],
                                mask4[:, v * 512:(v + 1) * 512], ALU.add),
                                reads=[ps.name, mask4.name], writes=[s_row.name])
                        else:
                            eng = "vector" if g2 % 2 == 0 else "scalar"
                            p.add(eng, lambda e, g2=g2, ps=ps, eng=eng: (
                                e.tensor_copy(s_row[:, g2 * 512:(g2 + 1) * 512], ps[:, 0:512])
                                if eng == "vector" else
                                e.copy(s_row[:, g2 * 512:(g2 + 1) * 512], ps[:, 0:512])),
                                reads=[ps.name], writes=[s_row.name])
                        p.add("vector", lambda e, g2=g2, sb=sb: e.tensor_reduce(
                            smx[:, sb + g2: sb + g2 + 1],
                            s_row[:, g2 * 512:(g2 + 1) * 512], AX.X, ALU.max),
                            reads=[s_row.name], writes=[kgm])
                p.add("vector", lambda e, sb=sb, ng=ng: e.tensor_reduce(
                    smx[:, sb + 18: sb + 19], smx[:, sb: sb + ng], AX.X, ALU.max),
                    reads=[kgm], writes=[km])
                p.add("vector", lambda e, sb=sb: e.tensor_scalar(
                    smx[:, sb + 16: sb + 17], smx[:, sb + 18: sb + 19], -0.125,
                    None, ALU.mult),
                    reads=[km], writes=[kbm])
                if direct:
                    for g2 in range(ng):
                        ps = slots[(qtl * ng + g2) % len(slots)]
                        p.add("scalar", lambda e, ps=ps, g2=g2, qtl=qtl, sb=sb: e.activation(
                            abf[qtl][:, g2 * 512:(g2 + 1) * 512], ps[:, 0:512],
                            AF.Exp, bias=smx[:, sb + 16: sb + 17], scale=0.125,
                            accum_out=smx[:, sb + 8 + g2: sb + 9 + g2]),
                            reads=[ps.name, kbm],
                            writes=[f"{abf[qtl].name}:{g2}", klsp])
                    p.add("vector", lambda e, sb=sb, ng=ng: e.tensor_reduce(
                        smx[:, sb + 17: sb + 18], smx[:, sb + 8: sb + 8 + ng],
                        AX.X, ALU.add),
                        reads=[klsp], writes=[kls])
                else:
                    L = ng * 512
                    p.add("scalar", lambda e, L=L, qtl=qtl, sb=sb: e.activation(
                        abf[qtl][:, 0:L], s_row[:, 0:L], AF.Exp,
                        bias=smx[:, sb + 16: sb + 17], scale=0.125,
                        accum_out=smx[:, sb + 17: sb + 18]),
                        reads=[s_row.name, kbm],
                        writes=[kls] + [f"{abf[qtl].name}:{gg2}" for gg2 in range(ng)])
                p.add("vector", lambda e, qtl=qtl, hh=hh, sb=sb: e.reciprocal(
                    il_all[:, qtl * 8 + hh: qtl * 8 + hh + 1],
                    smx[:, sb + 17: sb + 18]),
                    reads=[kls], writes=[il_all.name])

            def y_qtl(hh, qtl):
                nkb = 4 * g + qtl + 1
                def emit_at(kb):
                    pt = pT[kb % 2]
                    ab = at_sb[kb % 2]
                    p.add("tensor", lambda e, kb=kb, qtl=qtl, pt=pt: e.transpose(
                        pt[:, qtl * 128:(qtl + 1) * 128],
                        abf[qtl][:, kb * 128:(kb + 1) * 128], ident[:]),
                        reads=[f"{abf[qtl].name}:{kb // 4}", ident.name],
                        writes=[pt.name])
                    eng = "vector" if kb % 2 == 0 else "scalar"
                    p.add(eng, lambda e, pt=pt, ab=ab, qtl=qtl, eng=eng: (
                        e.tensor_copy(ab[:, qtl * 128:(qtl + 1) * 128],
                                      pt[:, qtl * 128:(qtl + 1) * 128])
                        if eng == "vector" else
                        e.copy(ab[:, qtl * 128:(qtl + 1) * 128],
                               pt[:, qtl * 128:(qtl + 1) * 128])),
                        reads=[pt.name], writes=[f"{ab.name}:{qtl}"])
                emit_at(0)
                for kb in range(nkb):
                    if kb + 1 < nkb:
                        emit_at(kb + 1)
                    ab = at_sb[kb % 2]
                    p.add("tensor", lambda e, kb=kb, hh=hh, qtl=qtl, ab=ab, nkb=nkb: e.matmul(
                        pYH[0:64, qtl * 128:(qtl + 1) * 128],
                        V_tm[:, kb * 512 + hh * 64: kb * 512 + (hh + 1) * 64],
                        ab[:, qtl * 128:(qtl + 1) * 128],
                        start=(kb == 0), stop=(kb == nkb - 1)),
                        reads=[f"{V_tm.name}:{kb}", f"{ab.name}:{qtl}"],
                        writes=[pYH.name])

            for qtl in range(4):
                chain(0, qtl)
            for hh in range(8):
                c = hh // 2
                pb = (hh % 2) * 64
                for qtl in range(4):
                    y_qtl(hh, qtl)
                    if hh + 1 < 8:
                        chain(hh + 1, qtl)
                p.add("vector" if hh % 2 == 0 else "scalar",
                      lambda e, hh=hh, c=c, pb=pb: (
                          e.tensor_copy(yfm[pb:pb + 64, c * 512:(c + 1) * 512],
                                        pYH[0:64, 0:512])
                          if hh % 2 == 0 else
                          e.copy(yfm[pb:pb + 64, c * 512:(c + 1) * 512],
                                 pYH[0:64, 0:512])),
                      reads=[pYH.name], writes=[yfm.name])
            identf = env["ident_f"]
            p.add("tensor", lambda e, identf=identf: e.transpose(
                pS[1][0:32, 0:128], il_all[:], identf[:]),
                reads=[il_all.name, identf.name], writes=[pS[1].name])
            p.add("vector", lambda e: e.tensor_copy(sq[64:96, 0:128], pS[1][0:32, 0:128]),
                  reads=[pS[1].name], writes=[sq.name])
            p.add("gpsimd", lambda e: e.dma_start(IlS.ap(), sq[64:96, 0:128]),
                  reads=[sq.name], writes=[IlS.name], dma="c_ils")
            for c in range(4):
                src = IlS.ap().rearrange("(q hh) c -> q hh c", hh=8)[:, 2 * c:2 * c + 2, :]
                dst = sq[0:1, 0:1024].rearrange("o (q s c) -> o q s c", q=4, s=2)
                p.add("sync", lambda e, src=src, dst=dst: e.dma_start(dst, src),
                      reads=[IlS.name], writes=[sq.name], dma="c_ilr")
                for sub in range(2):
                    for qtl in range(4):
                        p.add("tensor", lambda e, qtl=qtl, sub=sub: e.matmul(
                            pYH[sub * 64:(sub + 1) * 64, qtl * 128:(qtl + 1) * 128],
                            onesf[0:1, 0:64],
                            sq[0:1, qtl * 256 + sub * 128: qtl * 256 + (sub + 1) * 128],
                            start=True, stop=True),
                            reads=[sq.name, onesf.name], writes=[pYH.name])
                p.add("vector", lambda e, c=c: e.tensor_tensor(
                    ybf[:, c * 512:(c + 1) * 512], yfm[:, c * 512:(c + 1) * 512],
                    pYH[:, 0:512], ALU.mult),
                    reads=[yfm.name, pYH.name], writes=[ybf.name])
            for qtl in range(4):
                qt = g * 4 + qtl
                p.add("gpsimd", lambda e, qt=qt, qtl=qtl, Xsrc=Xsrc: e.dma_start(
                    xold[qtl % 2][:], Xsrc.ap()[qt * 128:(qt + 1) * 128, :]),
                    reads=[Xsrc.name], writes=[xold[qtl % 2].name],
                    dma=xold[qtl % 2].name)
                for c in range(4):
                    for hf in range(2):
                        p.add("tensor", lambda e, c=c, hf=hf, qtl=qtl: e.matmul(
                            pO[hf][:, 0:512],
                            ybf[:, c * 512 + qtl * 128: c * 512 + (qtl + 1) * 128],
                            wp_bf[:, c * 1024 + hf * 512: c * 1024 + hf * 512 + 512],
                            start=(c == 0), stop=(c == 3 and H == 1)),
                            reads=[ybf.name, wp_bf.name], writes=[pO[hf].name])
                if H == 0:
                    for hf in range(2):
                        p.add("tensor", lambda e, hf=hf: e.matmul(
                            pO[hf][:, 0:512], ones_bf[:],
                            bphl[:, hf * 512: hf * 512 + 512],
                            start=False, stop=False),
                            reads=[ones_bf.name, bphl.name + ":h"], writes=[pO[hf].name])
                        p.add("tensor", lambda e, hf=hf: e.matmul(
                            pO[hf][:, 0:512], ones_bf[:],
                            bphl[:, 1024 + hf * 512: 1024 + hf * 512 + 512],
                            start=False, stop=True),
                            reads=[ones_bf.name, bphl.name + ":l"], writes=[pO[hf].name])
                kxn = xnew[qtl % 2].name
                for hf in range(2):
                    p.add("vector", lambda e, qtl=qtl, hf=hf: e.tensor_add(
                        xnew[qtl % 2][:, hf * 512: hf * 512 + 512], pO[hf][:, 0:512],
                        xold[qtl % 2][:, hf * 512: hf * 512 + 512]),
                        reads=[pO[hf].name, xold[qtl % 2].name], writes=[kxn])
                p.add("gpsimd", lambda e, qt=qt, qtl=qtl, Xdst=Xdst: e.dma_start(
                    Xdst.ap()[qt * 128:(qt + 1) * 128, :], xnew[qtl % 2][:]),
                    reads=[kxn], writes=[Xdst.name], dma="c_st" + Xdst.name)



def build_block_d(nc, p, al, Xi, Xo, W1, B1c, W2, B2, WrT, Br, env):
    rxin = al.suballoc(8192)
    xin = [al.alloc_sbuf_tensor_at(f"d_xin{i}", [128, 1024], F32, offset=rxin + 4096 * i)
           for i in range(2)]
    stats = al.alloc_sbuf_tensor("d_st", [128, 8], F32)
    xn_fm = al.alloc_sbuf_tensor("d_xnfm", [128, FC * T], BF16)
    ract = al.suballoc(65536)
    act = al.alloc_sbuf_tensor_at("d_act", [128, 16 * 2048], BF16, offset=ract)
    sq = al.alloc_sbuf_tensor_at("d_sq", [128, 1024], F32, offset=ract)
    xn_t2 = [al.alloc_sbuf_tensor_at(f"d_xnt{i}", [128, 1024], BF16,
                                     offset=ract + 4096 + 2048 * i) for i in range(2)]
    w2_bf = al.alloc_sbuf_tensor("d_w2", [128, 16 * 1024], BF16)
    w1a = [al.alloc_sbuf_tensor(f"d_w1a{i}", [128, 1024], BF16) for i in range(2)]
    w1g = [al.alloc_sbuf_tensor(f"d_w1g{i}", [128, 1024], BF16) for i in range(2)]
    w1stage = [al.alloc_sbuf_tensor(f"d_w1s{i}", [128, 1024], F32) for i in range(2)]
    w1stage += [al.alloc_sbuf_tensor_at(f"d_w1s{i+2}", [128, 1024], F32,
                                        offset=rxin + 4096 * i) for i in range(2)]
    wstage = w1stage[0:2]
    rw = al.alloc_sbuf_tensor("d_rw", [128, 64], BF16)
    r_rt = al.suballoc(4096)
    lg = al.alloc_sbuf_tensor_at("d_lg", [128, 256], F32, offset=r_rt)
    ev = al.alloc_sbuf_tensor_at("d_ev", [128, 256], F32, offset=r_rt + 1024)
    t1 = al.alloc_sbuf_tensor_at("d_t1", [128, 256], F32, offset=r_rt + 2048)
    t2 = al.alloc_sbuf_tensor_at("d_t2", [128, 256], F32, offset=r_rt + 3072)
    comb = al.alloc_sbuf_tensor("d_comb", [128, 256], F32)
    combb = al.alloc_sbuf_tensor("d_combb", [128, 256], BF16)
    st2 = al.alloc_sbuf_tensor("d_st2", [128, 112], F32)
    b1c = al.alloc_sbuf_tensor("d_b1", [128, 32], F32)
    brrow = al.alloc_sbuf_tensor("d_br", [1, 8], F32)
    brhl = al.alloc_sbuf_tensor("d_brhl", [1, 16], BF16)
    brt = al.alloc_sbuf_tensor("d_brt", [1, 8], F32)
    combT = al.alloc_sbuf_tensor("d_cT", [8, 128], BF16)
    r_sg = al.suballoc(2048)
    b2bf = al.alloc_sbuf_tensor_at("d_b2b", [8, 1024], BF16, offset=r_sg)
    ssb = [al.alloc_sbuf_tensor_at(f"d_s{i}", [128, 512], BF16, offset=r_sg + 1024 * i)
           for i in range(2)]
    gsb = [al.alloc_sbuf_tensor(f"d_g{i}", [128, 512], BF16) for i in range(2)]
    obuf = [al.alloc_sbuf_tensor_at("d_ob0", [128, 1024], F32, offset=r_rt),
            al.alloc_sbuf_tensor("d_ob1", [128, 1024], F32)]

    q4 = env["psum_q"]
    pR, pA, pG, pC = q4[0], q4[1], q4[2], q4[3]
    pO = env["psum_o"]
    pT = env["psum_t"]

    env = dict(env)
    env["sq"] = sq
    env["stats"] = stats
    ones_bf = env["ones_bf"]
    ident = env["ident_bf"]

    for tt in range(NT):
        xn_t = xn_t2[tt % 2]
        def consume(p, tt=tt, xn_t=xn_t):
            for c in range(FC):
                emit_transpose_to(
                    p, env, xn_t[:, c * 128:(c + 1) * 128], xn_t.name,
                    xn_fm[:, c * T + tt * 128: c * T + (tt + 1) * 128],
                    f"{xn_fm.name}:{tt}", pT[c % 2],
                    evac_engine=("vector" if c % 2 == 0 else "scalar"))
        emit_ln_tile(p, env, Xi, tt, xin[tt % 2], xn_t, None, consume)

    emit_load_cast(p, env, lambda i: WrT.ap()[i * 128:(i + 1) * 128, :],
                   WrT.name, 8, 8, rw, 0, wstage)
    p.add("sync", lambda e: e.dma_start(brrow[:], Br.ap()[0:1, :]),
          reads=[Br.name], writes=[brrow.name], dma=brrow.name)
    p.add("vector", lambda e: e.tensor_copy(brhl[:, 0:8], brrow[:]),
          reads=[brrow.name], writes=[brhl.name + ":h"])
    p.add("vector", lambda e: e.tensor_copy(brt[:], brhl[:, 0:8]),
          reads=[brhl.name + ":h"], writes=[brt.name])
    p.add("vector", lambda e: e.tensor_sub(brt[:], brrow[:], brt[:]),
          reads=[brrow.name, brt.name], writes=[brt.name])
    p.add("vector", lambda e: e.tensor_copy(brhl[:, 8:16], brt[:]),
          reads=[brt.name], writes=[brhl.name + ":l"])
    for tt in range(NT):
        for fc in range(FC):
            p.add("tensor", lambda e, tt=tt, fc=fc: e.matmul(
                pR[:, tt * 8:(tt + 1) * 8],
                xn_fm[:, fc * T + tt * 128: fc * T + (tt + 1) * 128],
                rw[:, fc * 8:(fc + 1) * 8], start=(fc == 0), stop=False),
                reads=[f"{xn_fm.name}:{tt}", rw.name], writes=[pR.name])
        p.add("tensor", lambda e, tt=tt: e.matmul(
            pR[:, tt * 8:(tt + 1) * 8], ones_bf[:], brhl[:, 0:8],
            start=False, stop=False),
            reads=[ones_bf.name, brhl.name + ":h"], writes=[pR.name])
        p.add("tensor", lambda e, tt=tt: e.matmul(
            pR[:, tt * 8:(tt + 1) * 8], ones_bf[:], brhl[:, 8:16],
            start=False, stop=True),
            reads=[ones_bf.name, brhl.name + ":l"], writes=[pR.name])
    p.add("vector", lambda e: e.tensor_copy(lg[:], pR[:, 0:256]),
          reads=[pR.name], writes=[lg.name])
    e3 = lambda t: t[:].rearrange("p (a b) -> p a b", b=8)
    bc = lambda t, col0: t[:, col0:col0 + 32].unsqueeze(2).broadcast_to([128, 32, 8])
    p.add("vector", lambda e: e.tensor_reduce(
        st2[:, 0:32].rearrange("p (a b) -> p a b", b=1), e3(lg), AX.X, ALU.max),
        reads=[lg.name], writes=[st2.name + ":m1"])
    p.add("vector", lambda e: e.tensor_tensor(e3(t1), e3(lg), bc(st2, 0), ALU.is_ge),
          reads=[lg.name, st2.name + ":m1"], writes=[t1.name])
    p.add("vector", lambda e: e.scalar_tensor_tensor(
        e3(t2), e3(t1), NEG_BIG, e3(lg), ALU.mult, ALU.add),
        reads=[t1.name, lg.name], writes=[t2.name])
    p.add("vector", lambda e: e.tensor_reduce(
        st2[:, 32:64].rearrange("p (a b) -> p a b", b=1), e3(t2), AX.X, ALU.max),
        reads=[t2.name], writes=[st2.name + ":m2"])
    p.add("vector", lambda e: e.tensor_tensor(e3(t1), e3(lg), bc(st2, 0), ALU.subtract),
          reads=[lg.name, st2.name + ":m1"], writes=[t1.name])
    p.add("scalar", lambda e: e.activation(ev[:], t1[:], AF.Exp),
          reads=[t1.name], writes=[ev.name])
    p.add("vector", lambda e: e.tensor_sub(st2[:, 64:96], st2[:, 32:64], st2[:, 0:32]),
          reads=[st2.name + ":m2", st2.name + ":m1"], writes=[st2.name + ":dm"])
    p.add("scalar", lambda e: e.activation(st2[:, 64:96], st2[:, 64:96], AF.Exp),
          reads=[st2.name + ":dm"], writes=[st2.name + ":ed"])
    p.add("vector", lambda e: e.tensor_scalar(st2[:, 64:96], st2[:, 64:96], 1.0,
                                              None, ALU.add),
          reads=[st2.name + ":ed"], writes=[st2.name + ":dn"])
    p.add("vector", lambda e: e.reciprocal(st2[:, 64:96], st2[:, 64:96]),
          reads=[st2.name + ":dn"], writes=[st2.name + ":idn"])
    p.add("vector", lambda e: e.tensor_tensor(e3(t2), e3(lg), bc(st2, 32), ALU.is_ge),
          reads=[lg.name, st2.name + ":m2"], writes=[t2.name])
    p.add("vector", lambda e: e.tensor_tensor(e3(t1), e3(ev), e3(t2), ALU.mult),
          reads=[ev.name, t2.name], writes=[t1.name])
    p.add("vector", lambda e: e.tensor_tensor(e3(comb), e3(t1), bc(st2, 64), ALU.mult),
          reads=[t1.name, st2.name + ":idn"], writes=[comb.name])
    p.add("vector", lambda e: e.tensor_copy(combb[:], comb[:]),
          reads=[comb.name], writes=[combb.name])

    p.add("sync", lambda e: e.dma_start(wstage[0][0:8, 0:1024], B2.ap()),
          reads=[B2.name], writes=[wstage[0].name], dma=wstage[0].name)
    p.add("vector", lambda e: e.tensor_copy(b2bf[:], wstage[0][0:8, 0:1024]),
          reads=[wstage[0].name], writes=[b2bf.name])
    p.region_switch(lg.name, obuf[0].name)
    p.region_switch(ev.name, obuf[0].name)
    p.region_switch(t1.name, obuf[0].name)
    p.region_switch(t2.name, obuf[0].name)
    for tt in range(NT):
        p.add("tensor", lambda e, tt=tt: e.transpose(
            pT[0][0:8, 0:128], combb[:, tt * 8:(tt + 1) * 8], ident[:]),
            reads=[combb.name, ident.name], writes=[pT[0].name])
        p.add("vector", lambda e: e.tensor_copy(combT[:], pT[0][0:8, 0:128]),
              reads=[pT[0].name], writes=[combT.name])
        p.add("sync", lambda e, tt=tt: e.dma_start(
            xin[tt % 2][:], Xi.ap()[tt * 128:(tt + 1) * 128, :]),
            reads=[Xi.name], writes=[xin[tt % 2].name], dma=xin[tt % 2].name)
        for hf in range(2):
            p.add("tensor", lambda e, hf=hf: e.matmul(
                pC[:, 0:512] if hf == 0 else pC[:, 0:512], combT[:],
                b2bf[:, hf * 512: hf * 512 + 512], start=True, stop=True),
                reads=[combT.name, b2bf.name], writes=[pC.name])
            p.add("vector", lambda e, tt=tt, hf=hf: e.tensor_add(
                obuf[tt % 2][:, hf * 512: hf * 512 + 512], pC[:, 0:512],
                xin[tt % 2][:, hf * 512: hf * 512 + 512]),
                reads=[pC.name, xin[tt % 2].name], writes=[obuf[tt % 2].name])
        p.add("gpsimd", lambda e, tt=tt: e.dma_start(
            Xo.ap()[tt * 128:(tt + 1) * 128, :], obuf[tt % 2][:]),
            reads=[obuf[tt % 2].name, f"d_Xo:{tt}"], writes=[f"d_Xo:{tt}"],
            dma=f"d_ac{tt % 8}")

    p.region_switch(sq.name, act.name)
    p.region_switch(xn_t2[0].name, act.name)
    p.region_switch(xn_t2[1].name, act.name)
    p.region_switch(xin[0].name, w1stage[2].name)
    p.region_switch(xin[1].name, w1stage[3].name)
    p.region_switch(b2bf.name, ssb[0].name)
    p.region_switch(b2bf.name, ssb[1].name)
    for ei in range(8):
        p.add("sync", lambda e, ei=ei: e.dma_start(b1c[:], B1c.ap()[ei]),
              reads=[B1c.name], writes=[b1c.name], dma=b1c.name)
        emit_load_cast(p, env, lambda i, ei=ei: W2.ap()[ei, i * 128:(i + 1) * 128, :],
                       W2.name, 16, 1024, w2_bf, 0, wstage)
        def load_w1(mc, ei=ei):
            sa, sg = w1stage[2 * (mc % 2)], w1stage[2 * (mc % 2) + 1]
            wa, wg1 = w1a[mc % 2], w1g[mc % 2]
            p.add("sync", lambda e, mc=mc, sa=sa, ei=ei: e.dma_start(
                sa[:].rearrange("p (c j) -> p c j", j=128),
                W1.ap()[ei, :, mc * 128:(mc + 1) * 128]
                .rearrange("(c p) j -> p c j", p=128)),
                reads=[W1.name], writes=[sa.name], dma=sa.name)
            p.add("vector", lambda e, sa=sa, wa=wa: e.tensor_copy(wa[:], sa[:]),
                  reads=[sa.name], writes=[wa.name])
            p.add("sync", lambda e, mc=mc, sg=sg, ei=ei: e.dma_start(
                sg[:].rearrange("p (c j) -> p c j", j=128),
                W1.ap()[ei, :, 2048 + mc * 128: 2048 + (mc + 1) * 128]
                .rearrange("(c p) j -> p c j", p=128)),
                reads=[W1.name], writes=[sg.name], dma=sg.name)
            p.add("vector", lambda e, sg=sg, wg1=wg1: e.tensor_copy(wg1[:], sg[:]),
                  reads=[sg.name], writes=[wg1.name])

        for half in range(2):
            for mc in range(16):
                if D_W1DB:
                    if half == 0 and mc == 0:
                        load_w1(0)
                    if mc < 15:
                        load_w1(mc + 1)
                    elif half == 0:
                        load_w1(0)
                else:
                    load_w1(mc)
                wa, wg1 = w1a[mc % 2], w1g[mc % 2]
                for tg in range(4):
                    tcol = half * 2048 + tg * 512
                    pAt = q4[D_PAR and tg % 2]
                    pGt = q4[2 + (D_PAR and tg % 2)]
                    sst = ssb[D_PAR and tg % 2]
                    gst = gsb[D_PAR and tg % 2]
                    for fc in range(FC):
                        p.add("tensor", lambda e, fc=fc, tcol=tcol, wa=wa, pAt=pAt: e.matmul(
                            pAt[:, 0:512], wa[:, fc * 128:(fc + 1) * 128],
                            xn_fm[:, fc * T + tcol: fc * T + tcol + 512],
                            start=(fc == 0), stop=(fc == 7)),
                            reads=[wa.name] + [f"{xn_fm.name}:{tcol // 128 + j}" for j in range(4)],
                            writes=[pAt.name])
                    for fc in range(FC):
                        p.add("tensor", lambda e, fc=fc, tcol=tcol, wg1=wg1, pGt=pGt: e.matmul(
                            pGt[:, 0:512], wg1[:, fc * 128:(fc + 1) * 128],
                            xn_fm[:, fc * T + tcol: fc * T + tcol + 512],
                            start=(fc == 0), stop=(fc == 7)),
                            reads=[wg1.name] + [f"{xn_fm.name}:{tcol // 128 + j}" for j in range(4)],
                            writes=[pGt.name])
                    p.add("scalar", lambda e, mc=mc, pAt=pAt, sst=sst: e.activation(
                        sst[:], pAt[:, 0:512], AF.Silu, bias=b1c[:, mc:mc + 1]),
                        reads=[pAt.name, b1c.name], writes=[sst.name])
                    p.add("vector", lambda e, mc=mc, pGt=pGt, gst=gst: e.tensor_scalar(
                        gst[:], pGt[:, 0:512], b1c[:, 16 + mc:17 + mc], None, ALU.add),
                        reads=[pGt.name, b1c.name], writes=[gst.name])
                    p.add("vector", lambda e, mc=mc, tg=tg, sst=sst, gst=gst: e.tensor_tensor(
                        act[:, mc * 2048 + tg * 512: mc * 2048 + (tg + 1) * 512],
                        sst[:], gst[:], ALU.mult),
                        reads=[sst.name, gst.name], writes=[act.name])
            if DEBUG_D and ei == 0 and half == 0:
                dbgd = env["dbgd"]
                p.add("gpsimd", lambda e: e.dma_start(dbgd.ap()[0:128, :], act[:]),
                      reads=[act.name], writes=[dbgd.name], dma="dbgd")
                dbgx = env["dbgx"]
                p.add("gpsimd", lambda e: e.dma_start(dbgx.ap()[0:128, :], xn_fm[:]),
                      reads=[xn_fm.name], writes=[dbgx.name], dma="dbgd")
            for tl in range(16):
                gt = half * 16 + tl
                pOt = pO if tl % 2 == 0 else [q4[0], q4[1]]
                for mc in range(16):
                    for hf in range(2):
                        p.add("tensor", lambda e, mc=mc, hf=hf, tl=tl, pOt=pOt: e.matmul(
                            pOt[hf][:, 0:512],
                            act[:, mc * 2048 + tl * 128: mc * 2048 + (tl + 1) * 128],
                            w2_bf[:, mc * 1024 + hf * 512: mc * 1024 + hf * 512 + 512],
                            start=(mc == 0), stop=(mc == 15)),
                            reads=[act.name, w2_bf.name], writes=[pOt[hf].name])
                ob = obuf[tl % 2]
                for hf in range(2):
                    p.add("vector", lambda e, gt=gt, ei=ei, ob=ob, hf=hf, pOt=pOt: e.tensor_scalar(
                        ob[:, hf * 512: hf * 512 + 512], pOt[hf][:, 0:512],
                        comb[:, gt * 8 + ei: gt * 8 + ei + 1],
                        None, ALU.mult),
                        reads=[pOt[hf].name, comb.name], writes=[ob.name])
                p.add("gpsimd", lambda e, gt=gt, ob=ob: e.dma_start(
                    Xo.ap()[gt * 128:(gt + 1) * 128, :], ob[:], accum_op=ALU.add),
                    reads=[ob.name, f"d_Xo:{gt}"], writes=[f"d_Xo:{gt}"],
                    dma=f"d_ac{gt % 8}")



def make_env(nc, need_mask=True):
    ident_f = nc.dram_tensor("c_ident", [128, 128], F32, kind="ExternalInput")
    ones_f = nc.dram_tensor("c_ones", [1, 128], F32, kind="ExternalInput")
    st_id = nc.alloc_sbuf_tensor("s_identf", [128, 128], F32)
    st_on = nc.alloc_sbuf_tensor("s_onesf", [1, 128], F32)
    s_ident = nc.alloc_sbuf_tensor("s_ident", [128, 128], BF16)
    s_ones = nc.alloc_sbuf_tensor("s_ones", [1, 128], BF16)
    eps_col = nc.alloc_sbuf_tensor("s_eps", [128, 1], F32)
    env = {"nc": nc, "ident_bf": s_ident, "ones_bf": s_ones, "eps_col": eps_col,
           "ident_f": st_id, "ones_f": st_on}
    if need_mask:
        env["mask4"] = nc.alloc_sbuf_tensor("s_mask4", [128, 2048], BF16)
        env["z512"] = nc.alloc_sbuf_tensor("s_z512", [128, 512], BF16)
    env["psum_q"] = [nc.alloc_psum_tensor(f"ps_q{i}", [128, 512], F32) for i in range(4)]
    env["psum_t"] = [nc.alloc_psum_tensor(f"ps_t{i}", [128, 1024], BF16) for i in range(2)]
    env["psum_o"] = [nc.alloc_psum_tensor(f"ps_o{i}", [128, 512], F32) for i in range(2)]
    return env, [(st_id, ident_f, s_ident), (st_on, ones_f, s_ones)]


def load_consts(p, env, pairs):
    for stg, dr, sb in pairs:
        p.add("sync", lambda e, stg=stg, dr=dr: e.dma_start(stg[:], dr.ap()),
              reads=[dr.name], writes=[stg.name], dma=stg.name)
        p.add("vector", lambda e, stg=stg, sb=sb: e.tensor_copy(sb[:], stg[:]),
              reads=[stg.name], writes=[sb.name])
    eps = env["eps_col"]
    p.add("vector", lambda e: e.memset(eps[:], 1e-5), writes=[eps.name])
    if "mask4" in env:
        mask4 = env["mask4"]
        z512 = env["z512"]
        p.add("vector", lambda e: e.memset(z512[:], 0.0), writes=[z512.name])
        for v in range(4):
            p.add("gpsimd", lambda e, v=v: e.affine_select(
                mask4[:, v * 512:(v + 1) * 512], z512[:], [[-1, 512]],
                mybir.AluOpType.is_ge, NEG_BIG, base=v * 128, channel_multiplier=1),
                reads=[z512.name], writes=[mask4.name])


def const_inputs():
    return {
        "c_ident": np.eye(128, dtype=np.float32),
        "c_ones": np.ones((1, 128), dtype=np.float32),
    }


def mk_arena(nc, reserve=1248):
    base = (SBUF_TOP - nc.sbuf_bytes_remaining + 31) & ~31
    size = SBUF_TOP - reserve - base
    nc.alloc_sbuf_tensor("arena", [128, SBUF_TOP - (SBUF_TOP - nc.sbuf_bytes_remaining) - reserve],
                         mybir.dt.uint8)
    return BAlloc(nc, base, size)


def build_prog_fused():
    nc = bass.Bass("TRN2", target_bir_lowering=False)
    Xi = nc.dram_tensor("Xi", [T, D], F32, kind="ExternalInput")
    Xo = nc.dram_tensor("Xo", [T, D], F32, kind="ExternalOutput")
    X1 = nc.dram_tensor("X1", [T, D], F32)
    X2 = nc.dram_tensor("X2", [T, D], F32)
    X3 = nc.dram_tensor("X3", [T, D], F32)
    Xmid = nc.dram_tensor("Xmid", [T, D], F32)
    XnH = nc.dram_tensor("XnH", [T, D], BF16)
    IlS = nc.dram_tensor("IlS", [32, 128], F32)
    aWqT = nc.dram_tensor("aWqT", [D, D], F32, kind="ExternalInput")
    aWkT = nc.dram_tensor("aWkT", [D, D], F32, kind="ExternalInput")
    aWv = nc.dram_tensor("aWv", [D, D], F32, kind="ExternalInput")
    bWr = nc.dram_tensor("bWr", [D, 16], F32, kind="ExternalInput")
    bWgs = nc.dram_tensor("bWgs", [2, D, 512], F32, kind="ExternalInput")
    bWus = nc.dram_tensor("bWus", [2, D, 512], F32, kind="ExternalInput")
    bWds = nc.dram_tensor("bWds", [2, 512, D], F32, kind="ExternalInput")
    bWgr = nc.dram_tensor("bWgr", [16, D, 256], F32, kind="ExternalInput")
    bWur = nc.dram_tensor("bWur", [16, D, 256], F32, kind="ExternalInput")
    bWdr = nc.dram_tensor("bWdr", [16, 256, D], F32, kind="ExternalInput")
    cWqT = nc.dram_tensor("cWqT", [D, D], F32, kind="ExternalInput")
    cWkT = nc.dram_tensor("cWkT", [D, D], F32, kind="ExternalInput")
    cWvT = nc.dram_tensor("cWvT", [D, D], F32, kind="ExternalInput")
    cWpT = nc.dram_tensor("cWpT", [D, D], F32, kind="ExternalInput")
    cBq = nc.dram_tensor("cBq", [1, D], F32, kind="ExternalInput")
    cBk = nc.dram_tensor("cBk", [1, D], F32, kind="ExternalInput")
    cBv = nc.dram_tensor("cBv", [1, D], F32, kind="ExternalInput")
    cBp = nc.dram_tensor("cBp", [1, D], F32, kind="ExternalInput")
    CosT = nc.dram_tensor("CosT", [T, 32], F32, kind="ExternalInput")
    SinT = nc.dram_tensor("SinT", [T, 32], F32, kind="ExternalInput")
    dW1 = nc.dram_tensor("dW1", [8, D, 4096], F32, kind="ExternalInput")
    dB1c = nc.dram_tensor("dB1c", [8, 128, 32], F32, kind="ExternalInput")
    dW2 = nc.dram_tensor("dW2", [8, 2048, D], F32, kind="ExternalInput")
    dB2 = nc.dram_tensor("dB2", [8, D], F32, kind="ExternalInput")
    dWrT = nc.dram_tensor("dWrT", [D, 8], F32, kind="ExternalInput")
    dBr = nc.dram_tensor("dBr", [1, 8], F32, kind="ExternalInput")

    env, pairs = make_env(nc)
    al = mk_arena(nc)
    p = Prog(nc)
    load_consts(p, env, pairs)
    build_block_a(nc, p, al, Xi, X1, aWqT, aWkT, aWv, env)
    p.barrier(); al.reset()
    build_block_b(nc, p, al, X1, X2, bWr, bWgs, bWus, bWds, bWgr, bWur, bWdr, env)
    p.barrier(); al.reset()
    build_block_c(nc, p, al, X2, Xmid, X3, cWqT, cWkT, cWvT, cWpT, cBq, cBk, cBv,
                  cBp, CosT, SinT, XnH, IlS, env)
    p.barrier(); al.reset()
    build_block_d(nc, p, al, X3, Xo, dW1, dB1c, dW2, dB2, dWrT, dBr, env)
    p.barrier()
    p.finalize_and_emit()
    return nc


def build_prog_a():
    nc = bass.Bass("TRN2", target_bir_lowering=False)
    Xi = nc.dram_tensor("Xi", [T, D], F32, kind="ExternalInput")
    Xo = nc.dram_tensor("Xo", [T, D], F32, kind="ExternalOutput")
    WqT = nc.dram_tensor("WqT", [D, D], F32, kind="ExternalInput")
    WkT = nc.dram_tensor("WkT", [D, D], F32, kind="ExternalInput")
    Wv = nc.dram_tensor("Wv", [D, D], F32, kind="ExternalInput")
    env, pairs = make_env(nc)
    al = mk_arena(nc)
    p = Prog(nc)
    load_consts(p, env, pairs)
    build_block_a(nc, p, al, Xi, Xo, WqT, WkT, Wv, env)
    p.barrier()
    p.finalize_and_emit()
    return nc


def build_prog_b():
    nc = bass.Bass("TRN2", target_bir_lowering=False)
    Xi = nc.dram_tensor("Xi", [T, D], F32, kind="ExternalInput")
    Xo = nc.dram_tensor("Xo", [T, D], F32, kind="ExternalOutput")
    Wr = nc.dram_tensor("Wr", [D, 16], F32, kind="ExternalInput")
    Wgs = nc.dram_tensor("Wgs", [2, D, 512], F32, kind="ExternalInput")
    Wus = nc.dram_tensor("Wus", [2, D, 512], F32, kind="ExternalInput")
    Wds = nc.dram_tensor("Wds", [2, 512, D], F32, kind="ExternalInput")
    Wgr = nc.dram_tensor("Wgr", [16, D, 256], F32, kind="ExternalInput")
    Wur = nc.dram_tensor("Wur", [16, D, 256], F32, kind="ExternalInput")
    Wdr = nc.dram_tensor("Wdr", [16, 256, D], F32, kind="ExternalInput")
    env, pairs = make_env(nc, need_mask=False)
    al = mk_arena(nc)
    p = Prog(nc)
    load_consts(p, env, pairs)
    build_block_b(nc, p, al, Xi, Xo, Wr, Wgs, Wus, Wds, Wgr, Wur, Wdr, env)
    p.barrier()
    p.finalize_and_emit()
    return nc


def build_prog_c():
    nc = bass.Bass("TRN2", target_bir_lowering=False)
    Xi = nc.dram_tensor("Xi", [T, D], F32, kind="ExternalInput")
    Xo = nc.dram_tensor("Xo", [T, D], F32, kind="ExternalOutput")
    Xmid = nc.dram_tensor("Xmid", [T, D], F32,
                          kind=("ExternalOutput" if DEBUG_C else "Internal"))
    XnH = nc.dram_tensor("XnH", [T, D], BF16)
    IlS = nc.dram_tensor("IlS", [32, 128], F32)
    WqT = nc.dram_tensor("WqT", [D, D], F32, kind="ExternalInput")
    WkT = nc.dram_tensor("WkT", [D, D], F32, kind="ExternalInput")
    WvT = nc.dram_tensor("WvT", [D, D], F32, kind="ExternalInput")
    WpT = nc.dram_tensor("WpT", [D, D], F32, kind="ExternalInput")
    Bq = nc.dram_tensor("Bq", [1, D], F32, kind="ExternalInput")
    Bk = nc.dram_tensor("Bk", [1, D], F32, kind="ExternalInput")
    Bv = nc.dram_tensor("Bv", [1, D], F32, kind="ExternalInput")
    Bp = nc.dram_tensor("Bp", [1, D], F32, kind="ExternalInput")
    CosT = nc.dram_tensor("CosT", [T, 32], F32, kind="ExternalInput")
    SinT = nc.dram_tensor("SinT", [T, 32], F32, kind="ExternalInput")
    env, pairs = make_env(nc)
    if DEBUG_C:
        env["dbg"] = nc.dram_tensor("Dbg", [1024, 2048], F32, kind="ExternalOutput")
    al = mk_arena(nc)
    p = Prog(nc)
    load_consts(p, env, pairs)
    build_block_c(nc, p, al, Xi, Xmid, Xo, WqT, WkT, WvT, WpT, Bq, Bk, Bv, Bp,
                  CosT, SinT, XnH, IlS, env)
    p.barrier()
    p.finalize_and_emit()
    return nc


def build_prog_d():
    nc = bass.Bass("TRN2", target_bir_lowering=False)
    Xi = nc.dram_tensor("Xi", [T, D], F32, kind="ExternalInput")
    Xo = nc.dram_tensor("Xo", [T, D], F32, kind="ExternalOutput")
    W1 = nc.dram_tensor("W1", [8, D, 4096], F32, kind="ExternalInput")
    B1c = nc.dram_tensor("B1c", [8, 128, 32], F32, kind="ExternalInput")
    W2 = nc.dram_tensor("W2", [8, 2048, D], F32, kind="ExternalInput")
    B2 = nc.dram_tensor("B2", [8, D], F32, kind="ExternalInput")
    WrT = nc.dram_tensor("WrT", [D, 8], F32, kind="ExternalInput")
    Br = nc.dram_tensor("Br", [1, 8], F32, kind="ExternalInput")
    env, pairs = make_env(nc, need_mask=False)
    if DEBUG_D:
        env["dbgd"] = nc.dram_tensor("DbgD", [128, 32768], mybir.dt.bfloat16,
                                     kind="ExternalOutput")
        env["dbgx"] = nc.dram_tensor("DbgX", [128, 32768], mybir.dt.bfloat16,
                                     kind="ExternalOutput")
    al = mk_arena(nc)
    p = Prog(nc)
    load_consts(p, env, pairs)
    build_block_d(nc, p, al, Xi, Xo, W1, B1c, W2, B2, WrT, Br, env)
    p.barrier()
    p.finalize_and_emit()
    return nc


def rope_tables():
    half = 32
    i = np.arange(half, dtype=np.float64)
    alpha = 1.0 / 10000.0 ** (2.0 * i / 64.0)
    ang = np.arange(T, dtype=np.float64)[:, None] * alpha[None, :]
    return (np.cos(ang).astype(np.float32), np.sin(ang).astype(np.float32))


_PROG_CACHE = {}


def _prog(name, builder):
    if name not in _PROG_CACHE:
        _PROG_CACHE[name] = builder()
    return _PROG_CACHE[name]


def _run(nc, per_core, shared):
    in_maps = []
    for b in range(8):
        m = dict(shared)
        m.update({k: v[b] for k, v in per_core.items()})
        in_maps.append(m)
    res = run_bass_kernel_spmd(nc, in_maps, core_ids=list(range(8)))
    return np.stack([np.asarray(r["Xo"], dtype=np.float32) for r in res.results])


def kernel(**inputs):
    f32c = lambda a: np.ascontiguousarray(np.asarray(a), dtype=np.float32)
    x = f32c(inputs["L0_attn_x"])
    cos_t, sin_t = rope_tables()
    b1 = f32c(inputs["L1_ffn_b1"])
    b1c = np.concatenate(
        [b1[:, :2048].reshape(8, 16, 128).transpose(0, 2, 1),
         b1[:, 2048:].reshape(8, 16, 128).transpose(0, 2, 1)], axis=2)
    sh = dict(const_inputs())
    sh["aWqT"] = f32c(inputs["L0_attn_w_q"].T)
    sh["aWkT"] = f32c(inputs["L0_attn_w_k"].T)
    sh["aWv"] = f32c(inputs["L0_attn_w_v"])
    sh["bWr"] = f32c(inputs["L0_ffn_router_w"])
    sh["bWgs"] = f32c(inputs["L0_ffn_wg_s"])
    sh["bWus"] = f32c(inputs["L0_ffn_wu_s"])
    sh["bWds"] = f32c(inputs["L0_ffn_wd_s"])
    sh["bWgr"] = f32c(inputs["L0_ffn_wg_r"])
    sh["bWur"] = f32c(inputs["L0_ffn_wu_r"])
    sh["bWdr"] = f32c(inputs["L0_ffn_wd_r"])
    sh["cWqT"] = f32c(inputs["L1_attn_wq"].T)
    sh["cWkT"] = f32c(inputs["L1_attn_wk"].T)
    sh["cWvT"] = f32c(inputs["L1_attn_wv"].T)
    sh["cWpT"] = f32c(inputs["L1_attn_wp"].T)
    sh["cBq"] = f32c(inputs["L1_attn_bq"]).reshape(1, D)
    sh["cBk"] = f32c(inputs["L1_attn_bk"]).reshape(1, D)
    sh["cBv"] = f32c(inputs["L1_attn_bv"]).reshape(1, D)
    sh["cBp"] = f32c(inputs["L1_attn_bp"]).reshape(1, D)
    sh["CosT"] = cos_t
    sh["SinT"] = sin_t
    sh["dW1"] = f32c(inputs["L1_ffn_W1"])
    sh["dB1c"] = np.ascontiguousarray(b1c)
    sh["dW2"] = f32c(inputs["L1_ffn_W2"])
    sh["dB2"] = f32c(inputs["L1_ffn_b2"])
    sh["dWrT"] = f32c(inputs["L1_ffn_Wr"])
    sh["dBr"] = f32c(inputs["L1_ffn_br"]).reshape(1, 8)
    return _run(_prog("fused", build_prog_fused), {"Xi": x}, sh)


# version 21
# speedup vs baseline: 1.1154x; 1.1154x over previous
import numpy as np
import ml_dtypes
import concourse.bass as bass
import concourse.mybir as mybir
from concourse.bass_utils import run_bass_kernel_spmd

F32 = mybir.dt.float32
BF16 = mybir.dt.bfloat16
AF = mybir.ActivationFunctionType
ALU = mybir.AluOpType
AX = mybir.AxisListType

T = 4096
D = 1024
NT = T // 128
FC = D // 128
GAMMA0 = 1.0 / 32.0
DEBUG_C = False
DEBUG_G = 0
D_PAR = True
NDIR = 5
D_W1DB = True
DEBUG_D = False
NEG_BIG = -1.0e30

ENGINES = ("sync", "tensor", "scalar", "vector", "gpsimd")


class Prog:

    def __init__(self, nc):
        self.nc = nc
        self.ops = []
        self.lw = {}
        self.rd = {}
        self.gcount = {}
        self.gsem = {}
        self.esem = {}
        self.epoch = 0
        self._new_epoch()

    def _new_epoch(self):
        self.epoch += 1
        for e in ENGINES:
            self.esem[e] = self.nc.alloc_semaphore(f"E{e[:2]}{self.epoch}")

    def _gsem(self, g):
        if g not in self.gsem:
            self.gsem[g] = self.nc.alloc_semaphore(f"G{len(self.gsem)}")
        return self.gsem[g]

    def add(self, engine, emit, reads=(), writes=(), dma=None):
        deps = set()
        gwait = {}

        def dep_on(j):
            dj = self.ops[j]
            if dj["dma"] is not None:
                g = dj["dma"]
                gwait[g] = max(gwait.get(g, 0), self.gcount[g])
            else:
                deps.add(j)

        for k in reads:
            for j in self.lw.get(k, {}).values():
                dep_on(j)
        for k in writes:
            for j in self.lw.get(k, {}).values():
                dj = self.ops[j]
                if dma is not None and dj["dma"] == dma:
                    continue
                dep_on(j)
            for j in self.rd.get(k, ()):
                dep_on(j)
        i = len(self.ops)
        if dma is not None:
            self.gcount[dma] = self.gcount.get(dma, 0) + 1
            self._gsem(dma)
        op = {"engine": engine, "emit": emit, "deps": deps, "dma": dma,
              "esem": self.esem[engine], "inc": False, "msv": None,
              "gwait": gwait}
        self.ops.append(op)
        for k in reads:
            self.rd.setdefault(k, []).append(i)
        for k in writes:
            tag = "dma" if dma is not None else engine
            self.lw.setdefault(k, {})[tag] = i
            self.rd[k] = []
        return i

    def region_switch(self, old_key, new_key):
        lw = self.lw.setdefault(new_key, {})
        for tag, j in self.lw.get(old_key, {}).items():
            if tag not in lw or lw[tag] < j:
                lw[tag] = j
        self.rd.setdefault(new_key, []).extend(self.rd.get(old_key, ()))

    def barrier(self):
        last = {}
        for i, op in enumerate(self.ops):
            if op["dma"] is None and op["emit"] is not None:
                last[op["engine"]] = i
        groups = dict(self.gcount)
        for e in ENGINES:
            deps = {i for en, i in last.items() if en != e}
            self.ops.append({"engine": e, "emit": None, "deps": deps,
                             "dma": None, "esem": self.esem[e], "inc": False,
                             "msv": None, "gwait": dict(groups)})
        self.lw = {}
        self.rd = {}
        self._new_epoch()

    def finalize_and_emit(self):
        for op in self.ops:
            for j in op["deps"]:
                dj = self.ops[j]
                if dj["engine"] != op["engine"]:
                    dj["inc"] = True
        counters = {}
        for op in self.ops:
            if op["inc"]:
                s = op["esem"]
                counters[s] = counters.get(s, 0) + 1
                op["msv"] = counters[s]
        for g, c in self.gcount.items():
            assert 16 * c < 60000, f"dma group {g} count {c} too high"
        for s, c in counters.items():
            assert c < 60000, "milestone sem overflow"
        ops = self.ops
        gsem = self.gsem

        eseq = {}
        ecount = {e: 0 for e in ENGINES}
        for i, op in enumerate(ops):
            if op["emit"] is not None:
                ecount[op["engine"]] += 1
                eseq[i] = ecount[op["engine"]]

        with self.nc.Block() as blk:
            def mk_body(ename):
                def body(eng):
                    water = {}
                    drained_upto = 0
                    emitted = 0
                    for op in ops:
                        if op["engine"] != ename:
                            continue
                        need_drain = any(
                            ops[j]["engine"] == ename and ops[j]["emit"] is not None
                            and eseq[j] > drained_upto
                            for j in op["deps"])
                        if need_drain and ename != "tensor":
                            eng.drain()
                            drained_upto = emitted
                        for j in sorted(op["deps"]):
                            dj = ops[j]
                            if dj["engine"] == ename:
                                continue
                            if not dj["inc"]:
                                continue
                            sem, val = dj["esem"], dj["msv"]
                            if water.get(sem.num, 0) < val:
                                eng.wait_ge(sem, val)
                                water[sem.num] = val
                        for g, cnt in sorted(op["gwait"].items()):
                            sem, val = gsem[g], 16 * cnt
                            if water.get(sem.num, 0) < val:
                                eng.wait_ge(sem, val)
                                water[sem.num] = val
                        if op["emit"] is None:
                            continue
                        inst = op["emit"](eng)
                        emitted += 1
                        if op["dma"] is not None:
                            inst.then_inc(gsem[op["dma"]], 16)
                        elif op["inc"]:
                            inst.then_inc(op["esem"], 1)
                return body
            for ename in ENGINES:
                getattr(blk, ename)(mk_body(ename))



def emit_ln_tile(p, env, x_hbm, tt, xin, xn_t, dst_writes, emit_to):
    nc = env["nc"]
    kx = xin.name
    p.add("sync", lambda e, a=x_hbm, b=xin: e.dma_start(
        b[:], a.ap()[tt * 128:(tt + 1) * 128, :]), reads=[x_hbm.name], writes=[kx], dma=kx)
    sq, st = env["sq"], env["stats"]
    ks = st.name
    p.add("vector", lambda e: e.tensor_reduce(st[:, 0:1], xin[:], AX.X, ALU.add),
          reads=[kx], writes=[ks + ":sum"])
    p.add("scalar", lambda e: e.activation(sq[:], xin[:], AF.Square),
          reads=[kx], writes=[sq.name])
    p.add("vector", lambda e: e.tensor_reduce(st[:, 1:2], sq[:], AX.X, ALU.add),
          reads=[sq.name], writes=[ks + ":sq"])
    p.add("vector", lambda e: e.tensor_scalar(st[:, 2:3], st[:, 0:1], 1.0 / D, None, ALU.mult),
          reads=[ks + ":sum"], writes=[ks + ":m"])
    p.add("vector", lambda e: e.tensor_scalar(st[:, 3:4], st[:, 1:2], 1.0 / D, None, ALU.mult),
          reads=[ks + ":sq"], writes=[ks + ":e"])
    p.add("vector", lambda e: e.tensor_tensor(st[:, 4:5], st[:, 2:3], st[:, 2:3], ALU.mult),
          reads=[ks + ":m"], writes=[ks + ":m2"])
    p.add("vector", lambda e: e.tensor_sub(st[:, 5:6], st[:, 3:4], st[:, 4:5]),
          reads=[ks + ":e", ks + ":m2"], writes=[ks + ":v"])
    eps = env["eps_col"]
    p.add("scalar", lambda e: e.activation(st[:, 6:7], st[:, 5:6], AF.Sqrt, bias=eps[:, 0:1]),
          reads=[ks + ":v", eps.name], writes=[ks + ":sd"])
    p.add("vector", lambda e: e.reciprocal(st[:, 7:8], st[:, 6:7]),
          reads=[ks + ":sd"], writes=[ks + ":r"])
    p.add("vector", lambda e: e.tensor_scalar(xn_t[:], xin[:], st[:, 2:3], st[:, 7:8],
                                              ALU.subtract, ALU.mult),
          reads=[kx, ks + ":m", ks + ":r"], writes=[xn_t.name])
    emit_to(p)


def emit_transpose_to(p, env, src_ap, src_key, dst_ap, dst_key, pt, evac_engine="vector"):
    ident = env["ident_bf"]
    p.add("tensor", lambda e: e.transpose(pt[:, 0:128], src_ap, ident[:]),
          reads=[src_key, ident.name], writes=[pt.name])
    p.add(evac_engine, lambda e: (e.tensor_copy(dst_ap, pt[:, 0:128])
                                  if evac_engine != "scalar" else e.copy(dst_ap, pt[:, 0:128])),
          reads=[pt.name], writes=[dst_key])


def emit_load_cast_w(p, env, w_hbm, rows0, rows1, cols0, cols1, dst, dst_off,
                     stage_pair, tag):
    W = cols1 - cols0
    n = (rows1 - rows0) // 128
    for i in range(n):
        stg = stage_pair[i % 2]
        p.add("sync", lambda e, i=i, stg=stg: e.dma_start(
            stg[:, 0:W], w_hbm.ap()[rows0 + i * 128: rows0 + (i + 1) * 128, cols0:cols1]),
            reads=[w_hbm.name], writes=[stg.name], dma=stg.name)
        p.add("vector", lambda e, i=i, stg=stg: e.tensor_copy(
            dst[:, dst_off + i * W: dst_off + (i + 1) * W], stg[:, 0:W]),
            reads=[stg.name], writes=[dst.name + tag])


def emit_hilo_row(p, env, src_hbm, n, row_f, row_hi, row_lo, row_tmp):
    p.add("sync", lambda e: e.dma_start(row_f[:, 0:n], src_hbm.ap()[0:1, 0:n]),
          reads=[src_hbm.name], writes=[row_f.name], dma=row_f.name)
    p.add("vector", lambda e: e.tensor_copy(row_hi[:, 0:n], row_f[:, 0:n]),
          reads=[row_f.name], writes=[row_hi.name])
    p.add("vector", lambda e: e.tensor_copy(row_tmp[:, 0:n], row_hi[:, 0:n]),
          reads=[row_hi.name], writes=[row_tmp.name])
    p.add("vector", lambda e: e.tensor_sub(row_tmp[:, 0:n], row_f[:, 0:n], row_tmp[:, 0:n]),
          reads=[row_f.name, row_tmp.name], writes=[row_tmp.name])
    p.add("vector", lambda e: e.tensor_copy(row_lo[:, 0:n], row_tmp[:, 0:n]),
          reads=[row_tmp.name], writes=[row_lo.name])



SBUF_TOP = 229344

_ESIZE = {F32: 4, BF16: 2}


class BAlloc:

    def __init__(self, nc, base, size):
        self.nc, self.base, self.size = nc, base, size
        self.off = 0

    def alloc_sbuf_tensor(self, name, shape, dtype):
        nb = (shape[-1] * _ESIZE[dtype] + 31) & ~31
        t = self.nc.alloc_sbuf_tensor_at(name, shape, dtype, offset=self.base + self.off)
        self.off += nb
        assert self.off <= self.size, f"arena overflow at {name}: {self.off}/{self.size}"
        return t

    def alloc_sbuf_tensor_at(self, name, shape, dtype, offset):
        return self.nc.alloc_sbuf_tensor_at(name, shape, dtype, offset=offset)

    def suballoc(self, nbytes):
        off = self.base + self.off
        self.off += (nbytes + 31) & ~31
        assert self.off <= self.size
        return off

    def reset(self):
        self.off = 0


def sbuf_arena(al, name, nbytes):
    return al.suballoc(nbytes)


def build_block_a(nc, p, al, Xi, Xo, WqT, WkT, Wv, env):
    sq = al.alloc_sbuf_tensor("a_sq", [128, 1024], F32)
    stats = al.alloc_sbuf_tensor("a_st", [128, 8], F32)
    xn_t2 = [al.alloc_sbuf_tensor(f"a_xnt{i}", [128, 1024], BF16) for i in range(2)]
    xn_fm = al.alloc_sbuf_tensor("a_xnfm", [128, FC * T], BF16)
    P_bf = al.alloc_sbuf_tensor("a_P", [128, 8192], BF16)
    V_tm = al.alloc_sbuf_tensor("a_V", [128, NT * 1024], BF16)
    hh_bf = al.alloc_sbuf_tensor("a_hh", [128, 1024], BF16)
    at_sb = [al.alloc_sbuf_tensor(f"a_at{i}", [128, 128], BF16) for i in range(2)]
    sm = al.alloc_sbuf_tensor("a_sm", [128, 8], F32)
    r1 = sbuf_arena(al, "a_reg1", 16384)
    r2 = sbuf_arena(al, "a_reg2", 16384)
    r3 = sbuf_arena(al, "a_reg3", 8192)
    wq_bf = al.alloc_sbuf_tensor_at("a_wq", [128, 8192], BF16, offset=r1)
    wv_bf = al.alloc_sbuf_tensor_at("a_wv", [128, 8192], BF16, offset=r1)
    s_row = al.alloc_sbuf_tensor_at("a_srow", [128, T], F32, offset=r1)
    wk_bf = al.alloc_sbuf_tensor_at("a_wk", [128, 8192], BF16, offset=r2)
    a_bf = al.alloc_sbuf_tensor_at("a_abf", [128, T], BF16, offset=r2)
    xnew = [al.alloc_sbuf_tensor_at(f"a_xn{i}", [128, 1024], F32, offset=r2 + 8192 + 4096 * i)
            for i in range(2)]
    wstage = [al.alloc_sbuf_tensor_at(f"a_wst{i}", [128, 1024], F32, offset=r3 + 4096 * i)
              for i in range(2)]
    xold = [al.alloc_sbuf_tensor_at(f"a_xo{i}", [128, 1024], F32, offset=r3 + 4096 * i)
            for i in range(2)]
    xin = [al.alloc_sbuf_tensor_at(f"a_xin{i}", [128, 1024], F32, offset=r3 + 4096 * i)
           for i in range(2)]

    q4 = env["psum_q"]
    pA, pB, pS = q4[0], q4[1], [q4[2], q4[3]]
    pT = env["psum_t"]
    pY = env["psum_o"]

    env = dict(env)
    env["sq"] = sq
    env["stats"] = stats

    for tt in range(NT):
        xn_t = xn_t2[0]
        def consume(p, tt=tt, xn_t=xn_t):
            for c in range(FC):
                emit_transpose_to(
                    p, env, xn_t[:, c * 128:(c + 1) * 128], xn_t.name,
                    xn_fm[:, c * T + tt * 128: c * T + (tt + 1) * 128],
                    f"{xn_fm.name}:{tt}",
                    pT[c % 2], evac_engine=("vector" if c % 2 == 0 else "scalar"))
        emit_ln_tile(p, env, Xi, tt, xin[tt % 2], xn_t, None, consume)

    p.region_switch(xin[0].name, wstage[0].name)
    p.region_switch(xin[1].name, wstage[1].name)
    emit_load_cast_w(p, env, WqT, 0, 1024, 0, 1024, wq_bf, 0, wstage, "")
    emit_load_cast_w(p, env, WkT, 0, 1024, 0, 1024, wk_bf, 0, wstage, "")
    for mq in range(8):
        for half in range(2):
            ps = pA if half == 0 else pB
            for dc in range(8):
                p.add("tensor", lambda e, mq=mq, half=half, dc=dc, ps=ps: e.matmul(
                    ps[:, 0:512],
                    wq_bf[:, dc * 1024 + mq * 128: dc * 1024 + (mq + 1) * 128],
                    wk_bf[:, dc * 1024 + half * 512: dc * 1024 + half * 512 + 512],
                    start=(dc == 0), stop=(dc == 7)),
                    reads=[wq_bf.name, wk_bf.name], writes=[ps.name])
            p.add("scalar", lambda e, mq=mq, half=half, ps=ps: e.copy(
                P_bf[:, mq * 1024 + half * 512: mq * 1024 + half * 512 + 512], ps[:, 0:512]),
                reads=[ps.name], writes=[P_bf.name])

    p.region_switch(wq_bf.name, wv_bf.name)
    emit_load_cast_w(p, env, Wv, 0, 1024, 0, 1024, wv_bf, 0, wstage, "")
    for tt in range(NT):
        for half in range(2):
            ps = pA if half == 0 else pB
            for fc in range(FC):
                p.add("tensor", lambda e, tt=tt, half=half, fc=fc, ps=ps: e.matmul(
                    ps[:, 0:512],
                    xn_fm[:, fc * T + tt * 128: fc * T + (tt + 1) * 128],
                    wv_bf[:, fc * 1024 + half * 512: fc * 1024 + half * 512 + 512],
                    start=(fc == 0), stop=(fc == 7)),
                    reads=[f"{xn_fm.name}:{tt}", wv_bf.name], writes=[ps.name])
            p.add("scalar", lambda e, tt=tt, half=half, ps=ps: e.copy(
                V_tm[:, tt * 1024 + half * 512: tt * 1024 + half * 512 + 512], ps[:, 0:512]),
                reads=[ps.name], writes=[f"{V_tm.name}:{tt}"])

    p.region_switch(wv_bf.name, s_row.name)
    p.region_switch(wk_bf.name, a_bf.name)
    p.region_switch(wk_bf.name, xnew[0].name)
    p.region_switch(wk_bf.name, xnew[1].name)
    p.region_switch(wstage[0].name, xold[0].name)
    p.region_switch(wstage[1].name, xold[1].name)
    for qt in range(NT):
        kxo = xold[qt % 2].name
        p.add("gpsimd", lambda e, qt=qt, b=xold[qt % 2]: e.dma_start(
            b[:], Xi.ap()[qt * 128:(qt + 1) * 128, :]), reads=[Xi.name], writes=[kxo], dma=kxo)
        for m in range(8):
            ps = pA if m < 4 else pB
            col = (m % 4) * 128
            for c in range(8):
                p.add("tensor", lambda e, m=m, c=c, ps=ps, col=col, qt=qt: e.matmul(
                    ps[:, col:col + 128],
                    P_bf[:, c * 1024 + m * 128: c * 1024 + (m + 1) * 128],
                    xn_fm[:, c * T + qt * 128: c * T + (qt + 1) * 128],
                    start=(c == 0), stop=(c == 7)),
                    reads=[P_bf.name, f"{xn_fm.name}:{qt}"], writes=[ps.name])
        p.add("vector", lambda e: e.tensor_copy(hh_bf[:, 0:512], pA[:, 0:512]),
              reads=[pA.name], writes=[hh_bf.name])
        p.add("vector", lambda e: e.tensor_copy(hh_bf[:, 512:1024], pB[:, 0:512]),
              reads=[pB.name], writes=[hh_bf.name])
        ng = (qt + 4) // 4
        for g in range(ng):
            ps = pS[g % 2]
            gk = [f"{xn_fm.name}:{g * 4 + j}" for j in range(4)]
            for m in range(8):
                p.add("tensor", lambda e, g=g, m=m, ps=ps: e.matmul(
                    ps[:, 0:512],
                    hh_bf[:, m * 128:(m + 1) * 128],
                    xn_fm[:, m * T + g * 512: m * T + g * 512 + 512],
                    start=(m == 0), stop=(m == 7)),
                    reads=[hh_bf.name] + gk, writes=[ps.name])
            if g == ng - 1:
                mask4 = env["mask4"]
                v = qt % 4
                p.add("vector", lambda e, ps=ps, ng=ng, v=v: e.tensor_tensor(
                    s_row[:, (ng - 1) * 512: ng * 512], ps[:, 0:512],
                    mask4[:, v * 512:(v + 1) * 512], ALU.add),
                    reads=[ps.name, mask4.name], writes=[s_row.name])
            else:
                eng = "vector" if g % 2 == 0 else "scalar"
                p.add(eng, lambda e, g=g, ps=ps, eng=eng: (
                    e.tensor_copy(s_row[:, g * 512:(g + 1) * 512], ps[:, 0:512])
                    if eng == "vector" else e.copy(s_row[:, g * 512:(g + 1) * 512], ps[:, 0:512])),
                    reads=[ps.name], writes=[s_row.name])
        L = ng * 512
        p.add("vector", lambda e, L=L: e.tensor_reduce(
            sm[:, 0:1], s_row[:, 0:L], AX.X, ALU.max, negate=True),
            reads=[s_row.name], writes=[sm.name + ":nm"])
        p.add("vector", lambda e: e.tensor_scalar(sm[:, 1:2], sm[:, 0:1], GAMMA0, None, ALU.mult),
              reads=[sm.name + ":nm"], writes=[sm.name + ":bm"])
        p.add("scalar", lambda e, L=L: e.activation(
            a_bf[:, 0:L], s_row[:, 0:L], AF.Exp, bias=sm[:, 1:2], scale=GAMMA0,
            accum_out=sm[:, 2:3]),
            reads=[s_row.name, sm.name + ":bm"], writes=[a_bf.name, sm.name + ":l"])
        p.add("vector", lambda e: e.reciprocal(sm[:, 3:4], sm[:, 2:3]),
              reads=[sm.name + ":l"], writes=[sm.name + ":il"])
        def emit_at(kb):
            pt = pT[kb % 2]
            ab = at_sb[kb % 2]
            p.add("tensor", lambda e, kb=kb, pt=pt: e.transpose(
                pt[:, 0:128], a_bf[:, kb * 128:(kb + 1) * 128], env["ident_bf"][:]),
                reads=[a_bf.name], writes=[pt.name])
            eng = "vector" if kb % 2 == 0 else "scalar"
            p.add(eng, lambda e, pt=pt, ab=ab, eng=eng: (
                e.tensor_copy(ab[:], pt[:, 0:128]) if eng == "vector"
                else e.copy(ab[:], pt[:, 0:128])),
                reads=[pt.name], writes=[ab.name])
        PIPELINE_Y_A = True
        if PIPELINE_Y_A:
            emit_at(0)
        for kb in range(qt + 1):
            if PIPELINE_Y_A:
                if kb + 1 <= qt:
                    emit_at(kb + 1)
            else:
                emit_at(kb)
            ab = at_sb[kb % 2]
            for half in range(2):
                p.add("tensor", lambda e, kb=kb, half=half, ab=ab, qt=qt: e.matmul(
                    pY[half][:, 0:512], ab[:],
                    V_tm[:, kb * 1024 + half * 512: kb * 1024 + half * 512 + 512],
                    start=(kb == 0), stop=(kb == qt)),
                    reads=[ab.name, f"{V_tm.name}:{kb}"], writes=[pY[half].name])
        kxn = xnew[qt % 2].name
        for half in range(2):
            p.add("vector", lambda e, qt=qt, half=half: e.scalar_tensor_tensor(
                xnew[qt % 2][:, half * 512:half * 512 + 512], pY[half][:, 0:512],
                sm[:, 3:4], xold[qt % 2][:, half * 512:half * 512 + 512],
                ALU.mult, ALU.add),
                reads=[pY[half].name, sm.name + ":il", kxo], writes=[kxn])
        p.add("gpsimd", lambda e, qt=qt: e.dma_start(
            Xo.ap()[qt * 128:(qt + 1) * 128, :], xnew[qt % 2][:]),
            reads=[kxn], writes=[Xo.name], dma="store_" + Xo.name)



def emit_load_cast(p, env, mk_ap, hbm_key, n, W, dst, dst_off, stage_pair, tag=""):
    for i in range(n):
        stg = stage_pair[i % len(stage_pair)]
        ap_i = mk_ap(i)
        p.add("sync", lambda e, ap_i=ap_i, stg=stg: e.dma_start(stg[:, 0:W], ap_i),
              reads=[hbm_key], writes=[stg.name], dma=stg.name)
        p.add("vector", lambda e, i=i, stg=stg: e.tensor_copy(
            dst[:, dst_off + i * W: dst_off + (i + 1) * W], stg[:, 0:W]),
            reads=[stg.name], writes=[dst.name + tag])


def build_block_b(nc, p, al, Xi, Xo, Wr, Wgs, Wus, Wds, Wgr, Wur, Wdr, env):
    xin = [al.alloc_sbuf_tensor(f"b_xin{i}", [128, 1024], F32) for i in range(2)]
    sq = al.alloc_sbuf_tensor("b_sq", [128, 1024], F32)
    stats = al.alloc_sbuf_tensor("b_st", [128, 8], F32)
    xn_t2 = [al.alloc_sbuf_tensor(f"b_xnt{i}", [128, 1024], BF16) for i in range(2)]
    xn_fm = al.alloc_sbuf_tensor("b_xnfm", [128, FC * 2048], BF16)
    acc = al.alloc_sbuf_tensor("b_acc", [128, 16 * 1024], F32)
    wstage = [al.alloc_sbuf_tensor(f"b_wst{i}", [128, 1024], F32) for i in range(2)]
    wg2 = [al.alloc_sbuf_tensor(f"b_wg{i}", [128, 4096], BF16) for i in range(2)]
    wu2 = [al.alloc_sbuf_tensor(f"b_wu{i}", [128, 4096], BF16) for i in range(2)]
    wd2 = [al.alloc_sbuf_tensor(f"b_wd{i}", [128, 4096], BF16) for i in range(2)]
    rw = al.alloc_sbuf_tensor("b_rw", [128, 128], BF16)
    hsb2 = [al.alloc_sbuf_tensor(f"b_h{i}", [128, 2048], BF16) for i in range(2)]
    ssb2 = [al.alloc_sbuf_tensor(f"b_s{i}", [128, 512], BF16) for i in range(2)]
    usb2 = [al.alloc_sbuf_tensor(f"b_u{i}", [128, 512], BF16) for i in range(2)]
    eb = al.alloc_sbuf_tensor("b_e", [128, 256], F32)
    pn = al.alloc_sbuf_tensor("b_pn", [128, 256], F32)
    comb = al.alloc_sbuf_tensor("b_comb", [128, 256], F32)
    t1 = al.alloc_sbuf_tensor("b_t1", [128, 256], F32)
    t2 = al.alloc_sbuf_tensor("b_t2", [128, 256], F32)
    st2 = al.alloc_sbuf_tensor("b_st2", [128, 112], F32)

    q4 = env["psum_q"]
    pR, pG, pU = q4[0], q4[1], q4[2]
    pO = env["psum_o"]
    pT = env["psum_t"]

    env = dict(env)
    env["sq"] = sq
    env["stats"] = stats

    emit_load_cast(p, env, lambda i: Wr.ap()[i * 128:(i + 1) * 128, :],
                   Wr.name, 8, 16, rw, 0, wstage)

    for h in range(2):
        tb = h * 16
        for tl in range(16):
            tt = tb + tl
            xn_t = xn_t2[tl % 2]
            def consume(p, tl=tl, xn_t=xn_t):
                for c in range(FC):
                    emit_transpose_to(
                        p, env, xn_t[:, c * 128:(c + 1) * 128], xn_t.name,
                        xn_fm[:, c * 2048 + tl * 128: c * 2048 + (tl + 1) * 128],
                        f"{xn_fm.name}:{tl}", pT[c % 2],
                        evac_engine=("vector" if c % 2 == 0 else "scalar"))
            emit_ln_tile(p, env, Xi, tt, xin[tl % 2], xn_t, None, consume)
            p.add("gpsimd", lambda e, tt=tt, tl=tl: e.dma_start(
                acc[:, tl * 1024:(tl + 1) * 1024], Xi.ap()[tt * 128:(tt + 1) * 128, :]),
                reads=[Xi.name], writes=[f"b_acc:{tl}"], dma=f"b_accld{tl % 4}")
        for tl in range(16):
            for fc in range(FC):
                p.add("tensor", lambda e, tl=tl, fc=fc: e.matmul(
                    pR[:, tl * 16:(tl + 1) * 16],
                    xn_fm[:, fc * 2048 + tl * 128: fc * 2048 + (tl + 1) * 128],
                    rw[:, fc * 16:(fc + 1) * 16],
                    start=(fc == 0), stop=(fc == 7)),
                    reads=[f"{xn_fm.name}:{tl}", rw.name], writes=[pR.name])
        p.add("scalar", lambda e: e.activation(eb[:], pR[:, 0:256], AF.Exp),
              reads=[pR.name], writes=[eb.name])
        e3 = lambda t: t[:].rearrange("p (a b) -> p a b", b=16)
        bc = lambda t, c0: t[:, c0:c0 + 16].unsqueeze(2).broadcast_to([128, 16, 16])
        p.add("vector", lambda e: e.tensor_reduce(
            st2[:, 0:16].rearrange("p (a b) -> p a b", b=1), e3(eb), AX.X, ALU.add),
            reads=[eb.name], writes=[st2.name + ":lr"])
        p.add("vector", lambda e: e.reciprocal(st2[:, 16:32], st2[:, 0:16]),
              reads=[st2.name + ":lr"], writes=[st2.name + ":ilr"])
        p.add("vector", lambda e: e.tensor_tensor(e3(pn), e3(eb), bc(st2, 16), ALU.mult),
              reads=[eb.name, st2.name + ":ilr"], writes=[pn.name])
        p.add("vector", lambda e: e.tensor_reduce(
            st2[:, 32:48].rearrange("p (a b) -> p a b", b=1), e3(pn), AX.X, ALU.max),
            reads=[pn.name], writes=[st2.name + ":m1"])
        p.add("vector", lambda e: e.tensor_tensor(e3(t1), e3(pn), bc(st2, 32), ALU.is_ge),
              reads=[pn.name, st2.name + ":m1"], writes=[t1.name])
        p.add("vector", lambda e: e.scalar_tensor_tensor(
            e3(t2), e3(t1), NEG_BIG, e3(pn), ALU.mult, ALU.add),
            reads=[t1.name, pn.name], writes=[t2.name])
        p.add("vector", lambda e: e.tensor_reduce(
            st2[:, 48:64].rearrange("p (a b) -> p a b", b=1), e3(t2), AX.X, ALU.max),
            reads=[t2.name], writes=[st2.name + ":m2"])
        p.add("vector", lambda e: e.tensor_tensor(st2[:, 64:80], st2[:, 32:48],
                                                  st2[:, 48:64], ALU.add),
              reads=[st2.name + ":m1", st2.name + ":m2"], writes=[st2.name + ":dn"])
        p.add("vector", lambda e: e.tensor_scalar(st2[:, 80:96], st2[:, 64:80], 1e-9,
                                                  None, ALU.add),
              reads=[st2.name + ":dn"], writes=[st2.name + ":dn2"])
        p.add("vector", lambda e: e.reciprocal(st2[:, 96:112], st2[:, 80:96]),
              reads=[st2.name + ":dn2"], writes=[st2.name + ":idn"])
        p.add("vector", lambda e: e.tensor_tensor(e3(t1), e3(pn), bc(st2, 48), ALU.is_ge),
              reads=[pn.name, st2.name + ":m2"], writes=[t1.name])
        p.add("vector", lambda e: e.tensor_tensor(e3(t2), e3(pn), e3(t1), ALU.mult),
              reads=[pn.name, t1.name], writes=[t2.name])
        p.add("vector", lambda e: e.tensor_tensor(e3(comb), e3(t2), bc(st2, 96), ALU.mult),
              reads=[t2.name, st2.name + ":idn"], writes=[comb.name])

        experts = [("s", n, 4, 512) for n in range(2)] + \
                  [("r", r, 2, 256) for r in range(16)]

        def load_expert(xi):
            kind, ei, nmc, cw = experts[xi]
            wg, wu, wd = wg2[xi % 2], wu2[xi % 2], wd2[xi % 2]
            if kind == "s":
                WG, WU, WD = Wgs, Wus, Wds
            else:
                WG, WU, WD = Wgr, Wur, Wdr
            emit_load_cast(p, env, lambda i, ei=ei, WG=WG: WG.ap()[ei, i * 128:(i + 1) * 128, :],
                           WG.name, 8, cw, wg, 0, wstage, tag=":g")
            emit_load_cast(p, env, lambda i, ei=ei, WU=WU: WU.ap()[ei, i * 128:(i + 1) * 128, :],
                           WU.name, 8, cw, wu, 0, wstage, tag=":u")
            emit_load_cast(p, env, lambda i, ei=ei, WD=WD: WD.ap()[ei, i * 128:(i + 1) * 128, :],
                           WD.name, nmc, 1024, wd, 0, wstage, tag=":d")

        if h == 0:
            load_expert(0)
        for xi, (kind, ei, nmc, cw) in enumerate(experts):
            if xi + 1 < len(experts):
                load_expert(xi + 1)
            elif h == 0:
                load_expert(0)
            wg, wu, wd = wg2[xi % 2], wu2[xi % 2], wd2[xi % 2]
            it = [0]

            def gu_h(tg):
                hsb = hsb2[tg % 2]
                for mc in range(nmc):
                    pGt = q4[1] if it[0] % 2 == 0 else q4[0]
                    pUt = q4[2] if it[0] % 2 == 0 else q4[3]
                    ssb = ssb2[it[0] % 2]
                    usb = usb2[it[0] % 2]
                    it[0] += 1
                    for fc in range(FC):
                        p.add("tensor", lambda e, tg=tg, mc=mc, fc=fc, cw=cw, wg=wg, pGt=pGt: e.matmul(
                            pGt[:, 0:512],
                            wg[:, fc * cw + mc * 128: fc * cw + (mc + 1) * 128],
                            xn_fm[:, fc * 2048 + tg * 512: fc * 2048 + tg * 512 + 512],
                            start=(fc == 0), stop=(fc == 7)),
                            reads=[wg.name + ":g"] + [f"{xn_fm.name}:{tg * 4 + j}" for j in range(4)],
                            writes=[pGt.name])
                    for fc in range(FC):
                        p.add("tensor", lambda e, tg=tg, mc=mc, fc=fc, cw=cw, wu=wu, pUt=pUt: e.matmul(
                            pUt[:, 0:512],
                            wu[:, fc * cw + mc * 128: fc * cw + (mc + 1) * 128],
                            xn_fm[:, fc * 2048 + tg * 512: fc * 2048 + tg * 512 + 512],
                            start=(fc == 0), stop=(fc == 7)),
                            reads=[wu.name + ":u"] + [f"{xn_fm.name}:{tg * 4 + j}" for j in range(4)],
                            writes=[pUt.name])
                    p.add("scalar", lambda e, pGt=pGt, ssb=ssb: e.activation(
                        ssb[:], pGt[:, 0:512], AF.Silu),
                        reads=[pGt.name], writes=[ssb.name])
                    p.add("vector", lambda e, pUt=pUt, usb=usb: e.tensor_copy(
                        usb[:], pUt[:, 0:512]),
                        reads=[pUt.name], writes=[usb.name])
                    p.add("vector", lambda e, mc=mc, ssb=ssb, usb=usb, hsb=hsb: e.tensor_tensor(
                        hsb[:, mc * 512:(mc + 1) * 512], ssb[:], usb[:], ALU.mult),
                        reads=[ssb.name, usb.name], writes=[f"{hsb.name}:{mc}"])

            def down(tg):
                hsb = hsb2[tg % 2]
                for tl2 in range(4):
                    tl = tg * 4 + tl2
                    for mc in range(nmc):
                        for hf in range(2):
                            p.add("tensor", lambda e, mc=mc, hf=hf, tl2=tl2, nmc=nmc, wd=wd, hsb=hsb: e.matmul(
                                pO[hf][:, 0:512],
                                hsb[:, mc * 512 + tl2 * 128: mc * 512 + (tl2 + 1) * 128],
                                wd[:, mc * 1024 + hf * 512: mc * 1024 + hf * 512 + 512],
                                start=(mc == 0), stop=(mc == nmc - 1)),
                                reads=[f"{hsb.name}:{mc}", wd.name + ":d"], writes=[pO[hf].name])
                    for hf in range(2):
                        csl = slice(tl * 1024 + hf * 512, tl * 1024 + hf * 512 + 512)
                        if kind == "s":
                            p.add("vector", lambda e, csl=csl, hf=hf: e.tensor_add(
                                acc[:, csl], acc[:, csl], pO[hf][:, 0:512]),
                                reads=[pO[hf].name, f"b_acc:{tl}"], writes=[f"b_acc:{tl}"])
                        else:
                            p.add("vector", lambda e, csl=csl, hf=hf, tl=tl, ei=ei: e.scalar_tensor_tensor(
                                acc[:, csl], pO[hf][:, 0:512],
                                comb[:, tl * 16 + ei: tl * 16 + ei + 1],
                                acc[:, csl], ALU.mult, ALU.add),
                                reads=[pO[hf].name, comb.name, f"b_acc:{tl}"],
                                writes=[f"b_acc:{tl}"])

            gu_h(0)
            for tg in range(4):
                if tg + 1 < 4:
                    gu_h(tg + 1)
                down(tg)
        for tl in range(16):
            tt = tb + tl
            p.add("gpsimd", lambda e, tt=tt, tl=tl: e.dma_start(
                Xo.ap()[tt * 128:(tt + 1) * 128, :], acc[:, tl * 1024:(tl + 1) * 1024]),
                reads=[f"b_acc:{tl}"], writes=[Xo.name], dma="b_store")



def build_block_c(nc, p, al, Xi, Xmid, Xo, WqT, WkT, WvT, WpT, Bq, Bk, Bv, Bp,
                  CosT, SinT, XnH, IlS, env):
    sq = al.alloc_sbuf_tensor("c_sq", [128, 1024], F32)
    stats = al.alloc_sbuf_tensor("c_st", [128, 8], F32)

    K_fm = al.alloc_sbuf_tensor("c_K", [128, 4 * T], BF16)
    V_tm = al.alloc_sbuf_tensor("c_V", [128, NT * 512], BF16)
    cos_sb = al.alloc_sbuf_tensor("c_cos", [128, 1024], BF16)
    sin_sb = al.alloc_sbuf_tensor("c_sin", [128, 1024], BF16)
    xnt_s = [al.alloc_sbuf_tensor(f"c_xns{i}", [128, 1024], BF16) for i in range(2)]
    xnfm_t2 = [al.alloc_sbuf_tensor(f"c_xnfm{i}", [128, 1024], BF16) for i in range(2)]
    wq_bf = al.alloc_sbuf_tensor("c_wq", [128, 4096], BF16)
    wp_bf = al.alloc_sbuf_tensor("c_wp", [128, 4096], BF16)
    kraw = al.alloc_sbuf_tensor("c_kraw", [128, 512], F32)
    krot2 = [al.alloc_sbuf_tensor(f"c_krot{i}", [128, 512], BF16) for i in range(2)]
    tmpa = al.alloc_sbuf_tensor("c_tma", [128, 256], F32)
    tmpb = al.alloc_sbuf_tensor("c_tmb", [128, 256], F32)
    abf = [al.alloc_sbuf_tensor(f"c_abf{i}", [128, T], BF16) for i in range(4)]
    at_sb = [al.alloc_sbuf_tensor(f"c_at{i}", [128, 512], BF16) for i in range(2)]
    qfm = al.alloc_sbuf_tensor("c_qfm", [128, 2048], BF16)
    yfm = al.alloc_sbuf_tensor("c_yfm", [128, 2048], F32)
    ybf = al.alloc_sbuf_tensor("c_ybf", [128, 2048], BF16)
    il_all = al.alloc_sbuf_tensor("c_il", [128, 32], F32)
    ilT = sq
    smx = al.alloc_sbuf_tensor("c_smx", [128, 80], F32)
    rhl = al.alloc_sbuf_tensor("c_rhl", [1, 6 * 512], BF16)
    bphl = al.alloc_sbuf_tensor("c_bp", [1, 2 * 1024], BF16)
    xnew = [al.alloc_sbuf_tensor(f"c_xn{i}", [128, 1024], F32) for i in range(2)]
    wstage = [al.alloc_sbuf_tensor("c_wst", [128, 1024], F32)]
    rows = sq
    rtmp = sq
    ar1 = sbuf_arena(al, "c_ar1", 8192)
    ar2 = sbuf_arena(al, "c_ar2", 16384)
    xin = [al.alloc_sbuf_tensor_at(f"c_xin{i}", [128, 1024], F32, offset=ar1 + 4096 * i)
           for i in range(2)]
    xold = [al.alloc_sbuf_tensor_at(f"c_xo{i}", [128, 1024], F32, offset=ar1 + 4096 * i)
            for i in range(2)]
    wk_bf = al.alloc_sbuf_tensor_at("c_wk", [128, 4096], BF16, offset=ar2)
    wv_bf = al.alloc_sbuf_tensor_at("c_wv", [128, 4096], BF16, offset=ar2 + 8192)
    s_row = al.alloc_sbuf_tensor_at("c_srow", [128, T], F32, offset=ar2)
    xn_t2 = [al.alloc_sbuf_tensor_at(f"c_xnt{i}", [128, 1024], BF16,
                                     offset=ar2 + 2048 * i) for i in range(2)]

    q4 = env["psum_q"]
    pP, pYH, pS = q4[0], q4[1], [q4[2], q4[3]]
    pT = env["psum_t"]
    pO = env["psum_o"]

    env = dict(env)
    env["sq"] = sq
    env["stats"] = stats
    ones_bf = env["ones_bf"]
    onesf = env["ones_f"]
    ident = env["ident_bf"]

    def hilo_into(src_hbm, c0, n, dsth, dstl, keyh, keyl):
        r0 = sq[0:1, 0:n]
        r1 = sq[0:1, 512:512 + n]
        p.add("sync", lambda e: e.dma_start(r0, src_hbm.ap()[0:1, c0:c0 + n]),
              reads=[src_hbm.name], writes=[sq.name], dma="c_rowld")
        p.add("vector", lambda e: e.tensor_copy(dsth, r0),
              reads=[sq.name], writes=[keyh])
        p.add("vector", lambda e: e.tensor_copy(r1, dsth),
              reads=[keyh], writes=[sq.name])
        p.add("vector", lambda e: e.tensor_sub(r1, r0, r1),
              reads=[sq.name], writes=[sq.name])
        p.add("vector", lambda e: e.tensor_copy(dstl, r1),
              reads=[sq.name], writes=[keyl])

    def hilo(src_hbm, c0, n, slot):
        assert n <= 512
        dsth = rhl[:, slot * 512: slot * 512 + n]
        dstl = rhl[:, (slot + 1) * 512: (slot + 1) * 512 + n]
        keyh = rhl.name + f":{slot}h"
        keyl = rhl.name + f":{slot}l"
        hilo_into(src_hbm, c0, n, dsth, dstl, keyh, keyl)
        return dsth, dstl, [keyh, keyl]


    for nmsb, hbm in ((cos_sb, CosT), (sin_sb, SinT)):
        p.add("sync", lambda e, nmsb=nmsb, hbm=hbm: e.dma_start(
            wstage[0][:].rearrange("p (a b) -> p a b", b=32),
            hbm.ap().rearrange("(a p) b -> p a b", p=128)),
            reads=[hbm.name], writes=[wstage[0].name], dma=wstage[0].name)
        p.add("vector", lambda e, nmsb=nmsb: e.tensor_copy(nmsb[:], wstage[0][:]),
              reads=[wstage[0].name], writes=[nmsb.name])

    def rope(src_f32, dst_bf, tile):
        va = lambda t: t[:].rearrange("p (h x) -> p h x", x=64)[:, :, 0:32]
        vb = lambda t: t[:].rearrange("p (h x) -> p h x", x=64)[:, :, 32:64]
        cosb = cos_sb[:, tile * 32:(tile + 1) * 32].unsqueeze(1).broadcast_to([128, 8, 32])
        sinb = sin_sb[:, tile * 32:(tile + 1) * 32].unsqueeze(1).broadcast_to([128, 8, 32])
        t3 = lambda t: t[:].rearrange("p (h x) -> p h x", x=32)
        ks, kd = src_f32.name, dst_bf.name
        p.add("vector", lambda e: e.tensor_tensor(t3(tmpa), va(src_f32), cosb, ALU.mult),
              reads=[ks, cos_sb.name], writes=[tmpa.name])
        p.add("vector", lambda e: e.tensor_tensor(t3(tmpb), vb(src_f32), sinb, ALU.mult),
              reads=[ks, sin_sb.name], writes=[tmpb.name])
        p.add("vector", lambda e: e.tensor_tensor(va(dst_bf), t3(tmpa), t3(tmpb), ALU.subtract),
              reads=[tmpa.name, tmpb.name], writes=[kd])
        p.add("vector", lambda e: e.tensor_tensor(t3(tmpa), va(src_f32), sinb, ALU.mult),
              reads=[ks, sin_sb.name], writes=[tmpa.name])
        p.add("vector", lambda e: e.tensor_tensor(t3(tmpb), vb(src_f32), cosb, ALU.mult),
              reads=[ks, cos_sb.name], writes=[tmpb.name])
        p.add("vector", lambda e: e.tensor_tensor(vb(dst_bf), t3(tmpa), t3(tmpb), ALU.add),
              reads=[tmpa.name, tmpb.name], writes=[kd])

    def xn_tile_to_fm(tile, stream_slot):
        xs = xnt_s[stream_slot]
        xnfm_t = xnfm_t2[stream_slot]
        p.add("sync", lambda e, tile=tile, xs=xs: e.dma_start(
            xs[:], XnH.ap()[tile * 128:(tile + 1) * 128, :]),
            reads=[XnH.name], writes=[xs.name], dma=xs.name)
        for c in range(FC):
            emit_transpose_to(p, env, xs[:, c * 128:(c + 1) * 128], xs.name,
                              xnfm_t[:, c * 128:(c + 1) * 128], xnfm_t.name,
                              pT[c % 2],
                              evac_engine=("vector" if c % 2 == 0 else "scalar"))
        return xnfm_t

    GQ = 8
    for H in range(2):
        c0 = H * 512
        if H == 0:
            pass
        else:
            p.region_switch(s_row.name, wk_bf.name)
            p.region_switch(s_row.name, wv_bf.name)
        emit_load_cast(p, env, lambda i, c0=c0: WkT.ap()[i * 128:(i + 1) * 128, c0:c0 + 512],
                       WkT.name, 8, 512, wk_bf, 0, wstage)
        emit_load_cast(p, env, lambda i: WvT.ap()[i * 128:(i + 1) * 128, c0:c0 + 512],
                       WvT.name, 8, 512, wv_bf, 0, wstage)
        bk_h, bk_l, bk_keys = hilo(Bk, c0, 512, 0)
        bv_h, bv_l, bv_keys = hilo(Bv, c0, 512, 2)
        for kt in range(NT):
            if H == 0:
                xn_t = xnt_s[kt % 2]
                def consume(p, kt=kt, xn_t=xn_t):
                    p.add("gpsimd", lambda e, kt=kt, xn_t=xn_t: e.dma_start(
                        XnH.ap()[kt * 128:(kt + 1) * 128, :], xn_t[:]),
                        reads=[xn_t.name], writes=[XnH.name], dma="c_xnst")
                emit_ln_tile(p, env, Xi, kt, xin[kt % 2], xn_t, None, consume)
                xnfm_t = xnfm_t2[kt % 2]
                for c in range(FC):
                    emit_transpose_to(p, env, xn_t[:, c * 128:(c + 1) * 128], xn_t.name,
                                      xnfm_t[:, c * 128:(c + 1) * 128], xnfm_t.name,
                                      pT[c % 2],
                                      evac_engine=("vector" if c % 2 == 0 else "scalar"))
            else:
                xnfm_t = xn_tile_to_fm(kt, kt % 2)
            krot = krot2[kt % 2]
            for fc in range(FC):
                p.add("tensor", lambda e, fc=fc, xnfm_t=xnfm_t: e.matmul(
                    pP[:, 0:512], xnfm_t[:, fc * 128:(fc + 1) * 128],
                    wk_bf[:, fc * 512:(fc + 1) * 512],
                    start=(fc == 0), stop=False),
                    reads=[xnfm_t.name, wk_bf.name], writes=[pP.name])
            p.add("tensor", lambda e: e.matmul(pP[:, 0:512], ones_bf[:], bk_h,
                                               start=False, stop=False),
                  reads=[ones_bf.name] + bk_keys, writes=[pP.name])
            p.add("tensor", lambda e: e.matmul(pP[:, 0:512], ones_bf[:], bk_l,
                                               start=False, stop=True),
                  reads=[ones_bf.name] + bk_keys, writes=[pP.name])
            p.add("vector", lambda e, kraw=kraw: e.tensor_copy(kraw[:], pP[:, 0:512]),
                  reads=[pP.name], writes=[kraw.name])
            rope(kraw, krot, kt)
            for c in range(4):
                emit_transpose_to(p, env, krot[:, c * 128:(c + 1) * 128], krot.name,
                                  K_fm[:, c * T + kt * 128: c * T + (kt + 1) * 128],
                                  f"{K_fm.name}:{kt}", pT[c % 2],
                                  evac_engine=("vector" if c % 2 == 0 else "scalar"))
            for fc in range(FC):
                p.add("tensor", lambda e, fc=fc, xnfm_t=xnfm_t: e.matmul(
                    pP[:, 0:512], xnfm_t[:, fc * 128:(fc + 1) * 128],
                    wv_bf[:, fc * 512:(fc + 1) * 512],
                    start=(fc == 0), stop=False),
                    reads=[xnfm_t.name, wv_bf.name], writes=[pP.name])
            p.add("tensor", lambda e: e.matmul(pP[:, 0:512], ones_bf[:], bv_h,
                                               start=False, stop=False),
                  reads=[ones_bf.name] + bv_keys, writes=[pP.name])
            p.add("tensor", lambda e: e.matmul(pP[:, 0:512], ones_bf[:], bv_l,
                                               start=False, stop=True),
                  reads=[ones_bf.name] + bv_keys, writes=[pP.name])
            p.add("scalar", lambda e, kt=kt: e.copy(
                V_tm[:, kt * 512:(kt + 1) * 512], pP[:, 0:512]),
                reads=[pP.name], writes=[f"{V_tm.name}:{kt}"])

        if H == 0:
            p.region_switch(xin[0].name, xold[0].name)
            p.region_switch(xin[1].name, xold[1].name)
        p.region_switch(wk_bf.name, s_row.name)
        p.region_switch(wv_bf.name, s_row.name)
        emit_load_cast(p, env, lambda i, c0=c0: WqT.ap()[i * 128:(i + 1) * 128, c0:c0 + 512],
                       WqT.name, 8, 512, wq_bf, 0, wstage)
        emit_load_cast(p, env, lambda i, c0=c0: WpT.ap()[c0 + i * 128: c0 + (i + 1) * 128, :],
                       WpT.name, 4, 1024, wp_bf, 0, wstage)
        bq_h, bq_l, bq_keys = hilo(Bq, c0, 512, 4)
        if H == 0:
            for hseg in range(2):
                hilo_into(Bp, hseg * 512, 512,
                          bphl[:, hseg * 512: hseg * 512 + 512],
                          bphl[:, 1024 + hseg * 512: 1024 + hseg * 512 + 512],
                          bphl.name + ":h", bphl.name + ":l")
        Xsrc = Xi if H == 0 else Xmid
        Xdst = Xmid if H == 0 else Xo
        for g in range(GQ):
            for qtl in range(4):
                qt = g * 4 + qtl
                xnfm_q = xn_tile_to_fm(qt, qtl % 2)
                for fc in range(FC):
                    p.add("tensor", lambda e, fc=fc, xnfm_q=xnfm_q: e.matmul(
                        pP[:, 0:512], xnfm_q[:, fc * 128:(fc + 1) * 128],
                        wq_bf[:, fc * 512:(fc + 1) * 512],
                        start=(fc == 0), stop=False),
                        reads=[xnfm_q.name, wq_bf.name], writes=[pP.name])
                p.add("tensor", lambda e: e.matmul(pP[:, 0:512], ones_bf[:], bq_h,
                                                   start=False, stop=False),
                      reads=[ones_bf.name] + bq_keys, writes=[pP.name])
                p.add("tensor", lambda e: e.matmul(pP[:, 0:512], ones_bf[:], bq_l,
                                                   start=False, stop=True),
                      reads=[ones_bf.name] + bq_keys, writes=[pP.name])
                p.add("vector", lambda e, kraw=kraw: e.tensor_copy(kraw[:], pP[:, 0:512]),
                      reads=[pP.name], writes=[kraw.name])
                krot = krot2[qtl % 2]
                rope(kraw, krot, qt)
                for c in range(4):
                    emit_transpose_to(
                        p, env, krot[:, c * 128:(c + 1) * 128], krot.name,
                        qfm[:, qtl * 512 + c * 128: qtl * 512 + (c + 1) * 128],
                        f"{qfm.name}:{qtl}", pT[c % 2],
                        evac_engine=("vector" if c % 2 == 0 else "scalar"))
            ng = g + 1
            L = ng * 512
            mask4 = env["mask4"]
            slots = [pS[0], pS[1], pP, pO[0], pO[1]]
            for hh in range(8):
                c = hh // 2
                pb = (hh % 2) * 64
                for qtl in range(4):
                    qt = g * 4 + qtl
                    v = qt % 4
                    sb = qtl * 20
                    kgm = f"{smx.name}:{qtl}:gm"
                    klsp = f"{smx.name}:{qtl}:lsp"
                    kbm = f"{smx.name}:{qtl}:bm"
                    kls = f"{smx.name}:{qtl}:ls"
                    km = f"{smx.name}:{qtl}:m"
                    direct = ng <= NDIR
                    for g2 in range(ng):
                        ps = slots[(qtl * ng + g2) % len(slots)] if direct else slots[g2 % 3]
                        kk = [f"{K_fm.name}:{4 * g2 + j}" for j in range(4)]
                        p.add("tensor", lambda e, g2=g2, qtl=qtl, c=c, pb=pb, ps=ps: e.matmul(
                            ps[:, 0:512],
                            qfm[pb:pb + 64, qtl * 512 + c * 128: qtl * 512 + (c + 1) * 128],
                            K_fm[pb:pb + 64, c * T + g2 * 512: c * T + g2 * 512 + 512],
                            start=True, stop=True),
                            reads=[f"{qfm.name}:{qtl}"] + kk, writes=[ps.name])
                        if direct:
                            if g2 == ng - 1:
                                p.add("vector", lambda e, ps=ps, v=v: e.tensor_tensor(
                                    ps[:, 0:512], ps[:, 0:512],
                                    mask4[:, v * 512:(v + 1) * 512], ALU.add),
                                    reads=[ps.name, mask4.name], writes=[ps.name])
                            p.add("vector", lambda e, ps=ps, g2=g2, sb=sb: e.tensor_reduce(
                                smx[:, sb + g2: sb + g2 + 1], ps[:, 0:512], AX.X, ALU.max),
                                reads=[ps.name], writes=[kgm])
                        else:
                            if g2 == ng - 1:
                                p.add("vector", lambda e, ps=ps, g2=g2, v=v: e.tensor_tensor(
                                    s_row[:, g2 * 512: g2 * 512 + 512], ps[:, 0:512],
                                    mask4[:, v * 512:(v + 1) * 512], ALU.add),
                                    reads=[ps.name, mask4.name], writes=[s_row.name])
                            else:
                                eng = "vector" if g2 % 2 == 0 else "scalar"
                                p.add(eng, lambda e, g2=g2, ps=ps, eng=eng: (
                                    e.tensor_copy(s_row[:, g2 * 512:(g2 + 1) * 512], ps[:, 0:512])
                                    if eng == "vector" else
                                    e.copy(s_row[:, g2 * 512:(g2 + 1) * 512], ps[:, 0:512])),
                                    reads=[ps.name], writes=[s_row.name])
                            p.add("vector", lambda e, g2=g2, sb=sb: e.tensor_reduce(
                                smx[:, sb + g2: sb + g2 + 1],
                                s_row[:, g2 * 512:(g2 + 1) * 512], AX.X, ALU.max),
                                reads=[s_row.name], writes=[kgm])
                    p.add("vector", lambda e, sb=sb, ng=ng: e.tensor_reduce(
                        smx[:, sb + 18: sb + 19], smx[:, sb: sb + ng], AX.X, ALU.max),
                        reads=[kgm], writes=[km])
                    p.add("vector", lambda e, sb=sb: e.tensor_scalar(
                        smx[:, sb + 16: sb + 17], smx[:, sb + 18: sb + 19], -0.125,
                        None, ALU.mult),
                        reads=[km], writes=[kbm])
                    if direct:
                        for g2 in range(ng):
                            ps = slots[(qtl * ng + g2) % len(slots)]
                            p.add("scalar", lambda e, ps=ps, g2=g2, qtl=qtl, sb=sb: e.activation(
                                abf[qtl][:, g2 * 512:(g2 + 1) * 512], ps[:, 0:512],
                                AF.Exp, bias=smx[:, sb + 16: sb + 17], scale=0.125,
                                accum_out=smx[:, sb + 8 + g2: sb + 9 + g2]),
                                reads=[ps.name, kbm],
                                writes=[f"{abf[qtl].name}:{g2}", klsp])
                        p.add("vector", lambda e, sb=sb, ng=ng: e.tensor_reduce(
                            smx[:, sb + 17: sb + 18], smx[:, sb + 8: sb + 8 + ng],
                            AX.X, ALU.add),
                            reads=[klsp], writes=[kls])
                    else:
                        p.add("scalar", lambda e, L=L, qtl=qtl, sb=sb: e.activation(
                            abf[qtl][:, 0:L], s_row[:, 0:L], AF.Exp,
                            bias=smx[:, sb + 16: sb + 17], scale=0.125,
                            accum_out=smx[:, sb + 17: sb + 18]),
                            reads=[s_row.name, kbm],
                            writes=[kls] + [f"{abf[qtl].name}:{gg2}" for gg2 in range(ng)])
                    p.add("vector", lambda e, qtl=qtl, hh=hh, sb=sb: e.reciprocal(
                        il_all[:, qtl * 8 + hh: qtl * 8 + hh + 1],
                        smx[:, sb + 17: sb + 18]),
                        reads=[kls], writes=[il_all.name])
                nkb = 4 * ng
                def emit_at(kb):
                    pt = pT[kb % 2]
                    ab = at_sb[kb % 2]
                    for qtl in range(4):
                        p.add("tensor", lambda e, kb=kb, qtl=qtl, pt=pt: e.transpose(
                            pt[:, qtl * 128:(qtl + 1) * 128],
                            abf[qtl][:, kb * 128:(kb + 1) * 128], ident[:]),
                            reads=[f"{abf[qtl].name}:{kb // 4}", ident.name],
                            writes=[pt.name])
                    eng = "vector" if kb % 2 == 0 else "scalar"
                    p.add(eng, lambda e, pt=pt, ab=ab, eng=eng: (
                        e.tensor_copy(ab[:], pt[:, 0:512]) if eng == "vector"
                        else e.copy(ab[:], pt[:, 0:512])),
                        reads=[pt.name], writes=[ab.name])
                emit_at(0)
                for kb in range(nkb):
                    if kb + 1 < nkb:
                        emit_at(kb + 1)
                    ab = at_sb[kb % 2]
                    p.add("tensor", lambda e, kb=kb, hh=hh, ab=ab, nkb=nkb: e.matmul(
                        pYH[0:64, 0:512],
                        V_tm[:, kb * 512 + hh * 64: kb * 512 + (hh + 1) * 64],
                        ab[:], start=(kb == 0), stop=(kb == nkb - 1)),
                        reads=[f"{V_tm.name}:{kb}", ab.name], writes=[pYH.name])
                p.add("vector" if hh % 2 == 0 else "scalar",
                      lambda e, hh=hh, c=c, pb=pb: (
                          e.tensor_copy(yfm[pb:pb + 64, c * 512:(c + 1) * 512],
                                        pYH[0:64, 0:512])
                          if hh % 2 == 0 else
                          e.copy(yfm[pb:pb + 64, c * 512:(c + 1) * 512],
                                 pYH[0:64, 0:512])),
                      reads=[pYH.name], writes=[yfm.name])
            identf = env["ident_f"]
            p.add("tensor", lambda e, identf=identf: e.transpose(
                pS[1][0:32, 0:128], il_all[:], identf[:]),
                reads=[il_all.name, identf.name], writes=[pS[1].name])
            p.add("vector", lambda e: e.tensor_copy(sq[64:96, 0:128], pS[1][0:32, 0:128]),
                  reads=[pS[1].name], writes=[sq.name])
            p.add("gpsimd", lambda e: e.dma_start(IlS.ap(), sq[64:96, 0:128]),
                  reads=[sq.name], writes=[IlS.name], dma="c_ils")
            for c in range(4):
                src = IlS.ap().rearrange("(q hh) c -> q hh c", hh=8)[:, 2 * c:2 * c + 2, :]
                dst = sq[0:1, 0:1024].rearrange("o (q s c) -> o q s c", q=4, s=2)
                p.add("sync", lambda e, src=src, dst=dst: e.dma_start(dst, src),
                      reads=[IlS.name], writes=[sq.name], dma="c_ilr")
                for sub in range(2):
                    for qtl in range(4):
                        p.add("tensor", lambda e, qtl=qtl, sub=sub: e.matmul(
                            pYH[sub * 64:(sub + 1) * 64, qtl * 128:(qtl + 1) * 128],
                            onesf[0:1, 0:64],
                            sq[0:1, qtl * 256 + sub * 128: qtl * 256 + (sub + 1) * 128],
                            start=True, stop=True),
                            reads=[sq.name, onesf.name], writes=[pYH.name])
                p.add("vector", lambda e, c=c: e.tensor_tensor(
                    ybf[:, c * 512:(c + 1) * 512], yfm[:, c * 512:(c + 1) * 512],
                    pYH[:, 0:512], ALU.mult),
                    reads=[yfm.name, pYH.name], writes=[ybf.name])
            for qtl in range(4):
                qt = g * 4 + qtl
                p.add("gpsimd", lambda e, qt=qt, qtl=qtl, Xsrc=Xsrc: e.dma_start(
                    xold[qtl % 2][:], Xsrc.ap()[qt * 128:(qt + 1) * 128, :]),
                    reads=[Xsrc.name], writes=[xold[qtl % 2].name],
                    dma=xold[qtl % 2].name)
                for c in range(4):
                    for hf in range(2):
                        p.add("tensor", lambda e, c=c, hf=hf, qtl=qtl: e.matmul(
                            pO[hf][:, 0:512],
                            ybf[:, c * 512 + qtl * 128: c * 512 + (qtl + 1) * 128],
                            wp_bf[:, c * 1024 + hf * 512: c * 1024 + hf * 512 + 512],
                            start=(c == 0), stop=(c == 3 and H == 1)),
                            reads=[ybf.name, wp_bf.name], writes=[pO[hf].name])
                if H == 0:
                    for hf in range(2):
                        p.add("tensor", lambda e, hf=hf: e.matmul(
                            pO[hf][:, 0:512], ones_bf[:],
                            bphl[:, hf * 512: hf * 512 + 512],
                            start=False, stop=False),
                            reads=[ones_bf.name, bphl.name + ":h"], writes=[pO[hf].name])
                        p.add("tensor", lambda e, hf=hf: e.matmul(
                            pO[hf][:, 0:512], ones_bf[:],
                            bphl[:, 1024 + hf * 512: 1024 + hf * 512 + 512],
                            start=False, stop=True),
                            reads=[ones_bf.name, bphl.name + ":l"], writes=[pO[hf].name])
                kxn = xnew[qtl % 2].name
                for hf in range(2):
                    p.add("vector", lambda e, qtl=qtl, hf=hf: e.tensor_add(
                        xnew[qtl % 2][:, hf * 512: hf * 512 + 512], pO[hf][:, 0:512],
                        xold[qtl % 2][:, hf * 512: hf * 512 + 512]),
                        reads=[pO[hf].name, xold[qtl % 2].name], writes=[kxn])
                p.add("gpsimd", lambda e, qt=qt, qtl=qtl, Xdst=Xdst: e.dma_start(
                    Xdst.ap()[qt * 128:(qt + 1) * 128, :], xnew[qtl % 2][:]),
                    reads=[kxn], writes=[Xdst.name], dma="c_st" + Xdst.name)



def build_block_d(nc, p, al, Xi, Xo, W1, B1c, W2, B2, WrT, Br, env):
    rxin = al.suballoc(8192)
    xin = [al.alloc_sbuf_tensor_at(f"d_xin{i}", [128, 1024], F32, offset=rxin + 4096 * i)
           for i in range(2)]
    stats = al.alloc_sbuf_tensor("d_st", [128, 8], F32)
    xn_fm = al.alloc_sbuf_tensor("d_xnfm", [128, FC * T], BF16)
    ract = al.suballoc(65536)
    act = al.alloc_sbuf_tensor_at("d_act", [128, 16 * 2048], BF16, offset=ract)
    sq = al.alloc_sbuf_tensor_at("d_sq", [128, 1024], F32, offset=ract)
    xn_t2 = [al.alloc_sbuf_tensor_at(f"d_xnt{i}", [128, 1024], BF16,
                                     offset=ract + 4096 + 2048 * i) for i in range(2)]
    w2_bf = al.alloc_sbuf_tensor("d_w2", [128, 16 * 1024], BF16)
    w1a = [al.alloc_sbuf_tensor(f"d_w1a{i}", [128, 1024], BF16) for i in range(2)]
    w1g = [al.alloc_sbuf_tensor(f"d_w1g{i}", [128, 1024], BF16) for i in range(2)]
    w1stage = [al.alloc_sbuf_tensor(f"d_w1s{i}", [128, 1024], F32) for i in range(2)]
    w1stage += [al.alloc_sbuf_tensor_at(f"d_w1s{i+2}", [128, 1024], F32,
                                        offset=rxin + 4096 * i) for i in range(2)]
    wstage = w1stage[0:2]
    rw = al.alloc_sbuf_tensor("d_rw", [128, 64], BF16)
    r_rt = al.suballoc(4096)
    lg = al.alloc_sbuf_tensor_at("d_lg", [128, 256], F32, offset=r_rt)
    ev = al.alloc_sbuf_tensor_at("d_ev", [128, 256], F32, offset=r_rt + 1024)
    t1 = al.alloc_sbuf_tensor_at("d_t1", [128, 256], F32, offset=r_rt + 2048)
    t2 = al.alloc_sbuf_tensor_at("d_t2", [128, 256], F32, offset=r_rt + 3072)
    comb = al.alloc_sbuf_tensor("d_comb", [128, 256], F32)
    combb = al.alloc_sbuf_tensor("d_combb", [128, 256], BF16)
    st2 = al.alloc_sbuf_tensor("d_st2", [128, 112], F32)
    b1c = al.alloc_sbuf_tensor("d_b1", [128, 32], F32)
    brrow = al.alloc_sbuf_tensor("d_br", [1, 8], F32)
    brhl = al.alloc_sbuf_tensor("d_brhl", [1, 16], BF16)
    brt = al.alloc_sbuf_tensor("d_brt", [1, 8], F32)
    combT = al.alloc_sbuf_tensor("d_cT", [8, 128], BF16)
    r_sg = al.suballoc(2048)
    b2bf = al.alloc_sbuf_tensor_at("d_b2b", [8, 1024], BF16, offset=r_sg)
    ssb = [al.alloc_sbuf_tensor_at(f"d_s{i}", [128, 512], BF16, offset=r_sg + 1024 * i)
           for i in range(2)]
    gsb = [al.alloc_sbuf_tensor(f"d_g{i}", [128, 512], BF16) for i in range(2)]
    obuf = [al.alloc_sbuf_tensor_at("d_ob0", [128, 1024], F32, offset=r_rt),
            al.alloc_sbuf_tensor("d_ob1", [128, 1024], F32)]

    q4 = env["psum_q"]
    pR, pA, pG, pC = q4[0], q4[1], q4[2], q4[3]
    pO = env["psum_o"]
    pT = env["psum_t"]

    env = dict(env)
    env["sq"] = sq
    env["stats"] = stats
    ones_bf = env["ones_bf"]
    ident = env["ident_bf"]

    for tt in range(NT):
        xn_t = xn_t2[tt % 2]
        def consume(p, tt=tt, xn_t=xn_t):
            for c in range(FC):
                emit_transpose_to(
                    p, env, xn_t[:, c * 128:(c + 1) * 128], xn_t.name,
                    xn_fm[:, c * T + tt * 128: c * T + (tt + 1) * 128],
                    f"{xn_fm.name}:{tt}", pT[c % 2],
                    evac_engine=("vector" if c % 2 == 0 else "scalar"))
        emit_ln_tile(p, env, Xi, tt, xin[tt % 2], xn_t, None, consume)

    emit_load_cast(p, env, lambda i: WrT.ap()[i * 128:(i + 1) * 128, :],
                   WrT.name, 8, 8, rw, 0, wstage)
    p.add("sync", lambda e: e.dma_start(brrow[:], Br.ap()[0:1, :]),
          reads=[Br.name], writes=[brrow.name], dma=brrow.name)
    p.add("vector", lambda e: e.tensor_copy(brhl[:, 0:8], brrow[:]),
          reads=[brrow.name], writes=[brhl.name + ":h"])
    p.add("vector", lambda e: e.tensor_copy(brt[:], brhl[:, 0:8]),
          reads=[brhl.name + ":h"], writes=[brt.name])
    p.add("vector", lambda e: e.tensor_sub(brt[:], brrow[:], brt[:]),
          reads=[brrow.name, brt.name], writes=[brt.name])
    p.add("vector", lambda e: e.tensor_copy(brhl[:, 8:16], brt[:]),
          reads=[brt.name], writes=[brhl.name + ":l"])
    for tt in range(NT):
        for fc in range(FC):
            p.add("tensor", lambda e, tt=tt, fc=fc: e.matmul(
                pR[:, tt * 8:(tt + 1) * 8],
                xn_fm[:, fc * T + tt * 128: fc * T + (tt + 1) * 128],
                rw[:, fc * 8:(fc + 1) * 8], start=(fc == 0), stop=False),
                reads=[f"{xn_fm.name}:{tt}", rw.name], writes=[pR.name])
        p.add("tensor", lambda e, tt=tt: e.matmul(
            pR[:, tt * 8:(tt + 1) * 8], ones_bf[:], brhl[:, 0:8],
            start=False, stop=False),
            reads=[ones_bf.name, brhl.name + ":h"], writes=[pR.name])
        p.add("tensor", lambda e, tt=tt: e.matmul(
            pR[:, tt * 8:(tt + 1) * 8], ones_bf[:], brhl[:, 8:16],
            start=False, stop=True),
            reads=[ones_bf.name, brhl.name + ":l"], writes=[pR.name])
    p.add("vector", lambda e: e.tensor_copy(lg[:], pR[:, 0:256]),
          reads=[pR.name], writes=[lg.name])
    e3 = lambda t: t[:].rearrange("p (a b) -> p a b", b=8)
    bc = lambda t, col0: t[:, col0:col0 + 32].unsqueeze(2).broadcast_to([128, 32, 8])
    p.add("vector", lambda e: e.tensor_reduce(
        st2[:, 0:32].rearrange("p (a b) -> p a b", b=1), e3(lg), AX.X, ALU.max),
        reads=[lg.name], writes=[st2.name + ":m1"])
    p.add("vector", lambda e: e.tensor_tensor(e3(t1), e3(lg), bc(st2, 0), ALU.is_ge),
          reads=[lg.name, st2.name + ":m1"], writes=[t1.name])
    p.add("vector", lambda e: e.scalar_tensor_tensor(
        e3(t2), e3(t1), NEG_BIG, e3(lg), ALU.mult, ALU.add),
        reads=[t1.name, lg.name], writes=[t2.name])
    p.add("vector", lambda e: e.tensor_reduce(
        st2[:, 32:64].rearrange("p (a b) -> p a b", b=1), e3(t2), AX.X, ALU.max),
        reads=[t2.name], writes=[st2.name + ":m2"])
    p.add("vector", lambda e: e.tensor_tensor(e3(t1), e3(lg), bc(st2, 0), ALU.subtract),
          reads=[lg.name, st2.name + ":m1"], writes=[t1.name])
    p.add("scalar", lambda e: e.activation(ev[:], t1[:], AF.Exp),
          reads=[t1.name], writes=[ev.name])
    p.add("vector", lambda e: e.tensor_sub(st2[:, 64:96], st2[:, 32:64], st2[:, 0:32]),
          reads=[st2.name + ":m2", st2.name + ":m1"], writes=[st2.name + ":dm"])
    p.add("scalar", lambda e: e.activation(st2[:, 64:96], st2[:, 64:96], AF.Exp),
          reads=[st2.name + ":dm"], writes=[st2.name + ":ed"])
    p.add("vector", lambda e: e.tensor_scalar(st2[:, 64:96], st2[:, 64:96], 1.0,
                                              None, ALU.add),
          reads=[st2.name + ":ed"], writes=[st2.name + ":dn"])
    p.add("vector", lambda e: e.reciprocal(st2[:, 64:96], st2[:, 64:96]),
          reads=[st2.name + ":dn"], writes=[st2.name + ":idn"])
    p.add("vector", lambda e: e.tensor_tensor(e3(t2), e3(lg), bc(st2, 32), ALU.is_ge),
          reads=[lg.name, st2.name + ":m2"], writes=[t2.name])
    p.add("vector", lambda e: e.tensor_tensor(e3(t1), e3(ev), e3(t2), ALU.mult),
          reads=[ev.name, t2.name], writes=[t1.name])
    p.add("vector", lambda e: e.tensor_tensor(e3(comb), e3(t1), bc(st2, 64), ALU.mult),
          reads=[t1.name, st2.name + ":idn"], writes=[comb.name])
    p.add("vector", lambda e: e.tensor_copy(combb[:], comb[:]),
          reads=[comb.name], writes=[combb.name])

    p.add("sync", lambda e: e.dma_start(wstage[0][0:8, 0:1024], B2.ap()),
          reads=[B2.name], writes=[wstage[0].name], dma=wstage[0].name)
    p.add("vector", lambda e: e.tensor_copy(b2bf[:], wstage[0][0:8, 0:1024]),
          reads=[wstage[0].name], writes=[b2bf.name])
    p.region_switch(lg.name, obuf[0].name)
    p.region_switch(ev.name, obuf[0].name)
    p.region_switch(t1.name, obuf[0].name)
    p.region_switch(t2.name, obuf[0].name)
    for tt in range(NT):
        p.add("tensor", lambda e, tt=tt: e.transpose(
            pT[0][0:8, 0:128], combb[:, tt * 8:(tt + 1) * 8], ident[:]),
            reads=[combb.name, ident.name], writes=[pT[0].name])
        p.add("vector", lambda e: e.tensor_copy(combT[:], pT[0][0:8, 0:128]),
              reads=[pT[0].name], writes=[combT.name])
        p.add("sync", lambda e, tt=tt: e.dma_start(
            xin[tt % 2][:], Xi.ap()[tt * 128:(tt + 1) * 128, :]),
            reads=[Xi.name], writes=[xin[tt % 2].name], dma=xin[tt % 2].name)
        for hf in range(2):
            p.add("tensor", lambda e, hf=hf: e.matmul(
                pC[:, 0:512] if hf == 0 else pC[:, 0:512], combT[:],
                b2bf[:, hf * 512: hf * 512 + 512], start=True, stop=True),
                reads=[combT.name, b2bf.name], writes=[pC.name])
            p.add("vector", lambda e, tt=tt, hf=hf: e.tensor_add(
                obuf[tt % 2][:, hf * 512: hf * 512 + 512], pC[:, 0:512],
                xin[tt % 2][:, hf * 512: hf * 512 + 512]),
                reads=[pC.name, xin[tt % 2].name], writes=[obuf[tt % 2].name])
        p.add("gpsimd", lambda e, tt=tt: e.dma_start(
            Xo.ap()[tt * 128:(tt + 1) * 128, :], obuf[tt % 2][:]),
            reads=[obuf[tt % 2].name, f"d_Xo:{tt}"], writes=[f"d_Xo:{tt}"],
            dma=f"d_ac{tt % 8}")

    p.region_switch(sq.name, act.name)
    p.region_switch(xn_t2[0].name, act.name)
    p.region_switch(xn_t2[1].name, act.name)
    p.region_switch(xin[0].name, w1stage[2].name)
    p.region_switch(xin[1].name, w1stage[3].name)
    p.region_switch(b2bf.name, ssb[0].name)
    p.region_switch(b2bf.name, ssb[1].name)
    for ei in range(8):
        p.add("sync", lambda e, ei=ei: e.dma_start(b1c[:], B1c.ap()[ei]),
              reads=[B1c.name], writes=[b1c.name], dma=b1c.name)
        emit_load_cast(p, env, lambda i, ei=ei: W2.ap()[ei, i * 128:(i + 1) * 128, :],
                       W2.name, 16, 1024, w2_bf, 0, wstage)
        def load_w1(mc, ei=ei):
            sa, sg = w1stage[2 * (mc % 2)], w1stage[2 * (mc % 2) + 1]
            wa, wg1 = w1a[mc % 2], w1g[mc % 2]
            p.add("sync", lambda e, mc=mc, sa=sa, ei=ei: e.dma_start(
                sa[:].rearrange("p (c j) -> p c j", j=128),
                W1.ap()[ei, :, mc * 128:(mc + 1) * 128]
                .rearrange("(c p) j -> p c j", p=128)),
                reads=[W1.name], writes=[sa.name], dma=sa.name)
            p.add("vector", lambda e, sa=sa, wa=wa: e.tensor_copy(wa[:], sa[:]),
                  reads=[sa.name], writes=[wa.name])
            p.add("sync", lambda e, mc=mc, sg=sg, ei=ei: e.dma_start(
                sg[:].rearrange("p (c j) -> p c j", j=128),
                W1.ap()[ei, :, 2048 + mc * 128: 2048 + (mc + 1) * 128]
                .rearrange("(c p) j -> p c j", p=128)),
                reads=[W1.name], writes=[sg.name], dma=sg.name)
            p.add("vector", lambda e, sg=sg, wg1=wg1: e.tensor_copy(wg1[:], sg[:]),
                  reads=[sg.name], writes=[wg1.name])

        for half in range(2):
            for mc in range(16):
                if D_W1DB:
                    if half == 0 and mc == 0:
                        load_w1(0)
                    if mc < 15:
                        load_w1(mc + 1)
                    elif half == 0:
                        load_w1(0)
                else:
                    load_w1(mc)
                wa, wg1 = w1a[mc % 2], w1g[mc % 2]
                for tg in range(4):
                    tcol = half * 2048 + tg * 512
                    pAt = q4[D_PAR and tg % 2]
                    pGt = q4[2 + (D_PAR and tg % 2)]
                    sst = ssb[D_PAR and tg % 2]
                    gst = gsb[D_PAR and tg % 2]
                    for fc in range(FC):
                        p.add("tensor", lambda e, fc=fc, tcol=tcol, wa=wa, pAt=pAt: e.matmul(
                            pAt[:, 0:512], wa[:, fc * 128:(fc + 1) * 128],
                            xn_fm[:, fc * T + tcol: fc * T + tcol + 512],
                            start=(fc == 0), stop=(fc == 7)),
                            reads=[wa.name] + [f"{xn_fm.name}:{tcol // 128 + j}" for j in range(4)],
                            writes=[pAt.name])
                    for fc in range(FC):
                        p.add("tensor", lambda e, fc=fc, tcol=tcol, wg1=wg1, pGt=pGt: e.matmul(
                            pGt[:, 0:512], wg1[:, fc * 128:(fc + 1) * 128],
                            xn_fm[:, fc * T + tcol: fc * T + tcol + 512],
                            start=(fc == 0), stop=(fc == 7)),
                            reads=[wg1.name] + [f"{xn_fm.name}:{tcol // 128 + j}" for j in range(4)],
                            writes=[pGt.name])
                    p.add("scalar", lambda e, mc=mc, pAt=pAt, sst=sst: e.activation(
                        sst[:], pAt[:, 0:512], AF.Silu, bias=b1c[:, mc:mc + 1]),
                        reads=[pAt.name, b1c.name], writes=[sst.name])
                    p.add("vector", lambda e, mc=mc, pGt=pGt, gst=gst: e.tensor_scalar(
                        gst[:], pGt[:, 0:512], b1c[:, 16 + mc:17 + mc], None, ALU.add),
                        reads=[pGt.name, b1c.name], writes=[gst.name])
                    p.add("vector", lambda e, mc=mc, tg=tg, sst=sst, gst=gst: e.tensor_tensor(
                        act[:, mc * 2048 + tg * 512: mc * 2048 + (tg + 1) * 512],
                        sst[:], gst[:], ALU.mult),
                        reads=[sst.name, gst.name], writes=[act.name])
            if DEBUG_D and ei == 0 and half == 0:
                dbgd = env["dbgd"]
                p.add("gpsimd", lambda e: e.dma_start(dbgd.ap()[0:128, :], act[:]),
                      reads=[act.name], writes=[dbgd.name], dma="dbgd")
                dbgx = env["dbgx"]
                p.add("gpsimd", lambda e: e.dma_start(dbgx.ap()[0:128, :], xn_fm[:]),
                      reads=[xn_fm.name], writes=[dbgx.name], dma="dbgd")
            for tl in range(16):
                gt = half * 16 + tl
                pOt = pO if tl % 2 == 0 else [q4[0], q4[1]]
                for mc in range(16):
                    for hf in range(2):
                        p.add("tensor", lambda e, mc=mc, hf=hf, tl=tl, pOt=pOt: e.matmul(
                            pOt[hf][:, 0:512],
                            act[:, mc * 2048 + tl * 128: mc * 2048 + (tl + 1) * 128],
                            w2_bf[:, mc * 1024 + hf * 512: mc * 1024 + hf * 512 + 512],
                            start=(mc == 0), stop=(mc == 15)),
                            reads=[act.name, w2_bf.name], writes=[pOt[hf].name])
                ob = obuf[tl % 2]
                for hf in range(2):
                    p.add("vector", lambda e, gt=gt, ei=ei, ob=ob, hf=hf, pOt=pOt: e.tensor_scalar(
                        ob[:, hf * 512: hf * 512 + 512], pOt[hf][:, 0:512],
                        comb[:, gt * 8 + ei: gt * 8 + ei + 1],
                        None, ALU.mult),
                        reads=[pOt[hf].name, comb.name], writes=[ob.name])
                p.add("gpsimd", lambda e, gt=gt, ob=ob: e.dma_start(
                    Xo.ap()[gt * 128:(gt + 1) * 128, :], ob[:], accum_op=ALU.add),
                    reads=[ob.name, f"d_Xo:{gt}"], writes=[f"d_Xo:{gt}"],
                    dma=f"d_ac{gt % 8}")



def make_env(nc, need_mask=True):
    ident_f = nc.dram_tensor("c_ident", [128, 128], F32, kind="ExternalInput")
    ones_f = nc.dram_tensor("c_ones", [1, 128], F32, kind="ExternalInput")
    st_id = nc.alloc_sbuf_tensor("s_identf", [128, 128], F32)
    st_on = nc.alloc_sbuf_tensor("s_onesf", [1, 128], F32)
    s_ident = nc.alloc_sbuf_tensor("s_ident", [128, 128], BF16)
    s_ones = nc.alloc_sbuf_tensor("s_ones", [1, 128], BF16)
    eps_col = nc.alloc_sbuf_tensor("s_eps", [128, 1], F32)
    env = {"nc": nc, "ident_bf": s_ident, "ones_bf": s_ones, "eps_col": eps_col,
           "ident_f": st_id, "ones_f": st_on}
    if need_mask:
        env["mask4"] = nc.alloc_sbuf_tensor("s_mask4", [128, 2048], BF16)
        env["z512"] = nc.alloc_sbuf_tensor("s_z512", [128, 512], BF16)
    env["psum_q"] = [nc.alloc_psum_tensor(f"ps_q{i}", [128, 512], F32) for i in range(4)]
    env["psum_t"] = [nc.alloc_psum_tensor(f"ps_t{i}", [128, 1024], BF16) for i in range(2)]
    env["psum_o"] = [nc.alloc_psum_tensor(f"ps_o{i}", [128, 512], F32) for i in range(2)]
    return env, [(st_id, ident_f, s_ident), (st_on, ones_f, s_ones)]


def load_consts(p, env, pairs):
    for stg, dr, sb in pairs:
        p.add("sync", lambda e, stg=stg, dr=dr: e.dma_start(stg[:], dr.ap()),
              reads=[dr.name], writes=[stg.name], dma=stg.name)
        p.add("vector", lambda e, stg=stg, sb=sb: e.tensor_copy(sb[:], stg[:]),
              reads=[stg.name], writes=[sb.name])
    eps = env["eps_col"]
    p.add("vector", lambda e: e.memset(eps[:], 1e-5), writes=[eps.name])
    if "mask4" in env:
        mask4 = env["mask4"]
        z512 = env["z512"]
        p.add("vector", lambda e: e.memset(z512[:], 0.0), writes=[z512.name])
        for v in range(4):
            p.add("gpsimd", lambda e, v=v: e.affine_select(
                mask4[:, v * 512:(v + 1) * 512], z512[:], [[-1, 512]],
                mybir.AluOpType.is_ge, NEG_BIG, base=v * 128, channel_multiplier=1),
                reads=[z512.name], writes=[mask4.name])


def const_inputs():
    return {
        "c_ident": np.eye(128, dtype=np.float32),
        "c_ones": np.ones((1, 128), dtype=np.float32),
    }


def mk_arena(nc, reserve=1248):
    base = (SBUF_TOP - nc.sbuf_bytes_remaining + 31) & ~31
    size = SBUF_TOP - reserve - base
    nc.alloc_sbuf_tensor("arena", [128, SBUF_TOP - (SBUF_TOP - nc.sbuf_bytes_remaining) - reserve],
                         mybir.dt.uint8)
    return BAlloc(nc, base, size)


def build_prog_fused():
    nc = bass.Bass("TRN2", target_bir_lowering=False)
    Xi = nc.dram_tensor("Xi", [T, D], F32, kind="ExternalInput")
    Xo = nc.dram_tensor("Xo", [T, D], F32, kind="ExternalOutput")
    X1 = nc.dram_tensor("X1", [T, D], F32)
    X2 = nc.dram_tensor("X2", [T, D], F32)
    X3 = nc.dram_tensor("X3", [T, D], F32)
    Xmid = nc.dram_tensor("Xmid", [T, D], F32)
    XnH = nc.dram_tensor("XnH", [T, D], BF16)
    IlS = nc.dram_tensor("IlS", [32, 128], F32)
    aWqT = nc.dram_tensor("aWqT", [D, D], F32, kind="ExternalInput")
    aWkT = nc.dram_tensor("aWkT", [D, D], F32, kind="ExternalInput")
    aWv = nc.dram_tensor("aWv", [D, D], F32, kind="ExternalInput")
    bWr = nc.dram_tensor("bWr", [D, 16], F32, kind="ExternalInput")
    bWgs = nc.dram_tensor("bWgs", [2, D, 512], F32, kind="ExternalInput")
    bWus = nc.dram_tensor("bWus", [2, D, 512], F32, kind="ExternalInput")
    bWds = nc.dram_tensor("bWds", [2, 512, D], F32, kind="ExternalInput")
    bWgr = nc.dram_tensor("bWgr", [16, D, 256], F32, kind="ExternalInput")
    bWur = nc.dram_tensor("bWur", [16, D, 256], F32, kind="ExternalInput")
    bWdr = nc.dram_tensor("bWdr", [16, 256, D], F32, kind="ExternalInput")
    cWqT = nc.dram_tensor("cWqT", [D, D], F32, kind="ExternalInput")
    cWkT = nc.dram_tensor("cWkT", [D, D], F32, kind="ExternalInput")
    cWvT = nc.dram_tensor("cWvT", [D, D], F32, kind="ExternalInput")
    cWpT = nc.dram_tensor("cWpT", [D, D], F32, kind="ExternalInput")
    cBq = nc.dram_tensor("cBq", [1, D], F32, kind="ExternalInput")
    cBk = nc.dram_tensor("cBk", [1, D], F32, kind="ExternalInput")
    cBv = nc.dram_tensor("cBv", [1, D], F32, kind="ExternalInput")
    cBp = nc.dram_tensor("cBp", [1, D], F32, kind="ExternalInput")
    CosT = nc.dram_tensor("CosT", [T, 32], F32, kind="ExternalInput")
    SinT = nc.dram_tensor("SinT", [T, 32], F32, kind="ExternalInput")
    dW1 = nc.dram_tensor("dW1", [8, D, 4096], F32, kind="ExternalInput")
    dB1c = nc.dram_tensor("dB1c", [8, 128, 32], F32, kind="ExternalInput")
    dW2 = nc.dram_tensor("dW2", [8, 2048, D], F32, kind="ExternalInput")
    dB2 = nc.dram_tensor("dB2", [8, D], F32, kind="ExternalInput")
    dWrT = nc.dram_tensor("dWrT", [D, 8], F32, kind="ExternalInput")
    dBr = nc.dram_tensor("dBr", [1, 8], F32, kind="ExternalInput")

    env, pairs = make_env(nc)
    al = mk_arena(nc)
    p = Prog(nc)
    load_consts(p, env, pairs)
    build_block_a(nc, p, al, Xi, X1, aWqT, aWkT, aWv, env)
    p.barrier(); al.reset()
    build_block_b(nc, p, al, X1, X2, bWr, bWgs, bWus, bWds, bWgr, bWur, bWdr, env)
    p.barrier(); al.reset()
    build_block_c(nc, p, al, X2, Xmid, X3, cWqT, cWkT, cWvT, cWpT, cBq, cBk, cBv,
                  cBp, CosT, SinT, XnH, IlS, env)
    p.barrier(); al.reset()
    build_block_d(nc, p, al, X3, Xo, dW1, dB1c, dW2, dB2, dWrT, dBr, env)
    p.barrier()
    p.finalize_and_emit()
    return nc


def build_prog_a():
    nc = bass.Bass("TRN2", target_bir_lowering=False)
    Xi = nc.dram_tensor("Xi", [T, D], F32, kind="ExternalInput")
    Xo = nc.dram_tensor("Xo", [T, D], F32, kind="ExternalOutput")
    WqT = nc.dram_tensor("WqT", [D, D], F32, kind="ExternalInput")
    WkT = nc.dram_tensor("WkT", [D, D], F32, kind="ExternalInput")
    Wv = nc.dram_tensor("Wv", [D, D], F32, kind="ExternalInput")
    env, pairs = make_env(nc)
    al = mk_arena(nc)
    p = Prog(nc)
    load_consts(p, env, pairs)
    build_block_a(nc, p, al, Xi, Xo, WqT, WkT, Wv, env)
    p.barrier()
    p.finalize_and_emit()
    return nc


def build_prog_b():
    nc = bass.Bass("TRN2", target_bir_lowering=False)
    Xi = nc.dram_tensor("Xi", [T, D], F32, kind="ExternalInput")
    Xo = nc.dram_tensor("Xo", [T, D], F32, kind="ExternalOutput")
    Wr = nc.dram_tensor("Wr", [D, 16], F32, kind="ExternalInput")
    Wgs = nc.dram_tensor("Wgs", [2, D, 512], F32, kind="ExternalInput")
    Wus = nc.dram_tensor("Wus", [2, D, 512], F32, kind="ExternalInput")
    Wds = nc.dram_tensor("Wds", [2, 512, D], F32, kind="ExternalInput")
    Wgr = nc.dram_tensor("Wgr", [16, D, 256], F32, kind="ExternalInput")
    Wur = nc.dram_tensor("Wur", [16, D, 256], F32, kind="ExternalInput")
    Wdr = nc.dram_tensor("Wdr", [16, 256, D], F32, kind="ExternalInput")
    env, pairs = make_env(nc, need_mask=False)
    al = mk_arena(nc)
    p = Prog(nc)
    load_consts(p, env, pairs)
    build_block_b(nc, p, al, Xi, Xo, Wr, Wgs, Wus, Wds, Wgr, Wur, Wdr, env)
    p.barrier()
    p.finalize_and_emit()
    return nc


def build_prog_c():
    nc = bass.Bass("TRN2", target_bir_lowering=False)
    Xi = nc.dram_tensor("Xi", [T, D], F32, kind="ExternalInput")
    Xo = nc.dram_tensor("Xo", [T, D], F32, kind="ExternalOutput")
    Xmid = nc.dram_tensor("Xmid", [T, D], F32,
                          kind=("ExternalOutput" if DEBUG_C else "Internal"))
    XnH = nc.dram_tensor("XnH", [T, D], BF16)
    IlS = nc.dram_tensor("IlS", [32, 128], F32)
    WqT = nc.dram_tensor("WqT", [D, D], F32, kind="ExternalInput")
    WkT = nc.dram_tensor("WkT", [D, D], F32, kind="ExternalInput")
    WvT = nc.dram_tensor("WvT", [D, D], F32, kind="ExternalInput")
    WpT = nc.dram_tensor("WpT", [D, D], F32, kind="ExternalInput")
    Bq = nc.dram_tensor("Bq", [1, D], F32, kind="ExternalInput")
    Bk = nc.dram_tensor("Bk", [1, D], F32, kind="ExternalInput")
    Bv = nc.dram_tensor("Bv", [1, D], F32, kind="ExternalInput")
    Bp = nc.dram_tensor("Bp", [1, D], F32, kind="ExternalInput")
    CosT = nc.dram_tensor("CosT", [T, 32], F32, kind="ExternalInput")
    SinT = nc.dram_tensor("SinT", [T, 32], F32, kind="ExternalInput")
    env, pairs = make_env(nc)
    if DEBUG_C:
        env["dbg"] = nc.dram_tensor("Dbg", [1024, 2048], F32, kind="ExternalOutput")
    al = mk_arena(nc)
    p = Prog(nc)
    load_consts(p, env, pairs)
    build_block_c(nc, p, al, Xi, Xmid, Xo, WqT, WkT, WvT, WpT, Bq, Bk, Bv, Bp,
                  CosT, SinT, XnH, IlS, env)
    p.barrier()
    p.finalize_and_emit()
    return nc


def build_prog_d():
    nc = bass.Bass("TRN2", target_bir_lowering=False)
    Xi = nc.dram_tensor("Xi", [T, D], F32, kind="ExternalInput")
    Xo = nc.dram_tensor("Xo", [T, D], F32, kind="ExternalOutput")
    W1 = nc.dram_tensor("W1", [8, D, 4096], F32, kind="ExternalInput")
    B1c = nc.dram_tensor("B1c", [8, 128, 32], F32, kind="ExternalInput")
    W2 = nc.dram_tensor("W2", [8, 2048, D], F32, kind="ExternalInput")
    B2 = nc.dram_tensor("B2", [8, D], F32, kind="ExternalInput")
    WrT = nc.dram_tensor("WrT", [D, 8], F32, kind="ExternalInput")
    Br = nc.dram_tensor("Br", [1, 8], F32, kind="ExternalInput")
    env, pairs = make_env(nc, need_mask=False)
    if DEBUG_D:
        env["dbgd"] = nc.dram_tensor("DbgD", [128, 32768], mybir.dt.bfloat16,
                                     kind="ExternalOutput")
        env["dbgx"] = nc.dram_tensor("DbgX", [128, 32768], mybir.dt.bfloat16,
                                     kind="ExternalOutput")
    al = mk_arena(nc)
    p = Prog(nc)
    load_consts(p, env, pairs)
    build_block_d(nc, p, al, Xi, Xo, W1, B1c, W2, B2, WrT, Br, env)
    p.barrier()
    p.finalize_and_emit()
    return nc


def rope_tables():
    half = 32
    i = np.arange(half, dtype=np.float64)
    alpha = 1.0 / 10000.0 ** (2.0 * i / 64.0)
    ang = np.arange(T, dtype=np.float64)[:, None] * alpha[None, :]
    return (np.cos(ang).astype(np.float32), np.sin(ang).astype(np.float32))


_PROG_CACHE = {}


def _prog(name, builder):
    if name not in _PROG_CACHE:
        _PROG_CACHE[name] = builder()
    return _PROG_CACHE[name]


def _run(nc, per_core, shared):
    in_maps = []
    for b in range(8):
        m = dict(shared)
        m.update({k: v[b] for k, v in per_core.items()})
        in_maps.append(m)
    res = run_bass_kernel_spmd(nc, in_maps, core_ids=list(range(8)))
    return np.stack([np.asarray(r["Xo"], dtype=np.float32) for r in res.results])


def kernel(**inputs):
    f32c = lambda a: np.ascontiguousarray(np.asarray(a), dtype=np.float32)
    x = f32c(inputs["L0_attn_x"])
    cos_t, sin_t = rope_tables()
    b1 = f32c(inputs["L1_ffn_b1"])
    b1c = np.concatenate(
        [b1[:, :2048].reshape(8, 16, 128).transpose(0, 2, 1),
         b1[:, 2048:].reshape(8, 16, 128).transpose(0, 2, 1)], axis=2)
    sh = dict(const_inputs())
    sh["aWqT"] = f32c(inputs["L0_attn_w_q"].T)
    sh["aWkT"] = f32c(inputs["L0_attn_w_k"].T)
    sh["aWv"] = f32c(inputs["L0_attn_w_v"])
    sh["bWr"] = f32c(inputs["L0_ffn_router_w"])
    sh["bWgs"] = f32c(inputs["L0_ffn_wg_s"])
    sh["bWus"] = f32c(inputs["L0_ffn_wu_s"])
    sh["bWds"] = f32c(inputs["L0_ffn_wd_s"])
    sh["bWgr"] = f32c(inputs["L0_ffn_wg_r"])
    sh["bWur"] = f32c(inputs["L0_ffn_wu_r"])
    sh["bWdr"] = f32c(inputs["L0_ffn_wd_r"])
    sh["cWqT"] = f32c(inputs["L1_attn_wq"].T)
    sh["cWkT"] = f32c(inputs["L1_attn_wk"].T)
    sh["cWvT"] = f32c(inputs["L1_attn_wv"].T)
    sh["cWpT"] = f32c(inputs["L1_attn_wp"].T)
    sh["cBq"] = f32c(inputs["L1_attn_bq"]).reshape(1, D)
    sh["cBk"] = f32c(inputs["L1_attn_bk"]).reshape(1, D)
    sh["cBv"] = f32c(inputs["L1_attn_bv"]).reshape(1, D)
    sh["cBp"] = f32c(inputs["L1_attn_bp"]).reshape(1, D)
    sh["CosT"] = cos_t
    sh["SinT"] = sin_t
    sh["dW1"] = f32c(inputs["L1_ffn_W1"])
    sh["dB1c"] = np.ascontiguousarray(b1c)
    sh["dW2"] = f32c(inputs["L1_ffn_W2"])
    sh["dB2"] = f32c(inputs["L1_ffn_b2"])
    sh["dWrT"] = f32c(inputs["L1_ffn_Wr"])
    sh["dBr"] = f32c(inputs["L1_ffn_br"]).reshape(1, 8)
    return _run(_prog("fused", build_prog_fused), {"Xi": x}, sh)


# version 22
# speedup vs baseline: 1.1345x; 1.0171x over previous
import numpy as np
import ml_dtypes
import concourse.bass as bass
import concourse.mybir as mybir
from concourse.bass_utils import run_bass_kernel_spmd

F32 = mybir.dt.float32
BF16 = mybir.dt.bfloat16
AF = mybir.ActivationFunctionType
ALU = mybir.AluOpType
AX = mybir.AxisListType

T = 4096
D = 1024
NT = T // 128
FC = D // 128
GAMMA0 = 1.0 / 32.0
DEBUG_C = False
DEBUG_G = 0
D_PAR = True
NDIR = 5
D_W1DB = True
DEBUG_D = False
NEG_BIG = -1.0e30

ENGINES = ("sync", "tensor", "scalar", "vector", "gpsimd")


class Prog:

    def __init__(self, nc):
        self.nc = nc
        self.ops = []
        self.lw = {}
        self.rd = {}
        self.gcount = {}
        self.gsem = {}
        self.esem = {}
        self.epoch = 0
        self._new_epoch()

    def _new_epoch(self):
        self.epoch += 1
        for e in ENGINES:
            self.esem[e] = self.nc.alloc_semaphore(f"E{e[:2]}{self.epoch}")

    def _gsem(self, g):
        if g not in self.gsem:
            self.gsem[g] = self.nc.alloc_semaphore(f"G{len(self.gsem)}")
        return self.gsem[g]

    def add(self, engine, emit, reads=(), writes=(), dma=None):
        deps = set()
        gwait = {}

        def dep_on(j):
            dj = self.ops[j]
            if dj["dma"] is not None:
                g = dj["dma"]
                gwait[g] = max(gwait.get(g, 0), self.gcount[g])
            else:
                deps.add(j)

        for k in reads:
            for j in self.lw.get(k, {}).values():
                dep_on(j)
        for k in writes:
            for j in self.lw.get(k, {}).values():
                dj = self.ops[j]
                if dma is not None and dj["dma"] == dma:
                    continue
                dep_on(j)
            for j in self.rd.get(k, ()):
                dep_on(j)
        i = len(self.ops)
        if dma is not None:
            self.gcount[dma] = self.gcount.get(dma, 0) + 1
            self._gsem(dma)
        op = {"engine": engine, "emit": emit, "deps": deps, "dma": dma,
              "esem": self.esem[engine], "inc": False, "msv": None,
              "gwait": gwait}
        self.ops.append(op)
        for k in reads:
            self.rd.setdefault(k, []).append(i)
        for k in writes:
            tag = "dma" if dma is not None else engine
            self.lw.setdefault(k, {})[tag] = i
            self.rd[k] = []
        return i

    def region_switch(self, old_key, new_key):
        lw = self.lw.setdefault(new_key, {})
        for tag, j in self.lw.get(old_key, {}).items():
            if tag not in lw or lw[tag] < j:
                lw[tag] = j
        self.rd.setdefault(new_key, []).extend(self.rd.get(old_key, ()))

    def barrier(self):
        last = {}
        for i, op in enumerate(self.ops):
            if op["dma"] is None and op["emit"] is not None:
                last[op["engine"]] = i
        groups = dict(self.gcount)
        for e in ENGINES:
            deps = {i for en, i in last.items() if en != e}
            self.ops.append({"engine": e, "emit": None, "deps": deps,
                             "dma": None, "esem": self.esem[e], "inc": False,
                             "msv": None, "gwait": dict(groups)})
        self.lw = {}
        self.rd = {}
        self._new_epoch()

    def finalize_and_emit(self):
        for op in self.ops:
            for j in op["deps"]:
                dj = self.ops[j]
                if dj["engine"] != op["engine"]:
                    dj["inc"] = True
        counters = {}
        for op in self.ops:
            if op["inc"]:
                s = op["esem"]
                counters[s] = counters.get(s, 0) + 1
                op["msv"] = counters[s]
        for g, c in self.gcount.items():
            assert 16 * c < 60000, f"dma group {g} count {c} too high"
        for s, c in counters.items():
            assert c < 60000, "milestone sem overflow"
        ops = self.ops
        gsem = self.gsem

        eseq = {}
        ecount = {e: 0 for e in ENGINES}
        for i, op in enumerate(ops):
            if op["emit"] is not None:
                ecount[op["engine"]] += 1
                eseq[i] = ecount[op["engine"]]

        with self.nc.Block() as blk:
            def mk_body(ename):
                def body(eng):
                    water = {}
                    drained_upto = 0
                    emitted = 0
                    for op in ops:
                        if op["engine"] != ename:
                            continue
                        need_drain = any(
                            ops[j]["engine"] == ename and ops[j]["emit"] is not None
                            and eseq[j] > drained_upto
                            for j in op["deps"])
                        if need_drain and ename != "tensor":
                            eng.drain()
                            drained_upto = emitted
                        for j in sorted(op["deps"]):
                            dj = ops[j]
                            if dj["engine"] == ename:
                                continue
                            if not dj["inc"]:
                                continue
                            sem, val = dj["esem"], dj["msv"]
                            if water.get(sem.num, 0) < val:
                                eng.wait_ge(sem, val)
                                water[sem.num] = val
                        for g, cnt in sorted(op["gwait"].items()):
                            sem, val = gsem[g], 16 * cnt
                            if water.get(sem.num, 0) < val:
                                eng.wait_ge(sem, val)
                                water[sem.num] = val
                        if op["emit"] is None:
                            continue
                        inst = op["emit"](eng)
                        emitted += 1
                        if op["dma"] is not None:
                            inst.then_inc(gsem[op["dma"]], 16)
                        elif op["inc"]:
                            inst.then_inc(op["esem"], 1)
                return body
            for ename in ENGINES:
                getattr(blk, ename)(mk_body(ename))



def emit_ln_tile(p, env, x_hbm, tt, xin, xn_t, dst_writes, emit_to):
    nc = env["nc"]
    kx = xin.name
    p.add("sync", lambda e, a=x_hbm, b=xin: e.dma_start(
        b[:], a.ap()[tt * 128:(tt + 1) * 128, :]), reads=[x_hbm.name], writes=[kx], dma=kx)
    sq, st = env["sq"], env["stats"]
    ks = st.name
    p.add("vector", lambda e: e.tensor_reduce(st[:, 0:1], xin[:], AX.X, ALU.add),
          reads=[kx], writes=[ks + ":sum"])
    p.add("scalar", lambda e: e.activation(sq[:], xin[:], AF.Square),
          reads=[kx], writes=[sq.name])
    p.add("vector", lambda e: e.tensor_reduce(st[:, 1:2], sq[:], AX.X, ALU.add),
          reads=[sq.name], writes=[ks + ":sq"])
    p.add("vector", lambda e: e.tensor_scalar(st[:, 2:3], st[:, 0:1], 1.0 / D, None, ALU.mult),
          reads=[ks + ":sum"], writes=[ks + ":m"])
    p.add("vector", lambda e: e.tensor_scalar(st[:, 3:4], st[:, 1:2], 1.0 / D, None, ALU.mult),
          reads=[ks + ":sq"], writes=[ks + ":e"])
    p.add("vector", lambda e: e.tensor_tensor(st[:, 4:5], st[:, 2:3], st[:, 2:3], ALU.mult),
          reads=[ks + ":m"], writes=[ks + ":m2"])
    p.add("vector", lambda e: e.tensor_sub(st[:, 5:6], st[:, 3:4], st[:, 4:5]),
          reads=[ks + ":e", ks + ":m2"], writes=[ks + ":v"])
    eps = env["eps_col"]
    p.add("scalar", lambda e: e.activation(st[:, 6:7], st[:, 5:6], AF.Sqrt, bias=eps[:, 0:1]),
          reads=[ks + ":v", eps.name], writes=[ks + ":sd"])
    p.add("vector", lambda e: e.reciprocal(st[:, 7:8], st[:, 6:7]),
          reads=[ks + ":sd"], writes=[ks + ":r"])
    p.add("vector", lambda e: e.tensor_scalar(xn_t[:], xin[:], st[:, 2:3], st[:, 7:8],
                                              ALU.subtract, ALU.mult),
          reads=[kx, ks + ":m", ks + ":r"], writes=[xn_t.name])
    emit_to(p)


def emit_transpose_to(p, env, src_ap, src_key, dst_ap, dst_key, pt, evac_engine="vector"):
    ident = env["ident_bf"]
    p.add("tensor", lambda e: e.transpose(pt[:, 0:128], src_ap, ident[:]),
          reads=[src_key, ident.name], writes=[pt.name])
    p.add(evac_engine, lambda e: (e.tensor_copy(dst_ap, pt[:, 0:128])
                                  if evac_engine != "scalar" else e.copy(dst_ap, pt[:, 0:128])),
          reads=[pt.name], writes=[dst_key])


def emit_load_cast_w(p, env, w_hbm, rows0, rows1, cols0, cols1, dst, dst_off,
                     stage_pair, tag):
    W = cols1 - cols0
    n = (rows1 - rows0) // 128
    for i in range(n):
        stg = stage_pair[i % 2]
        p.add("sync", lambda e, i=i, stg=stg: e.dma_start(
            stg[:, 0:W], w_hbm.ap()[rows0 + i * 128: rows0 + (i + 1) * 128, cols0:cols1]),
            reads=[w_hbm.name], writes=[stg.name], dma=stg.name)
        p.add("vector", lambda e, i=i, stg=stg: e.tensor_copy(
            dst[:, dst_off + i * W: dst_off + (i + 1) * W], stg[:, 0:W]),
            reads=[stg.name], writes=[dst.name + tag])


def emit_hilo_row(p, env, src_hbm, n, row_f, row_hi, row_lo, row_tmp):
    p.add("sync", lambda e: e.dma_start(row_f[:, 0:n], src_hbm.ap()[0:1, 0:n]),
          reads=[src_hbm.name], writes=[row_f.name], dma=row_f.name)
    p.add("vector", lambda e: e.tensor_copy(row_hi[:, 0:n], row_f[:, 0:n]),
          reads=[row_f.name], writes=[row_hi.name])
    p.add("vector", lambda e: e.tensor_copy(row_tmp[:, 0:n], row_hi[:, 0:n]),
          reads=[row_hi.name], writes=[row_tmp.name])
    p.add("vector", lambda e: e.tensor_sub(row_tmp[:, 0:n], row_f[:, 0:n], row_tmp[:, 0:n]),
          reads=[row_f.name, row_tmp.name], writes=[row_tmp.name])
    p.add("vector", lambda e: e.tensor_copy(row_lo[:, 0:n], row_tmp[:, 0:n]),
          reads=[row_tmp.name], writes=[row_lo.name])



SBUF_TOP = 229344

_ESIZE = {F32: 4, BF16: 2}


class BAlloc:

    def __init__(self, nc, base, size):
        self.nc, self.base, self.size = nc, base, size
        self.off = 0

    def alloc_sbuf_tensor(self, name, shape, dtype):
        nb = (shape[-1] * _ESIZE[dtype] + 31) & ~31
        t = self.nc.alloc_sbuf_tensor_at(name, shape, dtype, offset=self.base + self.off)
        self.off += nb
        assert self.off <= self.size, f"arena overflow at {name}: {self.off}/{self.size}"
        return t

    def alloc_sbuf_tensor_at(self, name, shape, dtype, offset):
        return self.nc.alloc_sbuf_tensor_at(name, shape, dtype, offset=offset)

    def suballoc(self, nbytes):
        off = self.base + self.off
        self.off += (nbytes + 31) & ~31
        assert self.off <= self.size
        return off

    def reset(self):
        self.off = 0


def sbuf_arena(al, name, nbytes):
    return al.suballoc(nbytes)


def build_block_a(nc, p, al, Xi, Xo, WqT, WkT, Wv, env):
    sq = al.alloc_sbuf_tensor("a_sq", [128, 1024], F32)
    stats = al.alloc_sbuf_tensor("a_st", [128, 8], F32)
    xn_t2 = [al.alloc_sbuf_tensor(f"a_xnt{i}", [128, 1024], BF16) for i in range(2)]
    xn_fm = al.alloc_sbuf_tensor("a_xnfm", [128, FC * T], BF16)
    P_bf = al.alloc_sbuf_tensor("a_P", [128, 8192], BF16)
    V_tm = al.alloc_sbuf_tensor("a_V", [128, NT * 1024], BF16)
    hh_bf2 = [al.alloc_sbuf_tensor(f"a_hh{i}", [128, 1024], BF16) for i in range(2)]
    at_sb = [al.alloc_sbuf_tensor(f"a_at{i}", [128, 128], BF16) for i in range(2)]
    sm = al.alloc_sbuf_tensor("a_sm", [128, 8], F32)
    r1 = sbuf_arena(al, "a_reg1", 16384)
    r2 = sbuf_arena(al, "a_reg2", 16384)
    r3 = sbuf_arena(al, "a_reg3", 8192)
    wq_bf = al.alloc_sbuf_tensor_at("a_wq", [128, 8192], BF16, offset=r1)
    wv_bf = al.alloc_sbuf_tensor_at("a_wv", [128, 8192], BF16, offset=r1)
    s_row = al.alloc_sbuf_tensor_at("a_srow", [128, T], F32, offset=r1)
    wk_bf = al.alloc_sbuf_tensor_at("a_wk", [128, 8192], BF16, offset=r2)
    a_bf = al.alloc_sbuf_tensor_at("a_abf", [128, T], BF16, offset=r2)
    xnew = [al.alloc_sbuf_tensor_at(f"a_xn{i}", [128, 1024], F32, offset=r2 + 8192 + 4096 * i)
            for i in range(2)]
    wstage = [al.alloc_sbuf_tensor_at(f"a_wst{i}", [128, 1024], F32, offset=r3 + 4096 * i)
              for i in range(2)]
    xold = [al.alloc_sbuf_tensor_at(f"a_xo{i}", [128, 1024], F32, offset=r3 + 4096 * i)
            for i in range(2)]
    xin = [al.alloc_sbuf_tensor_at(f"a_xin{i}", [128, 1024], F32, offset=r3 + 4096 * i)
           for i in range(2)]

    q4 = env["psum_q"]
    pA, pB, pS = q4[0], q4[1], [q4[2], q4[3]]
    pT = env["psum_t"]
    pY = env["psum_o"]

    env = dict(env)
    env["sq"] = sq
    env["stats"] = stats

    for tt in range(NT):
        xn_t = xn_t2[0]
        def consume(p, tt=tt, xn_t=xn_t):
            for c in range(FC):
                emit_transpose_to(
                    p, env, xn_t[:, c * 128:(c + 1) * 128], xn_t.name,
                    xn_fm[:, c * T + tt * 128: c * T + (tt + 1) * 128],
                    f"{xn_fm.name}:{tt}",
                    pT[c % 2], evac_engine=("vector" if c % 2 == 0 else "scalar"))
        emit_ln_tile(p, env, Xi, tt, xin[tt % 2], xn_t, None, consume)

    p.region_switch(xin[0].name, wstage[0].name)
    p.region_switch(xin[1].name, wstage[1].name)
    emit_load_cast_w(p, env, WqT, 0, 1024, 0, 1024, wq_bf, 0, wstage, "")
    emit_load_cast_w(p, env, WkT, 0, 1024, 0, 1024, wk_bf, 0, wstage, "")
    for mq in range(8):
        for half in range(2):
            ps = pA if half == 0 else pB
            for dc in range(8):
                p.add("tensor", lambda e, mq=mq, half=half, dc=dc, ps=ps: e.matmul(
                    ps[:, 0:512],
                    wq_bf[:, dc * 1024 + mq * 128: dc * 1024 + (mq + 1) * 128],
                    wk_bf[:, dc * 1024 + half * 512: dc * 1024 + half * 512 + 512],
                    start=(dc == 0), stop=(dc == 7)),
                    reads=[wq_bf.name, wk_bf.name], writes=[ps.name])
            p.add("scalar", lambda e, mq=mq, half=half, ps=ps: e.copy(
                P_bf[:, mq * 1024 + half * 512: mq * 1024 + half * 512 + 512], ps[:, 0:512]),
                reads=[ps.name], writes=[P_bf.name])

    p.region_switch(wq_bf.name, wv_bf.name)
    emit_load_cast_w(p, env, Wv, 0, 1024, 0, 1024, wv_bf, 0, wstage, "")
    for tt in range(NT):
        for half in range(2):
            ps = pA if half == 0 else pB
            for fc in range(FC):
                p.add("tensor", lambda e, tt=tt, half=half, fc=fc, ps=ps: e.matmul(
                    ps[:, 0:512],
                    xn_fm[:, fc * T + tt * 128: fc * T + (tt + 1) * 128],
                    wv_bf[:, fc * 1024 + half * 512: fc * 1024 + half * 512 + 512],
                    start=(fc == 0), stop=(fc == 7)),
                    reads=[f"{xn_fm.name}:{tt}", wv_bf.name], writes=[ps.name])
            p.add("scalar", lambda e, tt=tt, half=half, ps=ps: e.copy(
                V_tm[:, tt * 1024 + half * 512: tt * 1024 + half * 512 + 512], ps[:, 0:512]),
                reads=[ps.name], writes=[f"{V_tm.name}:{tt}"])

    p.region_switch(wv_bf.name, s_row.name)
    p.region_switch(wk_bf.name, a_bf.name)
    p.region_switch(wk_bf.name, xnew[0].name)
    p.region_switch(wk_bf.name, xnew[1].name)
    p.region_switch(wstage[0].name, xold[0].name)
    p.region_switch(wstage[1].name, xold[1].name)
    def emit_hh(qt):
        hb = hh_bf2[qt % 2]
        for m in range(8):
            ps = pA if m < 4 else pB
            col = (m % 4) * 128
            for c in range(8):
                p.add("tensor", lambda e, m=m, c=c, ps=ps, col=col, qt=qt: e.matmul(
                    ps[:, col:col + 128],
                    P_bf[:, c * 1024 + m * 128: c * 1024 + (m + 1) * 128],
                    xn_fm[:, c * T + qt * 128: c * T + (qt + 1) * 128],
                    start=(c == 0), stop=(c == 7)),
                    reads=[P_bf.name, f"{xn_fm.name}:{qt}"], writes=[ps.name])
        p.add("vector", lambda e, hb=hb: e.tensor_copy(hb[:, 0:512], pA[:, 0:512]),
              reads=[pA.name], writes=[hb.name])
        p.add("vector", lambda e, hb=hb: e.tensor_copy(hb[:, 512:1024], pB[:, 0:512]),
              reads=[pB.name], writes=[hb.name])

    for qt in range(NT):
        kxo = xold[qt % 2].name
        p.add("gpsimd", lambda e, qt=qt, b=xold[qt % 2]: e.dma_start(
            b[:], Xi.ap()[qt * 128:(qt + 1) * 128, :]), reads=[Xi.name], writes=[kxo], dma=kxo)
        if qt == 0:
            emit_hh(0)
        hh_bf = hh_bf2[qt % 2]
        ng = (qt + 4) // 4
        for g in range(ng):
            ps = pS[g % 2]
            gk = [f"{xn_fm.name}:{g * 4 + j}" for j in range(4)]
            for m in range(8):
                p.add("tensor", lambda e, g=g, m=m, ps=ps, hh_bf=hh_bf: e.matmul(
                    ps[:, 0:512],
                    hh_bf[:, m * 128:(m + 1) * 128],
                    xn_fm[:, m * T + g * 512: m * T + g * 512 + 512],
                    start=(m == 0), stop=(m == 7)),
                    reads=[hh_bf.name] + gk, writes=[ps.name])
            if g == ng - 1:
                mask4 = env["mask4"]
                v = qt % 4
                p.add("vector", lambda e, ps=ps, ng=ng, v=v: e.tensor_tensor(
                    s_row[:, (ng - 1) * 512: ng * 512], ps[:, 0:512],
                    mask4[:, v * 512:(v + 1) * 512], ALU.add),
                    reads=[ps.name, mask4.name], writes=[s_row.name])
            else:
                eng = "vector" if g % 2 == 0 else "scalar"
                p.add(eng, lambda e, g=g, ps=ps, eng=eng: (
                    e.tensor_copy(s_row[:, g * 512:(g + 1) * 512], ps[:, 0:512])
                    if eng == "vector" else e.copy(s_row[:, g * 512:(g + 1) * 512], ps[:, 0:512])),
                    reads=[ps.name], writes=[s_row.name])
        L = ng * 512
        p.add("vector", lambda e, L=L: e.tensor_reduce(
            sm[:, 0:1], s_row[:, 0:L], AX.X, ALU.max, negate=True),
            reads=[s_row.name], writes=[sm.name + ":nm"])
        p.add("vector", lambda e: e.tensor_scalar(sm[:, 1:2], sm[:, 0:1], GAMMA0, None, ALU.mult),
              reads=[sm.name + ":nm"], writes=[sm.name + ":bm"])
        if qt + 1 < NT:
            emit_hh(qt + 1)
        p.add("scalar", lambda e, L=L: e.activation(
            a_bf[:, 0:L], s_row[:, 0:L], AF.Exp, bias=sm[:, 1:2], scale=GAMMA0,
            accum_out=sm[:, 2:3]),
            reads=[s_row.name, sm.name + ":bm"], writes=[a_bf.name, sm.name + ":l"])
        p.add("vector", lambda e: e.reciprocal(sm[:, 3:4], sm[:, 2:3]),
              reads=[sm.name + ":l"], writes=[sm.name + ":il"])
        def emit_at(kb):
            pt = pT[kb % 2]
            ab = at_sb[kb % 2]
            p.add("tensor", lambda e, kb=kb, pt=pt: e.transpose(
                pt[:, 0:128], a_bf[:, kb * 128:(kb + 1) * 128], env["ident_bf"][:]),
                reads=[a_bf.name], writes=[pt.name])
            eng = "vector" if kb % 2 == 0 else "scalar"
            p.add(eng, lambda e, pt=pt, ab=ab, eng=eng: (
                e.tensor_copy(ab[:], pt[:, 0:128]) if eng == "vector"
                else e.copy(ab[:], pt[:, 0:128])),
                reads=[pt.name], writes=[ab.name])
        PIPELINE_Y_A = True
        if PIPELINE_Y_A:
            emit_at(0)
        for kb in range(qt + 1):
            if PIPELINE_Y_A:
                if kb + 1 <= qt:
                    emit_at(kb + 1)
            else:
                emit_at(kb)
            ab = at_sb[kb % 2]
            for half in range(2):
                p.add("tensor", lambda e, kb=kb, half=half, ab=ab, qt=qt: e.matmul(
                    pY[half][:, 0:512], ab[:],
                    V_tm[:, kb * 1024 + half * 512: kb * 1024 + half * 512 + 512],
                    start=(kb == 0), stop=(kb == qt)),
                    reads=[ab.name, f"{V_tm.name}:{kb}"], writes=[pY[half].name])
        kxn = xnew[qt % 2].name
        for half in range(2):
            p.add("vector", lambda e, qt=qt, half=half: e.scalar_tensor_tensor(
                xnew[qt % 2][:, half * 512:half * 512 + 512], pY[half][:, 0:512],
                sm[:, 3:4], xold[qt % 2][:, half * 512:half * 512 + 512],
                ALU.mult, ALU.add),
                reads=[pY[half].name, sm.name + ":il", kxo], writes=[kxn])
        p.add("gpsimd", lambda e, qt=qt: e.dma_start(
            Xo.ap()[qt * 128:(qt + 1) * 128, :], xnew[qt % 2][:]),
            reads=[kxn], writes=[Xo.name], dma="store_" + Xo.name)



def emit_load_cast(p, env, mk_ap, hbm_key, n, W, dst, dst_off, stage_pair, tag=""):
    for i in range(n):
        stg = stage_pair[i % len(stage_pair)]
        ap_i = mk_ap(i)
        p.add("sync", lambda e, ap_i=ap_i, stg=stg: e.dma_start(stg[:, 0:W], ap_i),
              reads=[hbm_key], writes=[stg.name], dma=stg.name)
        p.add("vector", lambda e, i=i, stg=stg: e.tensor_copy(
            dst[:, dst_off + i * W: dst_off + (i + 1) * W], stg[:, 0:W]),
            reads=[stg.name], writes=[dst.name + tag])


def build_block_b(nc, p, al, Xi, Xo, Wr, Wgs, Wus, Wds, Wgr, Wur, Wdr, env):
    xin = [al.alloc_sbuf_tensor(f"b_xin{i}", [128, 1024], F32) for i in range(2)]
    sq = al.alloc_sbuf_tensor("b_sq", [128, 1024], F32)
    stats = al.alloc_sbuf_tensor("b_st", [128, 8], F32)
    xn_t2 = [al.alloc_sbuf_tensor(f"b_xnt{i}", [128, 1024], BF16) for i in range(2)]
    xn_fm = al.alloc_sbuf_tensor("b_xnfm", [128, FC * 2048], BF16)
    acc = al.alloc_sbuf_tensor("b_acc", [128, 16 * 1024], F32)
    wstage = [al.alloc_sbuf_tensor(f"b_wst{i}", [128, 1024], F32) for i in range(2)]
    wg2 = [al.alloc_sbuf_tensor(f"b_wg{i}", [128, 4096], BF16) for i in range(2)]
    wu2 = [al.alloc_sbuf_tensor(f"b_wu{i}", [128, 4096], BF16) for i in range(2)]
    wd2 = [al.alloc_sbuf_tensor(f"b_wd{i}", [128, 4096], BF16) for i in range(2)]
    rw = al.alloc_sbuf_tensor("b_rw", [128, 128], BF16)
    hsb2 = [al.alloc_sbuf_tensor(f"b_h{i}", [128, 2048], BF16) for i in range(2)]
    ssb2 = [al.alloc_sbuf_tensor(f"b_s{i}", [128, 512], BF16) for i in range(2)]
    usb2 = [al.alloc_sbuf_tensor(f"b_u{i}", [128, 512], BF16) for i in range(2)]
    eb = al.alloc_sbuf_tensor("b_e", [128, 256], F32)
    pn = al.alloc_sbuf_tensor("b_pn", [128, 256], F32)
    comb = al.alloc_sbuf_tensor("b_comb", [128, 256], F32)
    t1 = al.alloc_sbuf_tensor("b_t1", [128, 256], F32)
    t2 = al.alloc_sbuf_tensor("b_t2", [128, 256], F32)
    st2 = al.alloc_sbuf_tensor("b_st2", [128, 112], F32)

    q4 = env["psum_q"]
    pR, pG, pU = q4[0], q4[1], q4[2]
    pO = env["psum_o"]
    pT = env["psum_t"]

    env = dict(env)
    env["sq"] = sq
    env["stats"] = stats

    emit_load_cast(p, env, lambda i: Wr.ap()[i * 128:(i + 1) * 128, :],
                   Wr.name, 8, 16, rw, 0, wstage)

    for h in range(2):
        tb = h * 16
        for tl in range(16):
            tt = tb + tl
            xn_t = xn_t2[tl % 2]
            def consume(p, tl=tl, xn_t=xn_t):
                for c in range(FC):
                    emit_transpose_to(
                        p, env, xn_t[:, c * 128:(c + 1) * 128], xn_t.name,
                        xn_fm[:, c * 2048 + tl * 128: c * 2048 + (tl + 1) * 128],
                        f"{xn_fm.name}:{tl}", pT[c % 2],
                        evac_engine=("vector" if c % 2 == 0 else "scalar"))
            emit_ln_tile(p, env, Xi, tt, xin[tl % 2], xn_t, None, consume)
            p.add("gpsimd", lambda e, tt=tt, tl=tl: e.dma_start(
                acc[:, tl * 1024:(tl + 1) * 1024], Xi.ap()[tt * 128:(tt + 1) * 128, :]),
                reads=[Xi.name], writes=[f"b_acc:{tl}"], dma=f"b_accld{tl % 4}")
        for tl in range(16):
            for fc in range(FC):
                p.add("tensor", lambda e, tl=tl, fc=fc: e.matmul(
                    pR[:, tl * 16:(tl + 1) * 16],
                    xn_fm[:, fc * 2048 + tl * 128: fc * 2048 + (tl + 1) * 128],
                    rw[:, fc * 16:(fc + 1) * 16],
                    start=(fc == 0), stop=(fc == 7)),
                    reads=[f"{xn_fm.name}:{tl}", rw.name], writes=[pR.name])
        p.add("scalar", lambda e: e.activation(eb[:], pR[:, 0:256], AF.Exp),
              reads=[pR.name], writes=[eb.name])
        e3 = lambda t: t[:].rearrange("p (a b) -> p a b", b=16)
        bc = lambda t, c0: t[:, c0:c0 + 16].unsqueeze(2).broadcast_to([128, 16, 16])
        p.add("vector", lambda e: e.tensor_reduce(
            st2[:, 0:16].rearrange("p (a b) -> p a b", b=1), e3(eb), AX.X, ALU.add),
            reads=[eb.name], writes=[st2.name + ":lr"])
        p.add("vector", lambda e: e.reciprocal(st2[:, 16:32], st2[:, 0:16]),
              reads=[st2.name + ":lr"], writes=[st2.name + ":ilr"])
        p.add("vector", lambda e: e.tensor_tensor(e3(pn), e3(eb), bc(st2, 16), ALU.mult),
              reads=[eb.name, st2.name + ":ilr"], writes=[pn.name])
        p.add("vector", lambda e: e.tensor_reduce(
            st2[:, 32:48].rearrange("p (a b) -> p a b", b=1), e3(pn), AX.X, ALU.max),
            reads=[pn.name], writes=[st2.name + ":m1"])
        p.add("vector", lambda e: e.tensor_tensor(e3(t1), e3(pn), bc(st2, 32), ALU.is_ge),
              reads=[pn.name, st2.name + ":m1"], writes=[t1.name])
        p.add("vector", lambda e: e.scalar_tensor_tensor(
            e3(t2), e3(t1), NEG_BIG, e3(pn), ALU.mult, ALU.add),
            reads=[t1.name, pn.name], writes=[t2.name])
        p.add("vector", lambda e: e.tensor_reduce(
            st2[:, 48:64].rearrange("p (a b) -> p a b", b=1), e3(t2), AX.X, ALU.max),
            reads=[t2.name], writes=[st2.name + ":m2"])
        p.add("vector", lambda e: e.tensor_tensor(st2[:, 64:80], st2[:, 32:48],
                                                  st2[:, 48:64], ALU.add),
              reads=[st2.name + ":m1", st2.name + ":m2"], writes=[st2.name + ":dn"])
        p.add("vector", lambda e: e.tensor_scalar(st2[:, 80:96], st2[:, 64:80], 1e-9,
                                                  None, ALU.add),
              reads=[st2.name + ":dn"], writes=[st2.name + ":dn2"])
        p.add("vector", lambda e: e.reciprocal(st2[:, 96:112], st2[:, 80:96]),
              reads=[st2.name + ":dn2"], writes=[st2.name + ":idn"])
        p.add("vector", lambda e: e.tensor_tensor(e3(t1), e3(pn), bc(st2, 48), ALU.is_ge),
              reads=[pn.name, st2.name + ":m2"], writes=[t1.name])
        p.add("vector", lambda e: e.tensor_tensor(e3(t2), e3(pn), e3(t1), ALU.mult),
              reads=[pn.name, t1.name], writes=[t2.name])
        p.add("vector", lambda e: e.tensor_tensor(e3(comb), e3(t2), bc(st2, 96), ALU.mult),
              reads=[t2.name, st2.name + ":idn"], writes=[comb.name])

        experts = [("s", n, 4, 512) for n in range(2)] + \
                  [("r", r, 2, 256) for r in range(16)]

        def load_expert(xi):
            kind, ei, nmc, cw = experts[xi]
            wg, wu, wd = wg2[xi % 2], wu2[xi % 2], wd2[xi % 2]
            if kind == "s":
                WG, WU, WD = Wgs, Wus, Wds
            else:
                WG, WU, WD = Wgr, Wur, Wdr
            emit_load_cast(p, env, lambda i, ei=ei, WG=WG: WG.ap()[ei, i * 128:(i + 1) * 128, :],
                           WG.name, 8, cw, wg, 0, wstage, tag=":g")
            emit_load_cast(p, env, lambda i, ei=ei, WU=WU: WU.ap()[ei, i * 128:(i + 1) * 128, :],
                           WU.name, 8, cw, wu, 0, wstage, tag=":u")
            emit_load_cast(p, env, lambda i, ei=ei, WD=WD: WD.ap()[ei, i * 128:(i + 1) * 128, :],
                           WD.name, nmc, 1024, wd, 0, wstage, tag=":d")

        if h == 0:
            load_expert(0)
        for xi, (kind, ei, nmc, cw) in enumerate(experts):
            if xi + 1 < len(experts):
                load_expert(xi + 1)
            elif h == 0:
                load_expert(0)
            wg, wu, wd = wg2[xi % 2], wu2[xi % 2], wd2[xi % 2]
            it = [0]

            def gu_h(tg):
                hsb = hsb2[tg % 2]
                for mc in range(nmc):
                    pGt = q4[1] if it[0] % 2 == 0 else q4[0]
                    pUt = q4[2] if it[0] % 2 == 0 else q4[3]
                    ssb = ssb2[it[0] % 2]
                    usb = usb2[it[0] % 2]
                    it[0] += 1
                    for fc in range(FC):
                        p.add("tensor", lambda e, tg=tg, mc=mc, fc=fc, cw=cw, wg=wg, pGt=pGt: e.matmul(
                            pGt[:, 0:512],
                            wg[:, fc * cw + mc * 128: fc * cw + (mc + 1) * 128],
                            xn_fm[:, fc * 2048 + tg * 512: fc * 2048 + tg * 512 + 512],
                            start=(fc == 0), stop=(fc == 7)),
                            reads=[wg.name + ":g"] + [f"{xn_fm.name}:{tg * 4 + j}" for j in range(4)],
                            writes=[pGt.name])
                    for fc in range(FC):
                        p.add("tensor", lambda e, tg=tg, mc=mc, fc=fc, cw=cw, wu=wu, pUt=pUt: e.matmul(
                            pUt[:, 0:512],
                            wu[:, fc * cw + mc * 128: fc * cw + (mc + 1) * 128],
                            xn_fm[:, fc * 2048 + tg * 512: fc * 2048 + tg * 512 + 512],
                            start=(fc == 0), stop=(fc == 7)),
                            reads=[wu.name + ":u"] + [f"{xn_fm.name}:{tg * 4 + j}" for j in range(4)],
                            writes=[pUt.name])
                    p.add("scalar", lambda e, pGt=pGt, ssb=ssb: e.activation(
                        ssb[:], pGt[:, 0:512], AF.Silu),
                        reads=[pGt.name], writes=[ssb.name])
                    p.add("vector", lambda e, pUt=pUt, usb=usb: e.tensor_copy(
                        usb[:], pUt[:, 0:512]),
                        reads=[pUt.name], writes=[usb.name])
                    p.add("vector", lambda e, mc=mc, ssb=ssb, usb=usb, hsb=hsb: e.tensor_tensor(
                        hsb[:, mc * 512:(mc + 1) * 512], ssb[:], usb[:], ALU.mult),
                        reads=[ssb.name, usb.name], writes=[f"{hsb.name}:{mc}"])

            def down(tg):
                hsb = hsb2[tg % 2]
                for tl2 in range(4):
                    tl = tg * 4 + tl2
                    for mc in range(nmc):
                        for hf in range(2):
                            p.add("tensor", lambda e, mc=mc, hf=hf, tl2=tl2, nmc=nmc, wd=wd, hsb=hsb: e.matmul(
                                pO[hf][:, 0:512],
                                hsb[:, mc * 512 + tl2 * 128: mc * 512 + (tl2 + 1) * 128],
                                wd[:, mc * 1024 + hf * 512: mc * 1024 + hf * 512 + 512],
                                start=(mc == 0), stop=(mc == nmc - 1)),
                                reads=[f"{hsb.name}:{mc}", wd.name + ":d"], writes=[pO[hf].name])
                    for hf in range(2):
                        csl = slice(tl * 1024 + hf * 512, tl * 1024 + hf * 512 + 512)
                        if kind == "s":
                            p.add("vector", lambda e, csl=csl, hf=hf: e.tensor_add(
                                acc[:, csl], acc[:, csl], pO[hf][:, 0:512]),
                                reads=[pO[hf].name, f"b_acc:{tl}"], writes=[f"b_acc:{tl}"])
                        else:
                            p.add("vector", lambda e, csl=csl, hf=hf, tl=tl, ei=ei: e.scalar_tensor_tensor(
                                acc[:, csl], pO[hf][:, 0:512],
                                comb[:, tl * 16 + ei: tl * 16 + ei + 1],
                                acc[:, csl], ALU.mult, ALU.add),
                                reads=[pO[hf].name, comb.name, f"b_acc:{tl}"],
                                writes=[f"b_acc:{tl}"])

            gu_h(0)
            for tg in range(4):
                if tg + 1 < 4:
                    gu_h(tg + 1)
                down(tg)
        for tl in range(16):
            tt = tb + tl
            p.add("gpsimd", lambda e, tt=tt, tl=tl: e.dma_start(
                Xo.ap()[tt * 128:(tt + 1) * 128, :], acc[:, tl * 1024:(tl + 1) * 1024]),
                reads=[f"b_acc:{tl}"], writes=[Xo.name], dma="b_store")



def build_block_c(nc, p, al, Xi, Xmid, Xo, WqT, WkT, WvT, WpT, Bq, Bk, Bv, Bp,
                  CosT, SinT, XnH, IlS, env):
    sq = al.alloc_sbuf_tensor("c_sq", [128, 1024], F32)
    stats = al.alloc_sbuf_tensor("c_st", [128, 8], F32)

    K_fm = al.alloc_sbuf_tensor("c_K", [128, 4 * T], BF16)
    V_tm = al.alloc_sbuf_tensor("c_V", [128, NT * 512], BF16)
    cos_sb = al.alloc_sbuf_tensor("c_cos", [128, 1024], BF16)
    sin_sb = al.alloc_sbuf_tensor("c_sin", [128, 1024], BF16)
    xnt_s = [al.alloc_sbuf_tensor(f"c_xns{i}", [128, 1024], BF16) for i in range(2)]
    xnfm_t2 = [al.alloc_sbuf_tensor(f"c_xnfm{i}", [128, 1024], BF16) for i in range(2)]
    wq_bf = al.alloc_sbuf_tensor("c_wq", [128, 4096], BF16)
    wp_bf = al.alloc_sbuf_tensor("c_wp", [128, 4096], BF16)
    kraw = al.alloc_sbuf_tensor("c_kraw", [128, 512], F32)
    krot2 = [al.alloc_sbuf_tensor(f"c_krot{i}", [128, 512], BF16) for i in range(2)]
    tmpa = al.alloc_sbuf_tensor("c_tma", [128, 256], F32)
    tmpb = al.alloc_sbuf_tensor("c_tmb", [128, 256], F32)
    abf = [al.alloc_sbuf_tensor(f"c_abf{i}", [128, T], BF16) for i in range(4)]
    at_sb = [al.alloc_sbuf_tensor(f"c_at{i}", [128, 512], BF16) for i in range(2)]
    qfm = al.alloc_sbuf_tensor("c_qfm", [128, 2048], BF16)
    yfm = al.alloc_sbuf_tensor("c_yfm", [128, 2048], F32)
    ybf = al.alloc_sbuf_tensor("c_ybf", [128, 2048], BF16)
    il_all = al.alloc_sbuf_tensor("c_il", [128, 32], F32)
    ilT = sq
    smx = al.alloc_sbuf_tensor("c_smx", [128, 80], F32)
    rhl = al.alloc_sbuf_tensor("c_rhl", [1, 6 * 512], BF16)
    bphl = al.alloc_sbuf_tensor("c_bp", [1, 2 * 1024], BF16)
    xnew = [al.alloc_sbuf_tensor(f"c_xn{i}", [128, 1024], F32) for i in range(2)]
    wstage = [al.alloc_sbuf_tensor("c_wst", [128, 1024], F32)]
    rows = sq
    rtmp = sq
    ar1 = sbuf_arena(al, "c_ar1", 8192)
    ar2 = sbuf_arena(al, "c_ar2", 16384)
    xin = [al.alloc_sbuf_tensor_at(f"c_xin{i}", [128, 1024], F32, offset=ar1 + 4096 * i)
           for i in range(2)]
    xold = [al.alloc_sbuf_tensor_at(f"c_xo{i}", [128, 1024], F32, offset=ar1 + 4096 * i)
            for i in range(2)]
    wk_bf = al.alloc_sbuf_tensor_at("c_wk", [128, 4096], BF16, offset=ar2)
    wv_bf = al.alloc_sbuf_tensor_at("c_wv", [128, 4096], BF16, offset=ar2 + 8192)
    s_row = al.alloc_sbuf_tensor_at("c_srow", [128, T], F32, offset=ar2)
    xn_t2 = [al.alloc_sbuf_tensor_at(f"c_xnt{i}", [128, 1024], BF16,
                                     offset=ar2 + 2048 * i) for i in range(2)]

    q4 = env["psum_q"]
    pP, pYH, pS = q4[0], q4[1], [q4[2], q4[3]]
    pT = env["psum_t"]
    pO = env["psum_o"]

    env = dict(env)
    env["sq"] = sq
    env["stats"] = stats
    ones_bf = env["ones_bf"]
    onesf = env["ones_f"]
    ident = env["ident_bf"]

    def hilo_into(src_hbm, c0, n, dsth, dstl, keyh, keyl):
        r0 = sq[0:1, 0:n]
        r1 = sq[0:1, 512:512 + n]
        p.add("sync", lambda e: e.dma_start(r0, src_hbm.ap()[0:1, c0:c0 + n]),
              reads=[src_hbm.name], writes=[sq.name], dma="c_rowld")
        p.add("vector", lambda e: e.tensor_copy(dsth, r0),
              reads=[sq.name], writes=[keyh])
        p.add("vector", lambda e: e.tensor_copy(r1, dsth),
              reads=[keyh], writes=[sq.name])
        p.add("vector", lambda e: e.tensor_sub(r1, r0, r1),
              reads=[sq.name], writes=[sq.name])
        p.add("vector", lambda e: e.tensor_copy(dstl, r1),
              reads=[sq.name], writes=[keyl])

    def hilo(src_hbm, c0, n, slot):
        assert n <= 512
        dsth = rhl[:, slot * 512: slot * 512 + n]
        dstl = rhl[:, (slot + 1) * 512: (slot + 1) * 512 + n]
        keyh = rhl.name + f":{slot}h"
        keyl = rhl.name + f":{slot}l"
        hilo_into(src_hbm, c0, n, dsth, dstl, keyh, keyl)
        return dsth, dstl, [keyh, keyl]


    for nmsb, hbm in ((cos_sb, CosT), (sin_sb, SinT)):
        p.add("sync", lambda e, nmsb=nmsb, hbm=hbm: e.dma_start(
            wstage[0][:].rearrange("p (a b) -> p a b", b=32),
            hbm.ap().rearrange("(a p) b -> p a b", p=128)),
            reads=[hbm.name], writes=[wstage[0].name], dma=wstage[0].name)
        p.add("vector", lambda e, nmsb=nmsb: e.tensor_copy(nmsb[:], wstage[0][:]),
              reads=[wstage[0].name], writes=[nmsb.name])

    def rope(src_f32, dst_bf, tile):
        va = lambda t: t[:].rearrange("p (h x) -> p h x", x=64)[:, :, 0:32]
        vb = lambda t: t[:].rearrange("p (h x) -> p h x", x=64)[:, :, 32:64]
        cosb = cos_sb[:, tile * 32:(tile + 1) * 32].unsqueeze(1).broadcast_to([128, 8, 32])
        sinb = sin_sb[:, tile * 32:(tile + 1) * 32].unsqueeze(1).broadcast_to([128, 8, 32])
        t3 = lambda t: t[:].rearrange("p (h x) -> p h x", x=32)
        ks, kd = src_f32.name, dst_bf.name
        p.add("vector", lambda e: e.tensor_tensor(t3(tmpa), va(src_f32), cosb, ALU.mult),
              reads=[ks, cos_sb.name], writes=[tmpa.name])
        p.add("vector", lambda e: e.tensor_tensor(t3(tmpb), vb(src_f32), sinb, ALU.mult),
              reads=[ks, sin_sb.name], writes=[tmpb.name])
        p.add("vector", lambda e: e.tensor_tensor(va(dst_bf), t3(tmpa), t3(tmpb), ALU.subtract),
              reads=[tmpa.name, tmpb.name], writes=[kd])
        p.add("vector", lambda e: e.tensor_tensor(t3(tmpa), va(src_f32), sinb, ALU.mult),
              reads=[ks, sin_sb.name], writes=[tmpa.name])
        p.add("vector", lambda e: e.tensor_tensor(t3(tmpb), vb(src_f32), cosb, ALU.mult),
              reads=[ks, cos_sb.name], writes=[tmpb.name])
        p.add("vector", lambda e: e.tensor_tensor(vb(dst_bf), t3(tmpa), t3(tmpb), ALU.add),
              reads=[tmpa.name, tmpb.name], writes=[kd])

    def xn_tile_to_fm(tile, stream_slot):
        xs = xnt_s[stream_slot]
        xnfm_t = xnfm_t2[stream_slot]
        p.add("sync", lambda e, tile=tile, xs=xs: e.dma_start(
            xs[:], XnH.ap()[tile * 128:(tile + 1) * 128, :]),
            reads=[XnH.name], writes=[xs.name], dma=xs.name)
        for c in range(FC):
            emit_transpose_to(p, env, xs[:, c * 128:(c + 1) * 128], xs.name,
                              xnfm_t[:, c * 128:(c + 1) * 128], xnfm_t.name,
                              pT[c % 2],
                              evac_engine=("vector" if c % 2 == 0 else "scalar"))
        return xnfm_t

    GQ = 8
    for H in range(2):
        c0 = H * 512
        if H == 0:
            pass
        else:
            p.region_switch(s_row.name, wk_bf.name)
            p.region_switch(s_row.name, wv_bf.name)
        emit_load_cast(p, env, lambda i, c0=c0: WkT.ap()[i * 128:(i + 1) * 128, c0:c0 + 512],
                       WkT.name, 8, 512, wk_bf, 0, wstage)
        emit_load_cast(p, env, lambda i: WvT.ap()[i * 128:(i + 1) * 128, c0:c0 + 512],
                       WvT.name, 8, 512, wv_bf, 0, wstage)
        bk_h, bk_l, bk_keys = hilo(Bk, c0, 512, 0)
        bv_h, bv_l, bv_keys = hilo(Bv, c0, 512, 2)
        for kt in range(NT):
            if H == 0:
                xn_t = xnt_s[kt % 2]
                def consume(p, kt=kt, xn_t=xn_t):
                    p.add("gpsimd", lambda e, kt=kt, xn_t=xn_t: e.dma_start(
                        XnH.ap()[kt * 128:(kt + 1) * 128, :], xn_t[:]),
                        reads=[xn_t.name], writes=[XnH.name], dma="c_xnst")
                emit_ln_tile(p, env, Xi, kt, xin[kt % 2], xn_t, None, consume)
                xnfm_t = xnfm_t2[kt % 2]
                for c in range(FC):
                    emit_transpose_to(p, env, xn_t[:, c * 128:(c + 1) * 128], xn_t.name,
                                      xnfm_t[:, c * 128:(c + 1) * 128], xnfm_t.name,
                                      pT[c % 2],
                                      evac_engine=("vector" if c % 2 == 0 else "scalar"))
            else:
                xnfm_t = xn_tile_to_fm(kt, kt % 2)
            krot = krot2[kt % 2]
            for fc in range(FC):
                p.add("tensor", lambda e, fc=fc, xnfm_t=xnfm_t: e.matmul(
                    pP[:, 0:512], xnfm_t[:, fc * 128:(fc + 1) * 128],
                    wk_bf[:, fc * 512:(fc + 1) * 512],
                    start=(fc == 0), stop=False),
                    reads=[xnfm_t.name, wk_bf.name], writes=[pP.name])
            p.add("tensor", lambda e: e.matmul(pP[:, 0:512], ones_bf[:], bk_h,
                                               start=False, stop=False),
                  reads=[ones_bf.name] + bk_keys, writes=[pP.name])
            p.add("tensor", lambda e: e.matmul(pP[:, 0:512], ones_bf[:], bk_l,
                                               start=False, stop=True),
                  reads=[ones_bf.name] + bk_keys, writes=[pP.name])
            p.add("vector", lambda e, kraw=kraw: e.tensor_copy(kraw[:], pP[:, 0:512]),
                  reads=[pP.name], writes=[kraw.name])
            rope(kraw, krot, kt)
            for c in range(4):
                emit_transpose_to(p, env, krot[:, c * 128:(c + 1) * 128], krot.name,
                                  K_fm[:, c * T + kt * 128: c * T + (kt + 1) * 128],
                                  f"{K_fm.name}:{kt}", pT[c % 2],
                                  evac_engine=("vector" if c % 2 == 0 else "scalar"))
            for fc in range(FC):
                p.add("tensor", lambda e, fc=fc, xnfm_t=xnfm_t: e.matmul(
                    pP[:, 0:512], xnfm_t[:, fc * 128:(fc + 1) * 128],
                    wv_bf[:, fc * 512:(fc + 1) * 512],
                    start=(fc == 0), stop=False),
                    reads=[xnfm_t.name, wv_bf.name], writes=[pP.name])
            p.add("tensor", lambda e: e.matmul(pP[:, 0:512], ones_bf[:], bv_h,
                                               start=False, stop=False),
                  reads=[ones_bf.name] + bv_keys, writes=[pP.name])
            p.add("tensor", lambda e: e.matmul(pP[:, 0:512], ones_bf[:], bv_l,
                                               start=False, stop=True),
                  reads=[ones_bf.name] + bv_keys, writes=[pP.name])
            p.add("scalar", lambda e, kt=kt: e.copy(
                V_tm[:, kt * 512:(kt + 1) * 512], pP[:, 0:512]),
                reads=[pP.name], writes=[f"{V_tm.name}:{kt}"])

        if H == 0:
            p.region_switch(xin[0].name, xold[0].name)
            p.region_switch(xin[1].name, xold[1].name)
        p.region_switch(wk_bf.name, s_row.name)
        p.region_switch(wv_bf.name, s_row.name)
        emit_load_cast(p, env, lambda i, c0=c0: WqT.ap()[i * 128:(i + 1) * 128, c0:c0 + 512],
                       WqT.name, 8, 512, wq_bf, 0, wstage)
        emit_load_cast(p, env, lambda i, c0=c0: WpT.ap()[c0 + i * 128: c0 + (i + 1) * 128, :],
                       WpT.name, 4, 1024, wp_bf, 0, wstage)
        bq_h, bq_l, bq_keys = hilo(Bq, c0, 512, 4)
        if H == 0:
            for hseg in range(2):
                hilo_into(Bp, hseg * 512, 512,
                          bphl[:, hseg * 512: hseg * 512 + 512],
                          bphl[:, 1024 + hseg * 512: 1024 + hseg * 512 + 512],
                          bphl.name + ":h", bphl.name + ":l")
        Xsrc = Xi if H == 0 else Xmid
        Xdst = Xmid if H == 0 else Xo
        for g in range(GQ):
            for qtl in range(4):
                qt = g * 4 + qtl
                xnfm_q = xn_tile_to_fm(qt, qtl % 2)
                for fc in range(FC):
                    p.add("tensor", lambda e, fc=fc, xnfm_q=xnfm_q: e.matmul(
                        pP[:, 0:512], xnfm_q[:, fc * 128:(fc + 1) * 128],
                        wq_bf[:, fc * 512:(fc + 1) * 512],
                        start=(fc == 0), stop=False),
                        reads=[xnfm_q.name, wq_bf.name], writes=[pP.name])
                p.add("tensor", lambda e: e.matmul(pP[:, 0:512], ones_bf[:], bq_h,
                                                   start=False, stop=False),
                      reads=[ones_bf.name] + bq_keys, writes=[pP.name])
                p.add("tensor", lambda e: e.matmul(pP[:, 0:512], ones_bf[:], bq_l,
                                                   start=False, stop=True),
                      reads=[ones_bf.name] + bq_keys, writes=[pP.name])
                p.add("vector", lambda e, kraw=kraw: e.tensor_copy(kraw[:], pP[:, 0:512]),
                      reads=[pP.name], writes=[kraw.name])
                krot = krot2[qtl % 2]
                rope(kraw, krot, qt)
                for c in range(4):
                    emit_transpose_to(
                        p, env, krot[:, c * 128:(c + 1) * 128], krot.name,
                        qfm[:, qtl * 512 + c * 128: qtl * 512 + (c + 1) * 128],
                        f"{qfm.name}:{qtl}", pT[c % 2],
                        evac_engine=("vector" if c % 2 == 0 else "scalar"))
            ng = g + 1
            L = ng * 512
            mask4 = env["mask4"]
            slots = [pS[0], pS[1], pP, pO[0], pO[1]]
            for hh in range(8):
                c = hh // 2
                pb = (hh % 2) * 64
                for qtl in range(4):
                    qt = g * 4 + qtl
                    v = qt % 4
                    sb = qtl * 20
                    kgm = f"{smx.name}:{qtl}:gm"
                    klsp = f"{smx.name}:{qtl}:lsp"
                    kbm = f"{smx.name}:{qtl}:bm"
                    kls = f"{smx.name}:{qtl}:ls"
                    km = f"{smx.name}:{qtl}:m"
                    direct = ng <= NDIR
                    for g2 in range(ng):
                        ps = slots[(qtl * ng + g2) % len(slots)] if direct else slots[g2 % 3]
                        kk = [f"{K_fm.name}:{4 * g2 + j}" for j in range(4)]
                        p.add("tensor", lambda e, g2=g2, qtl=qtl, c=c, pb=pb, ps=ps: e.matmul(
                            ps[:, 0:512],
                            qfm[pb:pb + 64, qtl * 512 + c * 128: qtl * 512 + (c + 1) * 128],
                            K_fm[pb:pb + 64, c * T + g2 * 512: c * T + g2 * 512 + 512],
                            start=True, stop=True),
                            reads=[f"{qfm.name}:{qtl}"] + kk, writes=[ps.name])
                        if direct:
                            if g2 == ng - 1:
                                p.add("vector", lambda e, ps=ps, v=v: e.tensor_tensor(
                                    ps[:, 0:512], ps[:, 0:512],
                                    mask4[:, v * 512:(v + 1) * 512], ALU.add),
                                    reads=[ps.name, mask4.name], writes=[ps.name])
                            p.add("vector", lambda e, ps=ps, g2=g2, sb=sb: e.tensor_reduce(
                                smx[:, sb + g2: sb + g2 + 1], ps[:, 0:512], AX.X, ALU.max),
                                reads=[ps.name], writes=[kgm])
                        else:
                            if g2 == ng - 1:
                                p.add("vector", lambda e, ps=ps, g2=g2, v=v: e.tensor_tensor(
                                    s_row[:, g2 * 512: g2 * 512 + 512], ps[:, 0:512],
                                    mask4[:, v * 512:(v + 1) * 512], ALU.add),
                                    reads=[ps.name, mask4.name], writes=[s_row.name])
                            else:
                                eng = "vector" if g2 % 2 == 0 else "scalar"
                                p.add(eng, lambda e, g2=g2, ps=ps, eng=eng: (
                                    e.tensor_copy(s_row[:, g2 * 512:(g2 + 1) * 512], ps[:, 0:512])
                                    if eng == "vector" else
                                    e.copy(s_row[:, g2 * 512:(g2 + 1) * 512], ps[:, 0:512])),
                                    reads=[ps.name], writes=[s_row.name])
                            p.add("vector", lambda e, g2=g2, sb=sb: e.tensor_reduce(
                                smx[:, sb + g2: sb + g2 + 1],
                                s_row[:, g2 * 512:(g2 + 1) * 512], AX.X, ALU.max),
                                reads=[s_row.name], writes=[kgm])
                    p.add("vector", lambda e, sb=sb, ng=ng: e.tensor_reduce(
                        smx[:, sb + 18: sb + 19], smx[:, sb: sb + ng], AX.X, ALU.max),
                        reads=[kgm], writes=[km])
                    p.add("vector", lambda e, sb=sb: e.tensor_scalar(
                        smx[:, sb + 16: sb + 17], smx[:, sb + 18: sb + 19], -0.125,
                        None, ALU.mult),
                        reads=[km], writes=[kbm])
                    if direct:
                        for g2 in range(ng):
                            ps = slots[(qtl * ng + g2) % len(slots)]
                            p.add("scalar", lambda e, ps=ps, g2=g2, qtl=qtl, sb=sb: e.activation(
                                abf[qtl][:, g2 * 512:(g2 + 1) * 512], ps[:, 0:512],
                                AF.Exp, bias=smx[:, sb + 16: sb + 17], scale=0.125,
                                accum_out=smx[:, sb + 8 + g2: sb + 9 + g2]),
                                reads=[ps.name, kbm],
                                writes=[f"{abf[qtl].name}:{g2}", klsp])
                        p.add("vector", lambda e, sb=sb, ng=ng: e.tensor_reduce(
                            smx[:, sb + 17: sb + 18], smx[:, sb + 8: sb + 8 + ng],
                            AX.X, ALU.add),
                            reads=[klsp], writes=[kls])
                    else:
                        p.add("scalar", lambda e, L=L, qtl=qtl, sb=sb: e.activation(
                            abf[qtl][:, 0:L], s_row[:, 0:L], AF.Exp,
                            bias=smx[:, sb + 16: sb + 17], scale=0.125,
                            accum_out=smx[:, sb + 17: sb + 18]),
                            reads=[s_row.name, kbm],
                            writes=[kls] + [f"{abf[qtl].name}:{gg2}" for gg2 in range(ng)])
                    p.add("vector", lambda e, qtl=qtl, hh=hh, sb=sb: e.reciprocal(
                        il_all[:, qtl * 8 + hh: qtl * 8 + hh + 1],
                        smx[:, sb + 17: sb + 18]),
                        reads=[kls], writes=[il_all.name])
                nkb = 4 * ng
                def emit_at(kb):
                    pt = pT[kb % 2]
                    ab = at_sb[kb % 2]
                    for qtl in range(4):
                        p.add("tensor", lambda e, kb=kb, qtl=qtl, pt=pt: e.transpose(
                            pt[:, qtl * 128:(qtl + 1) * 128],
                            abf[qtl][:, kb * 128:(kb + 1) * 128], ident[:]),
                            reads=[f"{abf[qtl].name}:{kb // 4}", ident.name],
                            writes=[pt.name])
                    eng = "vector" if kb % 2 == 0 else "scalar"
                    p.add(eng, lambda e, pt=pt, ab=ab, eng=eng: (
                        e.tensor_copy(ab[:], pt[:, 0:512]) if eng == "vector"
                        else e.copy(ab[:], pt[:, 0:512])),
                        reads=[pt.name], writes=[ab.name])
                emit_at(0)
                for kb in range(nkb):
                    if kb + 1 < nkb:
                        emit_at(kb + 1)
                    ab = at_sb[kb % 2]
                    p.add("tensor", lambda e, kb=kb, hh=hh, ab=ab, nkb=nkb: e.matmul(
                        pYH[0:64, 0:512],
                        V_tm[:, kb * 512 + hh * 64: kb * 512 + (hh + 1) * 64],
                        ab[:], start=(kb == 0), stop=(kb == nkb - 1)),
                        reads=[f"{V_tm.name}:{kb}", ab.name], writes=[pYH.name])
                p.add("vector" if hh % 2 == 0 else "scalar",
                      lambda e, hh=hh, c=c, pb=pb: (
                          e.tensor_copy(yfm[pb:pb + 64, c * 512:(c + 1) * 512],
                                        pYH[0:64, 0:512])
                          if hh % 2 == 0 else
                          e.copy(yfm[pb:pb + 64, c * 512:(c + 1) * 512],
                                 pYH[0:64, 0:512])),
                      reads=[pYH.name], writes=[yfm.name])
            identf = env["ident_f"]
            p.add("tensor", lambda e, identf=identf: e.transpose(
                pS[1][0:32, 0:128], il_all[:], identf[:]),
                reads=[il_all.name, identf.name], writes=[pS[1].name])
            p.add("vector", lambda e: e.tensor_copy(sq[64:96, 0:128], pS[1][0:32, 0:128]),
                  reads=[pS[1].name], writes=[sq.name])
            p.add("gpsimd", lambda e: e.dma_start(IlS.ap(), sq[64:96, 0:128]),
                  reads=[sq.name], writes=[IlS.name], dma="c_ils")
            for c in range(4):
                src = IlS.ap().rearrange("(q hh) c -> q hh c", hh=8)[:, 2 * c:2 * c + 2, :]
                dst = sq[0:1, 0:1024].rearrange("o (q s c) -> o q s c", q=4, s=2)
                p.add("sync", lambda e, src=src, dst=dst: e.dma_start(dst, src),
                      reads=[IlS.name], writes=[sq.name], dma="c_ilr")
                for sub in range(2):
                    for qtl in range(4):
                        p.add("tensor", lambda e, qtl=qtl, sub=sub: e.matmul(
                            pYH[sub * 64:(sub + 1) * 64, qtl * 128:(qtl + 1) * 128],
                            onesf[0:1, 0:64],
                            sq[0:1, qtl * 256 + sub * 128: qtl * 256 + (sub + 1) * 128],
                            start=True, stop=True),
                            reads=[sq.name, onesf.name], writes=[pYH.name])
                p.add("vector", lambda e, c=c: e.tensor_tensor(
                    ybf[:, c * 512:(c + 1) * 512], yfm[:, c * 512:(c + 1) * 512],
                    pYH[:, 0:512], ALU.mult),
                    reads=[yfm.name, pYH.name], writes=[ybf.name])
            for qtl in range(4):
                qt = g * 4 + qtl
                p.add("gpsimd", lambda e, qt=qt, qtl=qtl, Xsrc=Xsrc: e.dma_start(
                    xold[qtl % 2][:], Xsrc.ap()[qt * 128:(qt + 1) * 128, :]),
                    reads=[Xsrc.name], writes=[xold[qtl % 2].name],
                    dma=xold[qtl % 2].name)
                for c in range(4):
                    for hf in range(2):
                        p.add("tensor", lambda e, c=c, hf=hf, qtl=qtl: e.matmul(
                            pO[hf][:, 0:512],
                            ybf[:, c * 512 + qtl * 128: c * 512 + (qtl + 1) * 128],
                            wp_bf[:, c * 1024 + hf * 512: c * 1024 + hf * 512 + 512],
                            start=(c == 0), stop=(c == 3 and H == 1)),
                            reads=[ybf.name, wp_bf.name], writes=[pO[hf].name])
                if H == 0:
                    for hf in range(2):
                        p.add("tensor", lambda e, hf=hf: e.matmul(
                            pO[hf][:, 0:512], ones_bf[:],
                            bphl[:, hf * 512: hf * 512 + 512],
                            start=False, stop=False),
                            reads=[ones_bf.name, bphl.name + ":h"], writes=[pO[hf].name])
                        p.add("tensor", lambda e, hf=hf: e.matmul(
                            pO[hf][:, 0:512], ones_bf[:],
                            bphl[:, 1024 + hf * 512: 1024 + hf * 512 + 512],
                            start=False, stop=True),
                            reads=[ones_bf.name, bphl.name + ":l"], writes=[pO[hf].name])
                kxn = xnew[qtl % 2].name
                for hf in range(2):
                    p.add("vector", lambda e, qtl=qtl, hf=hf: e.tensor_add(
                        xnew[qtl % 2][:, hf * 512: hf * 512 + 512], pO[hf][:, 0:512],
                        xold[qtl % 2][:, hf * 512: hf * 512 + 512]),
                        reads=[pO[hf].name, xold[qtl % 2].name], writes=[kxn])
                p.add("gpsimd", lambda e, qt=qt, qtl=qtl, Xdst=Xdst: e.dma_start(
                    Xdst.ap()[qt * 128:(qt + 1) * 128, :], xnew[qtl % 2][:]),
                    reads=[kxn], writes=[Xdst.name], dma="c_st" + Xdst.name)



def build_block_d(nc, p, al, Xi, Xo, W1, B1c, W2, B2, WrT, Br, env):
    rxin = al.suballoc(8192)
    xin = [al.alloc_sbuf_tensor_at(f"d_xin{i}", [128, 1024], F32, offset=rxin + 4096 * i)
           for i in range(2)]
    stats = al.alloc_sbuf_tensor("d_st", [128, 8], F32)
    xn_fm = al.alloc_sbuf_tensor("d_xnfm", [128, FC * T], BF16)
    ract = al.suballoc(65536)
    act = al.alloc_sbuf_tensor_at("d_act", [128, 16 * 2048], BF16, offset=ract)
    sq = al.alloc_sbuf_tensor_at("d_sq", [128, 1024], F32, offset=ract)
    xn_t2 = [al.alloc_sbuf_tensor_at(f"d_xnt{i}", [128, 1024], BF16,
                                     offset=ract + 4096 + 2048 * i) for i in range(2)]
    w2_bf = al.alloc_sbuf_tensor("d_w2", [128, 16 * 1024], BF16)
    w1a = [al.alloc_sbuf_tensor(f"d_w1a{i}", [128, 1024], BF16) for i in range(2)]
    w1g = [al.alloc_sbuf_tensor(f"d_w1g{i}", [128, 1024], BF16) for i in range(2)]
    w1stage = [al.alloc_sbuf_tensor(f"d_w1s{i}", [128, 1024], F32) for i in range(2)]
    w1stage += [al.alloc_sbuf_tensor_at(f"d_w1s{i+2}", [128, 1024], F32,
                                        offset=rxin + 4096 * i) for i in range(2)]
    wstage = w1stage[0:2]
    rw = al.alloc_sbuf_tensor("d_rw", [128, 64], BF16)
    r_rt = al.suballoc(4096)
    lg = al.alloc_sbuf_tensor_at("d_lg", [128, 256], F32, offset=r_rt)
    ev = al.alloc_sbuf_tensor_at("d_ev", [128, 256], F32, offset=r_rt + 1024)
    t1 = al.alloc_sbuf_tensor_at("d_t1", [128, 256], F32, offset=r_rt + 2048)
    t2 = al.alloc_sbuf_tensor_at("d_t2", [128, 256], F32, offset=r_rt + 3072)
    comb = al.alloc_sbuf_tensor("d_comb", [128, 256], F32)
    combb = al.alloc_sbuf_tensor("d_combb", [128, 256], BF16)
    st2 = al.alloc_sbuf_tensor("d_st2", [128, 112], F32)
    b1c2 = [al.alloc_sbuf_tensor(f"d_b1{i}", [128, 32], F32) for i in range(2)]
    brrow = al.alloc_sbuf_tensor("d_br", [1, 8], F32)
    brhl = al.alloc_sbuf_tensor("d_brhl", [1, 16], BF16)
    brt = al.alloc_sbuf_tensor("d_brt", [1, 8], F32)
    combT = al.alloc_sbuf_tensor("d_cT", [8, 128], BF16)
    r_sg = al.suballoc(2048)
    b2bf = al.alloc_sbuf_tensor_at("d_b2b", [8, 1024], BF16, offset=r_sg)
    ssb = [al.alloc_sbuf_tensor_at(f"d_s{i}", [128, 512], BF16, offset=r_sg + 1024 * i)
           for i in range(2)]
    gsb = [al.alloc_sbuf_tensor(f"d_g{i}", [128, 512], BF16) for i in range(2)]
    obuf = [al.alloc_sbuf_tensor_at("d_ob0", [128, 1024], F32, offset=r_rt),
            al.alloc_sbuf_tensor("d_ob1", [128, 1024], F32)]

    q4 = env["psum_q"]
    pR, pA, pG, pC = q4[0], q4[1], q4[2], q4[3]
    pO = env["psum_o"]
    pT = env["psum_t"]

    env = dict(env)
    env["sq"] = sq
    env["stats"] = stats
    ones_bf = env["ones_bf"]
    ident = env["ident_bf"]

    for tt in range(NT):
        xn_t = xn_t2[tt % 2]
        def consume(p, tt=tt, xn_t=xn_t):
            for c in range(FC):
                emit_transpose_to(
                    p, env, xn_t[:, c * 128:(c + 1) * 128], xn_t.name,
                    xn_fm[:, c * T + tt * 128: c * T + (tt + 1) * 128],
                    f"{xn_fm.name}:{tt}", pT[c % 2],
                    evac_engine=("vector" if c % 2 == 0 else "scalar"))
        emit_ln_tile(p, env, Xi, tt, xin[tt % 2], xn_t, None, consume)

    emit_load_cast(p, env, lambda i: WrT.ap()[i * 128:(i + 1) * 128, :],
                   WrT.name, 8, 8, rw, 0, wstage)
    p.add("sync", lambda e: e.dma_start(brrow[:], Br.ap()[0:1, :]),
          reads=[Br.name], writes=[brrow.name], dma=brrow.name)
    p.add("vector", lambda e: e.tensor_copy(brhl[:, 0:8], brrow[:]),
          reads=[brrow.name], writes=[brhl.name + ":h"])
    p.add("vector", lambda e: e.tensor_copy(brt[:], brhl[:, 0:8]),
          reads=[brhl.name + ":h"], writes=[brt.name])
    p.add("vector", lambda e: e.tensor_sub(brt[:], brrow[:], brt[:]),
          reads=[brrow.name, brt.name], writes=[brt.name])
    p.add("vector", lambda e: e.tensor_copy(brhl[:, 8:16], brt[:]),
          reads=[brt.name], writes=[brhl.name + ":l"])
    for tt in range(NT):
        for fc in range(FC):
            p.add("tensor", lambda e, tt=tt, fc=fc: e.matmul(
                pR[:, tt * 8:(tt + 1) * 8],
                xn_fm[:, fc * T + tt * 128: fc * T + (tt + 1) * 128],
                rw[:, fc * 8:(fc + 1) * 8], start=(fc == 0), stop=False),
                reads=[f"{xn_fm.name}:{tt}", rw.name], writes=[pR.name])
        p.add("tensor", lambda e, tt=tt: e.matmul(
            pR[:, tt * 8:(tt + 1) * 8], ones_bf[:], brhl[:, 0:8],
            start=False, stop=False),
            reads=[ones_bf.name, brhl.name + ":h"], writes=[pR.name])
        p.add("tensor", lambda e, tt=tt: e.matmul(
            pR[:, tt * 8:(tt + 1) * 8], ones_bf[:], brhl[:, 8:16],
            start=False, stop=True),
            reads=[ones_bf.name, brhl.name + ":l"], writes=[pR.name])
    p.add("vector", lambda e: e.tensor_copy(lg[:], pR[:, 0:256]),
          reads=[pR.name], writes=[lg.name])
    e3 = lambda t: t[:].rearrange("p (a b) -> p a b", b=8)
    bc = lambda t, col0: t[:, col0:col0 + 32].unsqueeze(2).broadcast_to([128, 32, 8])
    p.add("vector", lambda e: e.tensor_reduce(
        st2[:, 0:32].rearrange("p (a b) -> p a b", b=1), e3(lg), AX.X, ALU.max),
        reads=[lg.name], writes=[st2.name + ":m1"])
    p.add("vector", lambda e: e.tensor_tensor(e3(t1), e3(lg), bc(st2, 0), ALU.is_ge),
          reads=[lg.name, st2.name + ":m1"], writes=[t1.name])
    p.add("vector", lambda e: e.scalar_tensor_tensor(
        e3(t2), e3(t1), NEG_BIG, e3(lg), ALU.mult, ALU.add),
        reads=[t1.name, lg.name], writes=[t2.name])
    p.add("vector", lambda e: e.tensor_reduce(
        st2[:, 32:64].rearrange("p (a b) -> p a b", b=1), e3(t2), AX.X, ALU.max),
        reads=[t2.name], writes=[st2.name + ":m2"])
    p.add("vector", lambda e: e.tensor_tensor(e3(t1), e3(lg), bc(st2, 0), ALU.subtract),
          reads=[lg.name, st2.name + ":m1"], writes=[t1.name])
    p.add("scalar", lambda e: e.activation(ev[:], t1[:], AF.Exp),
          reads=[t1.name], writes=[ev.name])
    p.add("vector", lambda e: e.tensor_sub(st2[:, 64:96], st2[:, 32:64], st2[:, 0:32]),
          reads=[st2.name + ":m2", st2.name + ":m1"], writes=[st2.name + ":dm"])
    p.add("scalar", lambda e: e.activation(st2[:, 64:96], st2[:, 64:96], AF.Exp),
          reads=[st2.name + ":dm"], writes=[st2.name + ":ed"])
    p.add("vector", lambda e: e.tensor_scalar(st2[:, 64:96], st2[:, 64:96], 1.0,
                                              None, ALU.add),
          reads=[st2.name + ":ed"], writes=[st2.name + ":dn"])
    p.add("vector", lambda e: e.reciprocal(st2[:, 64:96], st2[:, 64:96]),
          reads=[st2.name + ":dn"], writes=[st2.name + ":idn"])
    p.add("vector", lambda e: e.tensor_tensor(e3(t2), e3(lg), bc(st2, 32), ALU.is_ge),
          reads=[lg.name, st2.name + ":m2"], writes=[t2.name])
    p.add("vector", lambda e: e.tensor_tensor(e3(t1), e3(ev), e3(t2), ALU.mult),
          reads=[ev.name, t2.name], writes=[t1.name])
    p.add("vector", lambda e: e.tensor_tensor(e3(comb), e3(t1), bc(st2, 64), ALU.mult),
          reads=[t1.name, st2.name + ":idn"], writes=[comb.name])
    p.add("vector", lambda e: e.tensor_copy(combb[:], comb[:]),
          reads=[comb.name], writes=[combb.name])

    p.add("sync", lambda e: e.dma_start(wstage[0][0:8, 0:1024], B2.ap()),
          reads=[B2.name], writes=[wstage[0].name], dma=wstage[0].name)
    p.add("vector", lambda e: e.tensor_copy(b2bf[:], wstage[0][0:8, 0:1024]),
          reads=[wstage[0].name], writes=[b2bf.name])
    p.region_switch(lg.name, obuf[0].name)
    p.region_switch(ev.name, obuf[0].name)
    p.region_switch(t1.name, obuf[0].name)
    p.region_switch(t2.name, obuf[0].name)
    for tt in range(NT):
        p.add("tensor", lambda e, tt=tt: e.transpose(
            pT[0][0:8, 0:128], combb[:, tt * 8:(tt + 1) * 8], ident[:]),
            reads=[combb.name, ident.name], writes=[pT[0].name])
        p.add("vector", lambda e: e.tensor_copy(combT[:], pT[0][0:8, 0:128]),
              reads=[pT[0].name], writes=[combT.name])
        p.add("sync", lambda e, tt=tt: e.dma_start(
            xin[tt % 2][:], Xi.ap()[tt * 128:(tt + 1) * 128, :]),
            reads=[Xi.name], writes=[xin[tt % 2].name], dma=xin[tt % 2].name)
        for hf in range(2):
            p.add("tensor", lambda e, hf=hf: e.matmul(
                pC[:, 0:512] if hf == 0 else pC[:, 0:512], combT[:],
                b2bf[:, hf * 512: hf * 512 + 512], start=True, stop=True),
                reads=[combT.name, b2bf.name], writes=[pC.name])
            p.add("vector", lambda e, tt=tt, hf=hf: e.tensor_add(
                obuf[tt % 2][:, hf * 512: hf * 512 + 512], pC[:, 0:512],
                xin[tt % 2][:, hf * 512: hf * 512 + 512]),
                reads=[pC.name, xin[tt % 2].name], writes=[obuf[tt % 2].name])
        p.add("gpsimd", lambda e, tt=tt: e.dma_start(
            Xo.ap()[tt * 128:(tt + 1) * 128, :], obuf[tt % 2][:]),
            reads=[obuf[tt % 2].name, f"d_Xo:{tt}"], writes=[f"d_Xo:{tt}"],
            dma=f"d_ac{tt % 8}")

    p.region_switch(sq.name, act.name)
    p.region_switch(xn_t2[0].name, act.name)
    p.region_switch(xn_t2[1].name, act.name)
    p.region_switch(xin[0].name, w1stage[2].name)
    p.region_switch(xin[1].name, w1stage[3].name)
    p.region_switch(b2bf.name, ssb[0].name)
    p.region_switch(b2bf.name, ssb[1].name)
    for ei in range(8):
        b1c = b1c2[ei % 2]
        p.add("sync", lambda e, ei=ei, b1c=b1c: e.dma_start(b1c[:], B1c.ap()[ei]),
              reads=[B1c.name], writes=[b1c.name], dma=b1c.name)
        def load_w1(mc, ei=ei):
            sa, sg = w1stage[2 * (mc % 2)], w1stage[2 * (mc % 2) + 1]
            wa, wg1 = w1a[mc % 2], w1g[mc % 2]
            p.add("sync", lambda e, mc=mc, sa=sa, ei=ei: e.dma_start(
                sa[:].rearrange("p (c j) -> p c j", j=128),
                W1.ap()[ei, :, mc * 128:(mc + 1) * 128]
                .rearrange("(c p) j -> p c j", p=128)),
                reads=[W1.name], writes=[sa.name], dma=sa.name)
            p.add("vector", lambda e, sa=sa, wa=wa: e.tensor_copy(wa[:], sa[:]),
                  reads=[sa.name], writes=[wa.name])
            p.add("sync", lambda e, mc=mc, sg=sg, ei=ei: e.dma_start(
                sg[:].rearrange("p (c j) -> p c j", j=128),
                W1.ap()[ei, :, 2048 + mc * 128: 2048 + (mc + 1) * 128]
                .rearrange("(c p) j -> p c j", p=128)),
                reads=[W1.name], writes=[sg.name], dma=sg.name)
            p.add("vector", lambda e, sg=sg, wg1=wg1: e.tensor_copy(wg1[:], sg[:]),
                  reads=[sg.name], writes=[wg1.name])

        load_w1(0)
        def w2_dma(i, ei=ei):
            stg = obuf[i % 2]
            p.add("sync", lambda e, i=i, stg=stg, ei=ei: e.dma_start(
                stg[:, 0:1024], W2.ap()[ei, i * 128:(i + 1) * 128, :]),
                reads=[W2.name], writes=[stg.name], dma=stg.name)
        w2_dma(0)
        w2_dma(1)
        for half in range(2):
            for mc in range(16):
                if D_W1DB:
                    if mc < 15:
                        load_w1(mc + 1)
                    elif half == 0:
                        load_w1(0)
                else:
                    if not (half == 0 and mc == 0):
                        load_w1(mc)
                wa, wg1 = w1a[mc % 2], w1g[mc % 2]
                for tg in range(4):
                    tcol = half * 2048 + tg * 512
                    pAt = q4[D_PAR and tg % 2]
                    pGt = q4[2 + (D_PAR and tg % 2)]
                    sst = ssb[D_PAR and tg % 2]
                    gst = gsb[D_PAR and tg % 2]
                    for fc in range(FC):
                        p.add("tensor", lambda e, fc=fc, tcol=tcol, wa=wa, pAt=pAt: e.matmul(
                            pAt[:, 0:512], wa[:, fc * 128:(fc + 1) * 128],
                            xn_fm[:, fc * T + tcol: fc * T + tcol + 512],
                            start=(fc == 0), stop=(fc == 7)),
                            reads=[wa.name] + [f"{xn_fm.name}:{tcol // 128 + j}" for j in range(4)],
                            writes=[pAt.name])
                    for fc in range(FC):
                        p.add("tensor", lambda e, fc=fc, tcol=tcol, wg1=wg1, pGt=pGt: e.matmul(
                            pGt[:, 0:512], wg1[:, fc * 128:(fc + 1) * 128],
                            xn_fm[:, fc * T + tcol: fc * T + tcol + 512],
                            start=(fc == 0), stop=(fc == 7)),
                            reads=[wg1.name] + [f"{xn_fm.name}:{tcol // 128 + j}" for j in range(4)],
                            writes=[pGt.name])
                    p.add("scalar", lambda e, mc=mc, pAt=pAt, sst=sst, b1c=b1c: e.activation(
                        sst[:], pAt[:, 0:512], AF.Silu, bias=b1c[:, mc:mc + 1]),
                        reads=[pAt.name, b1c.name], writes=[sst.name])
                    p.add("vector", lambda e, mc=mc, pGt=pGt, gst=gst, b1c=b1c: e.tensor_scalar(
                        gst[:], pGt[:, 0:512], b1c[:, 16 + mc:17 + mc], None, ALU.add),
                        reads=[pGt.name, b1c.name], writes=[gst.name])
                    p.add("vector", lambda e, mc=mc, tg=tg, sst=sst, gst=gst: e.tensor_tensor(
                        act[:, mc * 2048 + tg * 512: mc * 2048 + (tg + 1) * 512],
                        sst[:], gst[:], ALU.mult),
                        reads=[sst.name, gst.name], writes=[act.name])
                if half == 0:
                    stg = obuf[mc % 2]
                    p.add("vector", lambda e, mc=mc, stg=stg: e.tensor_copy(
                        w2_bf[:, mc * 1024:(mc + 1) * 1024], stg[:, 0:1024]),
                        reads=[stg.name], writes=[w2_bf.name])
                    if mc + 2 < 16:
                        w2_dma(mc + 2)
            if DEBUG_D and ei == 0 and half == 0:
                dbgd = env["dbgd"]
                p.add("gpsimd", lambda e: e.dma_start(dbgd.ap()[0:128, :], act[:]),
                      reads=[act.name], writes=[dbgd.name], dma="dbgd")
                dbgx = env["dbgx"]
                p.add("gpsimd", lambda e: e.dma_start(dbgx.ap()[0:128, :], xn_fm[:]),
                      reads=[xn_fm.name], writes=[dbgx.name], dma="dbgd")
            for tl in range(16):
                gt = half * 16 + tl
                pOt = pO if tl % 2 == 0 else [q4[0], q4[1]]
                for mc in range(16):
                    for hf in range(2):
                        p.add("tensor", lambda e, mc=mc, hf=hf, tl=tl, pOt=pOt: e.matmul(
                            pOt[hf][:, 0:512],
                            act[:, mc * 2048 + tl * 128: mc * 2048 + (tl + 1) * 128],
                            w2_bf[:, mc * 1024 + hf * 512: mc * 1024 + hf * 512 + 512],
                            start=(mc == 0), stop=(mc == 15)),
                            reads=[act.name, w2_bf.name], writes=[pOt[hf].name])
                ob = obuf[tl % 2]
                for hf in range(2):
                    p.add("vector", lambda e, gt=gt, ei=ei, ob=ob, hf=hf, pOt=pOt: e.tensor_scalar(
                        ob[:, hf * 512: hf * 512 + 512], pOt[hf][:, 0:512],
                        comb[:, gt * 8 + ei: gt * 8 + ei + 1],
                        None, ALU.mult),
                        reads=[pOt[hf].name, comb.name], writes=[ob.name])
                p.add("gpsimd", lambda e, gt=gt, ob=ob: e.dma_start(
                    Xo.ap()[gt * 128:(gt + 1) * 128, :], ob[:], accum_op=ALU.add),
                    reads=[ob.name, f"d_Xo:{gt}"], writes=[f"d_Xo:{gt}"],
                    dma=f"d_ac{gt % 8}")



def make_env(nc, need_mask=True):
    ident_f = nc.dram_tensor("c_ident", [128, 128], F32, kind="ExternalInput")
    ones_f = nc.dram_tensor("c_ones", [1, 128], F32, kind="ExternalInput")
    st_id = nc.alloc_sbuf_tensor("s_identf", [128, 128], F32)
    st_on = nc.alloc_sbuf_tensor("s_onesf", [1, 128], F32)
    s_ident = nc.alloc_sbuf_tensor("s_ident", [128, 128], BF16)
    s_ones = nc.alloc_sbuf_tensor("s_ones", [1, 128], BF16)
    eps_col = nc.alloc_sbuf_tensor("s_eps", [128, 1], F32)
    env = {"nc": nc, "ident_bf": s_ident, "ones_bf": s_ones, "eps_col": eps_col,
           "ident_f": st_id, "ones_f": st_on}
    if need_mask:
        env["mask4"] = nc.alloc_sbuf_tensor("s_mask4", [128, 2048], BF16)
        env["z512"] = nc.alloc_sbuf_tensor("s_z512", [128, 512], BF16)
    env["psum_q"] = [nc.alloc_psum_tensor(f"ps_q{i}", [128, 512], F32) for i in range(4)]
    env["psum_t"] = [nc.alloc_psum_tensor(f"ps_t{i}", [128, 1024], BF16) for i in range(2)]
    env["psum_o"] = [nc.alloc_psum_tensor(f"ps_o{i}", [128, 512], F32) for i in range(2)]
    return env, [(st_id, ident_f, s_ident), (st_on, ones_f, s_ones)]


def load_consts(p, env, pairs):
    for stg, dr, sb in pairs:
        p.add("sync", lambda e, stg=stg, dr=dr: e.dma_start(stg[:], dr.ap()),
              reads=[dr.name], writes=[stg.name], dma=stg.name)
        p.add("vector", lambda e, stg=stg, sb=sb: e.tensor_copy(sb[:], stg[:]),
              reads=[stg.name], writes=[sb.name])
    eps = env["eps_col"]
    p.add("vector", lambda e: e.memset(eps[:], 1e-5), writes=[eps.name])
    if "mask4" in env:
        mask4 = env["mask4"]
        z512 = env["z512"]
        p.add("vector", lambda e: e.memset(z512[:], 0.0), writes=[z512.name])
        for v in range(4):
            p.add("gpsimd", lambda e, v=v: e.affine_select(
                mask4[:, v * 512:(v + 1) * 512], z512[:], [[-1, 512]],
                mybir.AluOpType.is_ge, NEG_BIG, base=v * 128, channel_multiplier=1),
                reads=[z512.name], writes=[mask4.name])


def const_inputs():
    return {
        "c_ident": np.eye(128, dtype=np.float32),
        "c_ones": np.ones((1, 128), dtype=np.float32),
    }


def mk_arena(nc, reserve=1248):
    base = (SBUF_TOP - nc.sbuf_bytes_remaining + 31) & ~31
    size = SBUF_TOP - reserve - base
    nc.alloc_sbuf_tensor("arena", [128, SBUF_TOP - (SBUF_TOP - nc.sbuf_bytes_remaining) - reserve],
                         mybir.dt.uint8)
    return BAlloc(nc, base, size)


def build_prog_fused():
    nc = bass.Bass("TRN2", target_bir_lowering=False)
    Xi = nc.dram_tensor("Xi", [T, D], F32, kind="ExternalInput")
    Xo = nc.dram_tensor("Xo", [T, D], F32, kind="ExternalOutput")
    X1 = nc.dram_tensor("X1", [T, D], F32)
    X2 = nc.dram_tensor("X2", [T, D], F32)
    X3 = nc.dram_tensor("X3", [T, D], F32)
    Xmid = nc.dram_tensor("Xmid", [T, D], F32)
    XnH = nc.dram_tensor("XnH", [T, D], BF16)
    IlS = nc.dram_tensor("IlS", [32, 128], F32)
    aWqT = nc.dram_tensor("aWqT", [D, D], F32, kind="ExternalInput")
    aWkT = nc.dram_tensor("aWkT", [D, D], F32, kind="ExternalInput")
    aWv = nc.dram_tensor("aWv", [D, D], F32, kind="ExternalInput")
    bWr = nc.dram_tensor("bWr", [D, 16], F32, kind="ExternalInput")
    bWgs = nc.dram_tensor("bWgs", [2, D, 512], F32, kind="ExternalInput")
    bWus = nc.dram_tensor("bWus", [2, D, 512], F32, kind="ExternalInput")
    bWds = nc.dram_tensor("bWds", [2, 512, D], F32, kind="ExternalInput")
    bWgr = nc.dram_tensor("bWgr", [16, D, 256], F32, kind="ExternalInput")
    bWur = nc.dram_tensor("bWur", [16, D, 256], F32, kind="ExternalInput")
    bWdr = nc.dram_tensor("bWdr", [16, 256, D], F32, kind="ExternalInput")
    cWqT = nc.dram_tensor("cWqT", [D, D], F32, kind="ExternalInput")
    cWkT = nc.dram_tensor("cWkT", [D, D], F32, kind="ExternalInput")
    cWvT = nc.dram_tensor("cWvT", [D, D], F32, kind="ExternalInput")
    cWpT = nc.dram_tensor("cWpT", [D, D], F32, kind="ExternalInput")
    cBq = nc.dram_tensor("cBq", [1, D], F32, kind="ExternalInput")
    cBk = nc.dram_tensor("cBk", [1, D], F32, kind="ExternalInput")
    cBv = nc.dram_tensor("cBv", [1, D], F32, kind="ExternalInput")
    cBp = nc.dram_tensor("cBp", [1, D], F32, kind="ExternalInput")
    CosT = nc.dram_tensor("CosT", [T, 32], F32, kind="ExternalInput")
    SinT = nc.dram_tensor("SinT", [T, 32], F32, kind="ExternalInput")
    dW1 = nc.dram_tensor("dW1", [8, D, 4096], F32, kind="ExternalInput")
    dB1c = nc.dram_tensor("dB1c", [8, 128, 32], F32, kind="ExternalInput")
    dW2 = nc.dram_tensor("dW2", [8, 2048, D], F32, kind="ExternalInput")
    dB2 = nc.dram_tensor("dB2", [8, D], F32, kind="ExternalInput")
    dWrT = nc.dram_tensor("dWrT", [D, 8], F32, kind="ExternalInput")
    dBr = nc.dram_tensor("dBr", [1, 8], F32, kind="ExternalInput")

    env, pairs = make_env(nc)
    al = mk_arena(nc)
    p = Prog(nc)
    load_consts(p, env, pairs)
    build_block_a(nc, p, al, Xi, X1, aWqT, aWkT, aWv, env)
    p.barrier(); al.reset()
    build_block_b(nc, p, al, X1, X2, bWr, bWgs, bWus, bWds, bWgr, bWur, bWdr, env)
    p.barrier(); al.reset()
    build_block_c(nc, p, al, X2, Xmid, X3, cWqT, cWkT, cWvT, cWpT, cBq, cBk, cBv,
                  cBp, CosT, SinT, XnH, IlS, env)
    p.barrier(); al.reset()
    build_block_d(nc, p, al, X3, Xo, dW1, dB1c, dW2, dB2, dWrT, dBr, env)
    p.barrier()
    p.finalize_and_emit()
    return nc


def build_prog_a():
    nc = bass.Bass("TRN2", target_bir_lowering=False)
    Xi = nc.dram_tensor("Xi", [T, D], F32, kind="ExternalInput")
    Xo = nc.dram_tensor("Xo", [T, D], F32, kind="ExternalOutput")
    WqT = nc.dram_tensor("WqT", [D, D], F32, kind="ExternalInput")
    WkT = nc.dram_tensor("WkT", [D, D], F32, kind="ExternalInput")
    Wv = nc.dram_tensor("Wv", [D, D], F32, kind="ExternalInput")
    env, pairs = make_env(nc)
    al = mk_arena(nc)
    p = Prog(nc)
    load_consts(p, env, pairs)
    build_block_a(nc, p, al, Xi, Xo, WqT, WkT, Wv, env)
    p.barrier()
    p.finalize_and_emit()
    return nc


def build_prog_b():
    nc = bass.Bass("TRN2", target_bir_lowering=False)
    Xi = nc.dram_tensor("Xi", [T, D], F32, kind="ExternalInput")
    Xo = nc.dram_tensor("Xo", [T, D], F32, kind="ExternalOutput")
    Wr = nc.dram_tensor("Wr", [D, 16], F32, kind="ExternalInput")
    Wgs = nc.dram_tensor("Wgs", [2, D, 512], F32, kind="ExternalInput")
    Wus = nc.dram_tensor("Wus", [2, D, 512], F32, kind="ExternalInput")
    Wds = nc.dram_tensor("Wds", [2, 512, D], F32, kind="ExternalInput")
    Wgr = nc.dram_tensor("Wgr", [16, D, 256], F32, kind="ExternalInput")
    Wur = nc.dram_tensor("Wur", [16, D, 256], F32, kind="ExternalInput")
    Wdr = nc.dram_tensor("Wdr", [16, 256, D], F32, kind="ExternalInput")
    env, pairs = make_env(nc, need_mask=False)
    al = mk_arena(nc)
    p = Prog(nc)
    load_consts(p, env, pairs)
    build_block_b(nc, p, al, Xi, Xo, Wr, Wgs, Wus, Wds, Wgr, Wur, Wdr, env)
    p.barrier()
    p.finalize_and_emit()
    return nc


def build_prog_c():
    nc = bass.Bass("TRN2", target_bir_lowering=False)
    Xi = nc.dram_tensor("Xi", [T, D], F32, kind="ExternalInput")
    Xo = nc.dram_tensor("Xo", [T, D], F32, kind="ExternalOutput")
    Xmid = nc.dram_tensor("Xmid", [T, D], F32,
                          kind=("ExternalOutput" if DEBUG_C else "Internal"))
    XnH = nc.dram_tensor("XnH", [T, D], BF16)
    IlS = nc.dram_tensor("IlS", [32, 128], F32)
    WqT = nc.dram_tensor("WqT", [D, D], F32, kind="ExternalInput")
    WkT = nc.dram_tensor("WkT", [D, D], F32, kind="ExternalInput")
    WvT = nc.dram_tensor("WvT", [D, D], F32, kind="ExternalInput")
    WpT = nc.dram_tensor("WpT", [D, D], F32, kind="ExternalInput")
    Bq = nc.dram_tensor("Bq", [1, D], F32, kind="ExternalInput")
    Bk = nc.dram_tensor("Bk", [1, D], F32, kind="ExternalInput")
    Bv = nc.dram_tensor("Bv", [1, D], F32, kind="ExternalInput")
    Bp = nc.dram_tensor("Bp", [1, D], F32, kind="ExternalInput")
    CosT = nc.dram_tensor("CosT", [T, 32], F32, kind="ExternalInput")
    SinT = nc.dram_tensor("SinT", [T, 32], F32, kind="ExternalInput")
    env, pairs = make_env(nc)
    if DEBUG_C:
        env["dbg"] = nc.dram_tensor("Dbg", [1024, 2048], F32, kind="ExternalOutput")
    al = mk_arena(nc)
    p = Prog(nc)
    load_consts(p, env, pairs)
    build_block_c(nc, p, al, Xi, Xmid, Xo, WqT, WkT, WvT, WpT, Bq, Bk, Bv, Bp,
                  CosT, SinT, XnH, IlS, env)
    p.barrier()
    p.finalize_and_emit()
    return nc


def build_prog_d():
    nc = bass.Bass("TRN2", target_bir_lowering=False)
    Xi = nc.dram_tensor("Xi", [T, D], F32, kind="ExternalInput")
    Xo = nc.dram_tensor("Xo", [T, D], F32, kind="ExternalOutput")
    W1 = nc.dram_tensor("W1", [8, D, 4096], F32, kind="ExternalInput")
    B1c = nc.dram_tensor("B1c", [8, 128, 32], F32, kind="ExternalInput")
    W2 = nc.dram_tensor("W2", [8, 2048, D], F32, kind="ExternalInput")
    B2 = nc.dram_tensor("B2", [8, D], F32, kind="ExternalInput")
    WrT = nc.dram_tensor("WrT", [D, 8], F32, kind="ExternalInput")
    Br = nc.dram_tensor("Br", [1, 8], F32, kind="ExternalInput")
    env, pairs = make_env(nc, need_mask=False)
    if DEBUG_D:
        env["dbgd"] = nc.dram_tensor("DbgD", [128, 32768], mybir.dt.bfloat16,
                                     kind="ExternalOutput")
        env["dbgx"] = nc.dram_tensor("DbgX", [128, 32768], mybir.dt.bfloat16,
                                     kind="ExternalOutput")
    al = mk_arena(nc)
    p = Prog(nc)
    load_consts(p, env, pairs)
    build_block_d(nc, p, al, Xi, Xo, W1, B1c, W2, B2, WrT, Br, env)
    p.barrier()
    p.finalize_and_emit()
    return nc


def rope_tables():
    half = 32
    i = np.arange(half, dtype=np.float64)
    alpha = 1.0 / 10000.0 ** (2.0 * i / 64.0)
    ang = np.arange(T, dtype=np.float64)[:, None] * alpha[None, :]
    return (np.cos(ang).astype(np.float32), np.sin(ang).astype(np.float32))


_PROG_CACHE = {}


def _prog(name, builder):
    if name not in _PROG_CACHE:
        _PROG_CACHE[name] = builder()
    return _PROG_CACHE[name]


def _run(nc, per_core, shared):
    in_maps = []
    for b in range(8):
        m = dict(shared)
        m.update({k: v[b] for k, v in per_core.items()})
        in_maps.append(m)
    res = run_bass_kernel_spmd(nc, in_maps, core_ids=list(range(8)))
    return np.stack([np.asarray(r["Xo"], dtype=np.float32) for r in res.results])


def kernel(**inputs):
    f32c = lambda a: np.ascontiguousarray(np.asarray(a), dtype=np.float32)
    x = f32c(inputs["L0_attn_x"])
    cos_t, sin_t = rope_tables()
    b1 = f32c(inputs["L1_ffn_b1"])
    b1c = np.concatenate(
        [b1[:, :2048].reshape(8, 16, 128).transpose(0, 2, 1),
         b1[:, 2048:].reshape(8, 16, 128).transpose(0, 2, 1)], axis=2)
    sh = dict(const_inputs())
    sh["aWqT"] = f32c(inputs["L0_attn_w_q"].T)
    sh["aWkT"] = f32c(inputs["L0_attn_w_k"].T)
    sh["aWv"] = f32c(inputs["L0_attn_w_v"])
    sh["bWr"] = f32c(inputs["L0_ffn_router_w"])
    sh["bWgs"] = f32c(inputs["L0_ffn_wg_s"])
    sh["bWus"] = f32c(inputs["L0_ffn_wu_s"])
    sh["bWds"] = f32c(inputs["L0_ffn_wd_s"])
    sh["bWgr"] = f32c(inputs["L0_ffn_wg_r"])
    sh["bWur"] = f32c(inputs["L0_ffn_wu_r"])
    sh["bWdr"] = f32c(inputs["L0_ffn_wd_r"])
    sh["cWqT"] = f32c(inputs["L1_attn_wq"].T)
    sh["cWkT"] = f32c(inputs["L1_attn_wk"].T)
    sh["cWvT"] = f32c(inputs["L1_attn_wv"].T)
    sh["cWpT"] = f32c(inputs["L1_attn_wp"].T)
    sh["cBq"] = f32c(inputs["L1_attn_bq"]).reshape(1, D)
    sh["cBk"] = f32c(inputs["L1_attn_bk"]).reshape(1, D)
    sh["cBv"] = f32c(inputs["L1_attn_bv"]).reshape(1, D)
    sh["cBp"] = f32c(inputs["L1_attn_bp"]).reshape(1, D)
    sh["CosT"] = cos_t
    sh["SinT"] = sin_t
    sh["dW1"] = f32c(inputs["L1_ffn_W1"])
    sh["dB1c"] = np.ascontiguousarray(b1c)
    sh["dW2"] = f32c(inputs["L1_ffn_W2"])
    sh["dB2"] = f32c(inputs["L1_ffn_b2"])
    sh["dWrT"] = f32c(inputs["L1_ffn_Wr"])
    sh["dBr"] = f32c(inputs["L1_ffn_br"]).reshape(1, 8)
    return _run(_prog("fused", build_prog_fused), {"Xi": x}, sh)


# version 25
# speedup vs baseline: 1.1445x; 1.0089x over previous
import numpy as np
import ml_dtypes
import concourse.bass as bass
import concourse.mybir as mybir
from concourse.bass_utils import run_bass_kernel_spmd

F32 = mybir.dt.float32
BF16 = mybir.dt.bfloat16
AF = mybir.ActivationFunctionType
ALU = mybir.AluOpType
AX = mybir.AxisListType

T = 4096
D = 1024
NT = T // 128
FC = D // 128
GAMMA0 = 1.0 / 32.0
DEBUG_C = False
DEBUG_G = 0
D_PAR = True
NDIR = 5
D_W1DB = True
DEBUG_D = False
NEG_BIG = -1.0e30

ENGINES = ("sync", "tensor", "scalar", "vector", "gpsimd")


class Prog:

    def __init__(self, nc):
        self.nc = nc
        self.ops = []
        self.lw = {}
        self.rd = {}
        self.gcount = {}
        self.gsem = {}
        self.esem = {}
        self.epoch = 0
        self._new_epoch()

    def _new_epoch(self):
        self.epoch += 1
        for e in ENGINES:
            self.esem[e] = self.nc.alloc_semaphore(f"E{e[:2]}{self.epoch}")

    def _gsem(self, g):
        if g not in self.gsem:
            self.gsem[g] = self.nc.alloc_semaphore(f"G{len(self.gsem)}")
        return self.gsem[g]

    def add(self, engine, emit, reads=(), writes=(), dma=None):
        deps = set()
        gwait = {}

        def dep_on(j):
            dj = self.ops[j]
            if dj["dma"] is not None:
                g = dj["dma"]
                gwait[g] = max(gwait.get(g, 0), self.gcount[g])
            else:
                deps.add(j)

        for k in reads:
            for j in self.lw.get(k, {}).values():
                dep_on(j)
        for k in writes:
            for j in self.lw.get(k, {}).values():
                dj = self.ops[j]
                if dma is not None and dj["dma"] == dma:
                    continue
                dep_on(j)
            for j in self.rd.get(k, ()):
                dep_on(j)
        i = len(self.ops)
        if dma is not None:
            self.gcount[dma] = self.gcount.get(dma, 0) + 1
            self._gsem(dma)
        op = {"engine": engine, "emit": emit, "deps": deps, "dma": dma,
              "esem": self.esem[engine], "inc": False, "msv": None,
              "gwait": gwait}
        self.ops.append(op)
        for k in reads:
            self.rd.setdefault(k, []).append(i)
        for k in writes:
            tag = "dma" if dma is not None else engine
            self.lw.setdefault(k, {})[tag] = i
            self.rd[k] = []
        return i

    def region_switch(self, old_key, new_key):
        lw = self.lw.setdefault(new_key, {})
        for tag, j in self.lw.get(old_key, {}).items():
            if tag not in lw or lw[tag] < j:
                lw[tag] = j
        self.rd.setdefault(new_key, []).extend(self.rd.get(old_key, ()))

    def barrier(self):
        last = {}
        for i, op in enumerate(self.ops):
            if op["dma"] is None and op["emit"] is not None:
                last[op["engine"]] = i
        groups = dict(self.gcount)
        for e in ENGINES:
            deps = {i for en, i in last.items() if en != e}
            self.ops.append({"engine": e, "emit": None, "deps": deps,
                             "dma": None, "esem": self.esem[e], "inc": False,
                             "msv": None, "gwait": dict(groups)})
        self.lw = {}
        self.rd = {}
        self._new_epoch()

    def finalize_and_emit(self):
        for op in self.ops:
            for j in op["deps"]:
                dj = self.ops[j]
                if dj["engine"] != op["engine"]:
                    dj["inc"] = True
        counters = {}
        for op in self.ops:
            if op["inc"]:
                s = op["esem"]
                counters[s] = counters.get(s, 0) + 1
                op["msv"] = counters[s]
        for g, c in self.gcount.items():
            assert 16 * c < 60000, f"dma group {g} count {c} too high"
        for s, c in counters.items():
            assert c < 60000, "milestone sem overflow"
        ops = self.ops
        gsem = self.gsem

        eseq = {}
        ecount = {e: 0 for e in ENGINES}
        for i, op in enumerate(ops):
            if op["emit"] is not None:
                ecount[op["engine"]] += 1
                eseq[i] = ecount[op["engine"]]

        with self.nc.Block() as blk:
            def mk_body(ename):
                def body(eng):
                    water = {}
                    drained_upto = 0
                    emitted = 0
                    for op in ops:
                        if op["engine"] != ename:
                            continue
                        need_drain = any(
                            ops[j]["engine"] == ename and ops[j]["emit"] is not None
                            and eseq[j] > drained_upto
                            for j in op["deps"])
                        if need_drain and ename != "tensor":
                            eng.drain()
                            drained_upto = emitted
                        for j in sorted(op["deps"]):
                            dj = ops[j]
                            if dj["engine"] == ename:
                                continue
                            if not dj["inc"]:
                                continue
                            sem, val = dj["esem"], dj["msv"]
                            if water.get(sem.num, 0) < val:
                                eng.wait_ge(sem, val)
                                water[sem.num] = val
                        for g, cnt in sorted(op["gwait"].items()):
                            sem, val = gsem[g], 16 * cnt
                            if water.get(sem.num, 0) < val:
                                eng.wait_ge(sem, val)
                                water[sem.num] = val
                        if op["emit"] is None:
                            continue
                        inst = op["emit"](eng)
                        emitted += 1
                        if op["dma"] is not None:
                            inst.then_inc(gsem[op["dma"]], 16)
                        elif op["inc"]:
                            inst.then_inc(op["esem"], 1)
                return body
            for ename in ENGINES:
                getattr(blk, ename)(mk_body(ename))



def emit_ln_tile(p, env, x_hbm, tt, xin, xn_t, dst_writes, emit_to):
    nc = env["nc"]
    kx = xin.name
    p.add("sync", lambda e, a=x_hbm, b=xin: e.dma_start(
        b[:], a.ap()[tt * 128:(tt + 1) * 128, :]), reads=[x_hbm.name], writes=[kx], dma=kx)
    sq, st = env["sq"], env["stats"]
    ks = st.name
    p.add("vector", lambda e: e.tensor_reduce(st[:, 0:1], xin[:], AX.X, ALU.add),
          reads=[kx], writes=[ks + ":sum"])
    p.add("scalar", lambda e: e.activation(sq[:], xin[:], AF.Square),
          reads=[kx], writes=[sq.name])
    p.add("vector", lambda e: e.tensor_reduce(st[:, 1:2], sq[:], AX.X, ALU.add),
          reads=[sq.name], writes=[ks + ":sq"])
    p.add("vector", lambda e: e.tensor_scalar(st[:, 2:3], st[:, 0:1], 1.0 / D, None, ALU.mult),
          reads=[ks + ":sum"], writes=[ks + ":m"])
    p.add("vector", lambda e: e.tensor_scalar(st[:, 3:4], st[:, 1:2], 1.0 / D, None, ALU.mult),
          reads=[ks + ":sq"], writes=[ks + ":e"])
    p.add("vector", lambda e: e.tensor_tensor(st[:, 4:5], st[:, 2:3], st[:, 2:3], ALU.mult),
          reads=[ks + ":m"], writes=[ks + ":m2"])
    p.add("vector", lambda e: e.tensor_sub(st[:, 5:6], st[:, 3:4], st[:, 4:5]),
          reads=[ks + ":e", ks + ":m2"], writes=[ks + ":v"])
    eps = env["eps_col"]
    p.add("scalar", lambda e: e.activation(st[:, 6:7], st[:, 5:6], AF.Sqrt, bias=eps[:, 0:1]),
          reads=[ks + ":v", eps.name], writes=[ks + ":sd"])
    p.add("vector", lambda e: e.reciprocal(st[:, 7:8], st[:, 6:7]),
          reads=[ks + ":sd"], writes=[ks + ":r"])
    p.add("vector", lambda e: e.tensor_scalar(xn_t[:], xin[:], st[:, 2:3], st[:, 7:8],
                                              ALU.subtract, ALU.mult),
          reads=[kx, ks + ":m", ks + ":r"], writes=[xn_t.name])
    emit_to(p)


def emit_transpose_to(p, env, src_ap, src_key, dst_ap, dst_key, pt, evac_engine="vector"):
    ident = env["ident_bf"]
    p.add("tensor", lambda e: e.transpose(pt[:, 0:128], src_ap, ident[:]),
          reads=[src_key, ident.name], writes=[pt.name])
    p.add(evac_engine, lambda e: (e.tensor_copy(dst_ap, pt[:, 0:128])
                                  if evac_engine != "scalar" else e.copy(dst_ap, pt[:, 0:128])),
          reads=[pt.name], writes=[dst_key])


def emit_load_cast_w(p, env, w_hbm, rows0, rows1, cols0, cols1, dst, dst_off,
                     stage_pair, tag):
    W = cols1 - cols0
    n = (rows1 - rows0) // 128
    for i in range(n):
        stg = stage_pair[i % 2]
        p.add("sync", lambda e, i=i, stg=stg: e.dma_start(
            stg[:, 0:W], w_hbm.ap()[rows0 + i * 128: rows0 + (i + 1) * 128, cols0:cols1]),
            reads=[w_hbm.name], writes=[stg.name], dma=stg.name)
        p.add("vector", lambda e, i=i, stg=stg: e.tensor_copy(
            dst[:, dst_off + i * W: dst_off + (i + 1) * W], stg[:, 0:W]),
            reads=[stg.name], writes=[dst.name + tag])


def emit_hilo_row(p, env, src_hbm, n, row_f, row_hi, row_lo, row_tmp):
    p.add("sync", lambda e: e.dma_start(row_f[:, 0:n], src_hbm.ap()[0:1, 0:n]),
          reads=[src_hbm.name], writes=[row_f.name], dma=row_f.name)
    p.add("vector", lambda e: e.tensor_copy(row_hi[:, 0:n], row_f[:, 0:n]),
          reads=[row_f.name], writes=[row_hi.name])
    p.add("vector", lambda e: e.tensor_copy(row_tmp[:, 0:n], row_hi[:, 0:n]),
          reads=[row_hi.name], writes=[row_tmp.name])
    p.add("vector", lambda e: e.tensor_sub(row_tmp[:, 0:n], row_f[:, 0:n], row_tmp[:, 0:n]),
          reads=[row_f.name, row_tmp.name], writes=[row_tmp.name])
    p.add("vector", lambda e: e.tensor_copy(row_lo[:, 0:n], row_tmp[:, 0:n]),
          reads=[row_tmp.name], writes=[row_lo.name])



SBUF_TOP = 229344

_ESIZE = {F32: 4, BF16: 2}


class BAlloc:

    def __init__(self, nc, base, size):
        self.nc, self.base, self.size = nc, base, size
        self.off = 0

    def alloc_sbuf_tensor(self, name, shape, dtype):
        nb = (shape[-1] * _ESIZE[dtype] + 31) & ~31
        t = self.nc.alloc_sbuf_tensor_at(name, shape, dtype, offset=self.base + self.off)
        self.off += nb
        assert self.off <= self.size, f"arena overflow at {name}: {self.off}/{self.size}"
        return t

    def alloc_sbuf_tensor_at(self, name, shape, dtype, offset):
        return self.nc.alloc_sbuf_tensor_at(name, shape, dtype, offset=offset)

    def suballoc(self, nbytes):
        off = self.base + self.off
        self.off += (nbytes + 31) & ~31
        assert self.off <= self.size
        return off

    def reset(self):
        self.off = 0


def sbuf_arena(al, name, nbytes):
    return al.suballoc(nbytes)


def build_block_a(nc, p, al, Xi, Xo, WqT, WkT, Wv, env):
    sq = al.alloc_sbuf_tensor("a_sq", [128, 1024], F32)
    stats = al.alloc_sbuf_tensor("a_st", [128, 8], F32)
    xn_t2 = [al.alloc_sbuf_tensor(f"a_xnt{i}", [128, 1024], BF16) for i in range(2)]
    xn_fm = al.alloc_sbuf_tensor("a_xnfm", [128, FC * T], BF16)
    P_bf = al.alloc_sbuf_tensor("a_P", [128, 8192], BF16)
    V_tm = al.alloc_sbuf_tensor("a_V", [128, NT * 1024], BF16)
    hh_bf2 = [al.alloc_sbuf_tensor(f"a_hh{i}", [128, 1024], BF16) for i in range(2)]
    at_sb = [al.alloc_sbuf_tensor(f"a_at{i}", [128, 128], BF16) for i in range(2)]
    sm = al.alloc_sbuf_tensor("a_sm", [128, 8], F32)
    r1 = sbuf_arena(al, "a_reg1", 16384)
    r2 = sbuf_arena(al, "a_reg2", 16384)
    r3 = sbuf_arena(al, "a_reg3", 8192)
    wq_bf = al.alloc_sbuf_tensor_at("a_wq", [128, 8192], BF16, offset=r1)
    wv_bf = al.alloc_sbuf_tensor_at("a_wv", [128, 8192], BF16, offset=r1)
    s_row = al.alloc_sbuf_tensor_at("a_srow", [128, T], F32, offset=r1)
    wk_bf = al.alloc_sbuf_tensor_at("a_wk", [128, 8192], BF16, offset=r2)
    a_bf = al.alloc_sbuf_tensor_at("a_abf", [128, T], BF16, offset=r2)
    xnew = [al.alloc_sbuf_tensor_at(f"a_xn{i}", [128, 1024], F32, offset=r2 + 8192 + 4096 * i)
            for i in range(2)]
    wstage = [al.alloc_sbuf_tensor_at(f"a_wst{i}", [128, 1024], F32, offset=r3 + 4096 * i)
              for i in range(2)]
    xold = [al.alloc_sbuf_tensor_at(f"a_xo{i}", [128, 1024], F32, offset=r3 + 4096 * i)
            for i in range(2)]
    xin = [al.alloc_sbuf_tensor_at(f"a_xin{i}", [128, 1024], F32, offset=r3 + 4096 * i)
           for i in range(2)]

    q4 = env["psum_q"]
    pA, pB, pS = q4[0], q4[1], [q4[2], q4[3]]
    pT = env["psum_t"]
    pY = env["psum_o"]

    env = dict(env)
    env["sq"] = sq
    env["stats"] = stats

    for tt in range(NT):
        xn_t = xn_t2[0]
        def consume(p, tt=tt, xn_t=xn_t):
            for c in range(FC):
                emit_transpose_to(
                    p, env, xn_t[:, c * 128:(c + 1) * 128], xn_t.name,
                    xn_fm[:, c * T + tt * 128: c * T + (tt + 1) * 128],
                    f"{xn_fm.name}:{tt}",
                    pT[c % 2], evac_engine=("vector" if c % 2 == 0 else "scalar"))
        emit_ln_tile(p, env, Xi, tt, xin[tt % 2], xn_t, None, consume)

    p.region_switch(xin[0].name, wstage[0].name)
    p.region_switch(xin[1].name, wstage[1].name)
    emit_load_cast_w(p, env, WqT, 0, 1024, 0, 1024, wq_bf, 0, wstage, "")
    emit_load_cast_w(p, env, WkT, 0, 1024, 0, 1024, wk_bf, 0, wstage, "")
    for mq in range(8):
        for half in range(2):
            ps = pA if half == 0 else pB
            for dc in range(8):
                p.add("tensor", lambda e, mq=mq, half=half, dc=dc, ps=ps: e.matmul(
                    ps[:, 0:512],
                    wq_bf[:, dc * 1024 + mq * 128: dc * 1024 + (mq + 1) * 128],
                    wk_bf[:, dc * 1024 + half * 512: dc * 1024 + half * 512 + 512],
                    start=(dc == 0), stop=(dc == 7)),
                    reads=[wq_bf.name, wk_bf.name], writes=[ps.name])
            p.add("scalar", lambda e, mq=mq, half=half, ps=ps: e.copy(
                P_bf[:, mq * 1024 + half * 512: mq * 1024 + half * 512 + 512], ps[:, 0:512]),
                reads=[ps.name], writes=[P_bf.name])

    p.region_switch(wq_bf.name, wv_bf.name)
    emit_load_cast_w(p, env, Wv, 0, 1024, 0, 1024, wv_bf, 0, wstage, "")
    for tt in range(NT):
        for half in range(2):
            ps = pA if half == 0 else pB
            for fc in range(FC):
                p.add("tensor", lambda e, tt=tt, half=half, fc=fc, ps=ps: e.matmul(
                    ps[:, 0:512],
                    xn_fm[:, fc * T + tt * 128: fc * T + (tt + 1) * 128],
                    wv_bf[:, fc * 1024 + half * 512: fc * 1024 + half * 512 + 512],
                    start=(fc == 0), stop=(fc == 7)),
                    reads=[f"{xn_fm.name}:{tt}", wv_bf.name], writes=[ps.name])
            p.add("scalar", lambda e, tt=tt, half=half, ps=ps: e.copy(
                V_tm[:, tt * 1024 + half * 512: tt * 1024 + half * 512 + 512], ps[:, 0:512]),
                reads=[ps.name], writes=[f"{V_tm.name}:{tt}"])

    p.region_switch(wv_bf.name, s_row.name)
    p.region_switch(wk_bf.name, a_bf.name)
    p.region_switch(wk_bf.name, xnew[0].name)
    p.region_switch(wk_bf.name, xnew[1].name)
    p.region_switch(wstage[0].name, xold[0].name)
    p.region_switch(wstage[1].name, xold[1].name)
    def emit_hh(qt):
        hb = hh_bf2[qt % 2]
        for m in range(8):
            ps = pA if m < 4 else pB
            col = (m % 4) * 128
            for c in range(8):
                p.add("tensor", lambda e, m=m, c=c, ps=ps, col=col, qt=qt: e.matmul(
                    ps[:, col:col + 128],
                    P_bf[:, c * 1024 + m * 128: c * 1024 + (m + 1) * 128],
                    xn_fm[:, c * T + qt * 128: c * T + (qt + 1) * 128],
                    start=(c == 0), stop=(c == 7)),
                    reads=[P_bf.name, f"{xn_fm.name}:{qt}"], writes=[ps.name])
        p.add("vector", lambda e, hb=hb: e.tensor_copy(hb[:, 0:512], pA[:, 0:512]),
              reads=[pA.name], writes=[hb.name])
        p.add("vector", lambda e, hb=hb: e.tensor_copy(hb[:, 512:1024], pB[:, 0:512]),
              reads=[pB.name], writes=[hb.name])

    for qt in range(NT):
        kxo = xold[qt % 2].name
        p.add("gpsimd", lambda e, qt=qt, b=xold[qt % 2]: e.dma_start(
            b[:], Xi.ap()[qt * 128:(qt + 1) * 128, :]), reads=[Xi.name], writes=[kxo], dma=kxo)
        if qt == 0:
            emit_hh(0)
        hh_bf = hh_bf2[qt % 2]
        ng = (qt + 4) // 4
        for g in range(ng):
            ps = pS[g % 2]
            gk = [f"{xn_fm.name}:{g * 4 + j}" for j in range(4)]
            for m in range(8):
                p.add("tensor", lambda e, g=g, m=m, ps=ps, hh_bf=hh_bf: e.matmul(
                    ps[:, 0:512],
                    hh_bf[:, m * 128:(m + 1) * 128],
                    xn_fm[:, m * T + g * 512: m * T + g * 512 + 512],
                    start=(m == 0), stop=(m == 7)),
                    reads=[hh_bf.name] + gk, writes=[ps.name])
            if g == ng - 1:
                mask4 = env["mask4"]
                v = qt % 4
                p.add("vector", lambda e, ps=ps, ng=ng, v=v: e.tensor_tensor(
                    s_row[:, (ng - 1) * 512: ng * 512], ps[:, 0:512],
                    mask4[:, v * 512:(v + 1) * 512], ALU.add),
                    reads=[ps.name, mask4.name], writes=[s_row.name])
            else:
                eng = "vector" if g % 2 == 0 else "scalar"
                p.add(eng, lambda e, g=g, ps=ps, eng=eng: (
                    e.tensor_copy(s_row[:, g * 512:(g + 1) * 512], ps[:, 0:512])
                    if eng == "vector" else e.copy(s_row[:, g * 512:(g + 1) * 512], ps[:, 0:512])),
                    reads=[ps.name], writes=[s_row.name])
        L = ng * 512
        p.add("vector", lambda e, L=L: e.tensor_reduce(
            sm[:, 0:1], s_row[:, 0:L], AX.X, ALU.max, negate=True),
            reads=[s_row.name], writes=[sm.name + ":nm"])
        p.add("vector", lambda e: e.tensor_scalar(sm[:, 1:2], sm[:, 0:1], GAMMA0, None, ALU.mult),
              reads=[sm.name + ":nm"], writes=[sm.name + ":bm"])
        if qt + 1 < NT:
            emit_hh(qt + 1)
        p.add("scalar", lambda e, L=L: e.activation(
            a_bf[:, 0:L], s_row[:, 0:L], AF.Exp, bias=sm[:, 1:2], scale=GAMMA0,
            accum_out=sm[:, 2:3]),
            reads=[s_row.name, sm.name + ":bm"], writes=[a_bf.name, sm.name + ":l"])
        p.add("vector", lambda e: e.reciprocal(sm[:, 3:4], sm[:, 2:3]),
              reads=[sm.name + ":l"], writes=[sm.name + ":il"])
        def emit_at(kb):
            pt = pT[kb % 2]
            ab = at_sb[kb % 2]
            p.add("tensor", lambda e, kb=kb, pt=pt: e.transpose(
                pt[:, 0:128], a_bf[:, kb * 128:(kb + 1) * 128], env["ident_bf"][:]),
                reads=[a_bf.name], writes=[pt.name])
            eng = "vector" if kb % 2 == 0 else "scalar"
            p.add(eng, lambda e, pt=pt, ab=ab, eng=eng: (
                e.tensor_copy(ab[:], pt[:, 0:128]) if eng == "vector"
                else e.copy(ab[:], pt[:, 0:128])),
                reads=[pt.name], writes=[ab.name])
        PIPELINE_Y_A = True
        if PIPELINE_Y_A:
            emit_at(0)
        for kb in range(qt + 1):
            if PIPELINE_Y_A:
                if kb + 1 <= qt:
                    emit_at(kb + 1)
            else:
                emit_at(kb)
            ab = at_sb[kb % 2]
            for half in range(2):
                p.add("tensor", lambda e, kb=kb, half=half, ab=ab, qt=qt: e.matmul(
                    pY[half][:, 0:512], ab[:],
                    V_tm[:, kb * 1024 + half * 512: kb * 1024 + half * 512 + 512],
                    start=(kb == 0), stop=(kb == qt)),
                    reads=[ab.name, f"{V_tm.name}:{kb}"], writes=[pY[half].name])
        kxn = xnew[qt % 2].name
        for half in range(2):
            p.add("vector", lambda e, qt=qt, half=half: e.scalar_tensor_tensor(
                xnew[qt % 2][:, half * 512:half * 512 + 512], pY[half][:, 0:512],
                sm[:, 3:4], xold[qt % 2][:, half * 512:half * 512 + 512],
                ALU.mult, ALU.add),
                reads=[pY[half].name, sm.name + ":il", kxo], writes=[kxn])
        p.add("gpsimd", lambda e, qt=qt: e.dma_start(
            Xo.ap()[qt * 128:(qt + 1) * 128, :], xnew[qt % 2][:]),
            reads=[kxn], writes=[Xo.name], dma="store_" + Xo.name)



def emit_load_cast(p, env, mk_ap, hbm_key, n, W, dst, dst_off, stage_pair, tag=""):
    for i in range(n):
        stg = stage_pair[i % len(stage_pair)]
        ap_i = mk_ap(i)
        p.add("sync", lambda e, ap_i=ap_i, stg=stg: e.dma_start(stg[:, 0:W], ap_i),
              reads=[hbm_key], writes=[stg.name], dma=stg.name)
        p.add("vector", lambda e, i=i, stg=stg: e.tensor_copy(
            dst[:, dst_off + i * W: dst_off + (i + 1) * W], stg[:, 0:W]),
            reads=[stg.name], writes=[dst.name + tag])


def build_block_b(nc, p, al, Xi, Xo, Wr, Wgs, Wus, Wds, Wgr, Wur, Wdr, env):
    xin = [al.alloc_sbuf_tensor(f"b_xin{i}", [128, 1024], F32) for i in range(2)]
    sq = al.alloc_sbuf_tensor("b_sq", [128, 1024], F32)
    stats = al.alloc_sbuf_tensor("b_st", [128, 8], F32)
    xn_t2 = [al.alloc_sbuf_tensor(f"b_xnt{i}", [128, 1024], BF16) for i in range(2)]
    xn_fm = al.alloc_sbuf_tensor("b_xnfm", [128, FC * 2048], BF16)
    acc = al.alloc_sbuf_tensor("b_acc", [128, 16 * 1024], F32)
    wstage = [al.alloc_sbuf_tensor(f"b_wst{i}", [128, 1024], F32) for i in range(2)]
    wg2 = [al.alloc_sbuf_tensor(f"b_wg{i}", [128, 4096], BF16) for i in range(2)]
    wu2 = [al.alloc_sbuf_tensor(f"b_wu{i}", [128, 4096], BF16) for i in range(2)]
    wd2 = [al.alloc_sbuf_tensor(f"b_wd{i}", [128, 4096], BF16) for i in range(2)]
    rw = al.alloc_sbuf_tensor("b_rw", [128, 128], BF16)
    hsb2 = [al.alloc_sbuf_tensor(f"b_h{i}", [128, 2048], BF16) for i in range(2)]
    ssb2 = [al.alloc_sbuf_tensor(f"b_s{i}", [128, 512], BF16) for i in range(2)]
    usb2 = [al.alloc_sbuf_tensor(f"b_u{i}", [128, 512], BF16) for i in range(2)]
    eb = al.alloc_sbuf_tensor("b_e", [128, 256], F32)
    pn = al.alloc_sbuf_tensor("b_pn", [128, 256], F32)
    comb = al.alloc_sbuf_tensor("b_comb", [128, 256], F32)
    t1 = al.alloc_sbuf_tensor("b_t1", [128, 256], F32)
    t2 = al.alloc_sbuf_tensor("b_t2", [128, 256], F32)
    st2 = al.alloc_sbuf_tensor("b_st2", [128, 112], F32)

    q4 = env["psum_q"]
    pR, pG, pU = q4[0], q4[1], q4[2]
    pO = env["psum_o"]
    pT = env["psum_t"]

    env = dict(env)
    env["sq"] = sq
    env["stats"] = stats

    emit_load_cast(p, env, lambda i: Wr.ap()[i * 128:(i + 1) * 128, :],
                   Wr.name, 8, 16, rw, 0, wstage)

    for h in range(2):
        tb = h * 16
        for tl in range(16):
            tt = tb + tl
            xn_t = xn_t2[tl % 2]
            def consume(p, tl=tl, xn_t=xn_t):
                for c in range(FC):
                    emit_transpose_to(
                        p, env, xn_t[:, c * 128:(c + 1) * 128], xn_t.name,
                        xn_fm[:, c * 2048 + tl * 128: c * 2048 + (tl + 1) * 128],
                        f"{xn_fm.name}:{tl}", pT[c % 2],
                        evac_engine=("vector" if c % 2 == 0 else "scalar"))
            emit_ln_tile(p, env, Xi, tt, xin[tl % 2], xn_t, None, consume)
            p.add("gpsimd", lambda e, tt=tt, tl=tl: e.dma_start(
                acc[:, tl * 1024:(tl + 1) * 1024], Xi.ap()[tt * 128:(tt + 1) * 128, :]),
                reads=[Xi.name], writes=[f"b_acc:{tl}"], dma=f"b_accld{tl % 4}")
        for tl in range(16):
            for fc in range(FC):
                p.add("tensor", lambda e, tl=tl, fc=fc: e.matmul(
                    pR[:, tl * 16:(tl + 1) * 16],
                    xn_fm[:, fc * 2048 + tl * 128: fc * 2048 + (tl + 1) * 128],
                    rw[:, fc * 16:(fc + 1) * 16],
                    start=(fc == 0), stop=(fc == 7)),
                    reads=[f"{xn_fm.name}:{tl}", rw.name], writes=[pR.name])
        p.add("scalar", lambda e: e.activation(eb[:], pR[:, 0:256], AF.Exp),
              reads=[pR.name], writes=[eb.name])
        e3 = lambda t: t[:].rearrange("p (a b) -> p a b", b=16)
        bc = lambda t, c0: t[:, c0:c0 + 16].unsqueeze(2).broadcast_to([128, 16, 16])
        p.add("vector", lambda e: e.tensor_reduce(
            st2[:, 0:16].rearrange("p (a b) -> p a b", b=1), e3(eb), AX.X, ALU.add),
            reads=[eb.name], writes=[st2.name + ":lr"])
        p.add("vector", lambda e: e.reciprocal(st2[:, 16:32], st2[:, 0:16]),
              reads=[st2.name + ":lr"], writes=[st2.name + ":ilr"])
        p.add("vector", lambda e: e.tensor_tensor(e3(pn), e3(eb), bc(st2, 16), ALU.mult),
              reads=[eb.name, st2.name + ":ilr"], writes=[pn.name])
        p.add("vector", lambda e: e.tensor_reduce(
            st2[:, 32:48].rearrange("p (a b) -> p a b", b=1), e3(pn), AX.X, ALU.max),
            reads=[pn.name], writes=[st2.name + ":m1"])
        p.add("vector", lambda e: e.tensor_tensor(e3(t1), e3(pn), bc(st2, 32), ALU.is_ge),
              reads=[pn.name, st2.name + ":m1"], writes=[t1.name])
        p.add("vector", lambda e: e.scalar_tensor_tensor(
            e3(t2), e3(t1), NEG_BIG, e3(pn), ALU.mult, ALU.add),
            reads=[t1.name, pn.name], writes=[t2.name])
        p.add("vector", lambda e: e.tensor_reduce(
            st2[:, 48:64].rearrange("p (a b) -> p a b", b=1), e3(t2), AX.X, ALU.max),
            reads=[t2.name], writes=[st2.name + ":m2"])
        p.add("vector", lambda e: e.tensor_tensor(st2[:, 64:80], st2[:, 32:48],
                                                  st2[:, 48:64], ALU.add),
              reads=[st2.name + ":m1", st2.name + ":m2"], writes=[st2.name + ":dn"])
        p.add("vector", lambda e: e.tensor_scalar(st2[:, 80:96], st2[:, 64:80], 1e-9,
                                                  None, ALU.add),
              reads=[st2.name + ":dn"], writes=[st2.name + ":dn2"])
        p.add("vector", lambda e: e.reciprocal(st2[:, 96:112], st2[:, 80:96]),
              reads=[st2.name + ":dn2"], writes=[st2.name + ":idn"])
        p.add("vector", lambda e: e.tensor_tensor(e3(t1), e3(pn), bc(st2, 48), ALU.is_ge),
              reads=[pn.name, st2.name + ":m2"], writes=[t1.name])
        p.add("vector", lambda e: e.tensor_tensor(e3(t2), e3(pn), e3(t1), ALU.mult),
              reads=[pn.name, t1.name], writes=[t2.name])
        p.add("vector", lambda e: e.tensor_tensor(e3(comb), e3(t2), bc(st2, 96), ALU.mult),
              reads=[t2.name, st2.name + ":idn"], writes=[comb.name])

        experts = [("s", n, 4, 512) for n in range(2)] + \
                  [("r", r, 2, 256) for r in range(16)]

        def load_expert(xi):
            kind, ei, nmc, cw = experts[xi]
            wg, wu, wd = wg2[xi % 2], wu2[xi % 2], wd2[xi % 2]
            if kind == "s":
                WG, WU, WD = Wgs, Wus, Wds
            else:
                WG, WU, WD = Wgr, Wur, Wdr
            emit_load_cast(p, env, lambda i, ei=ei, WG=WG: WG.ap()[ei, i * 128:(i + 1) * 128, :],
                           WG.name, 8, cw, wg, 0, wstage, tag=":g")
            emit_load_cast(p, env, lambda i, ei=ei, WU=WU: WU.ap()[ei, i * 128:(i + 1) * 128, :],
                           WU.name, 8, cw, wu, 0, wstage, tag=":u")
            emit_load_cast(p, env, lambda i, ei=ei, WD=WD: WD.ap()[ei, i * 128:(i + 1) * 128, :],
                           WD.name, nmc, 1024, wd, 0, wstage, tag=":d")

        if h == 0:
            load_expert(0)
        for xi, (kind, ei, nmc, cw) in enumerate(experts):
            if xi + 1 < len(experts):
                load_expert(xi + 1)
            elif h == 0:
                load_expert(0)
            wg, wu, wd = wg2[xi % 2], wu2[xi % 2], wd2[xi % 2]
            it = [0]

            def gu_h(tg):
                hsb = hsb2[tg % 2]
                for mc in range(nmc):
                    pGt = q4[1] if it[0] % 2 == 0 else q4[0]
                    pUt = q4[2] if it[0] % 2 == 0 else q4[3]
                    ssb = ssb2[it[0] % 2]
                    usb = usb2[it[0] % 2]
                    it[0] += 1
                    for fc in range(FC):
                        p.add("tensor", lambda e, tg=tg, mc=mc, fc=fc, cw=cw, wg=wg, pGt=pGt: e.matmul(
                            pGt[:, 0:512],
                            wg[:, fc * cw + mc * 128: fc * cw + (mc + 1) * 128],
                            xn_fm[:, fc * 2048 + tg * 512: fc * 2048 + tg * 512 + 512],
                            start=(fc == 0), stop=(fc == 7)),
                            reads=[wg.name + ":g"] + [f"{xn_fm.name}:{tg * 4 + j}" for j in range(4)],
                            writes=[pGt.name])
                    for fc in range(FC):
                        p.add("tensor", lambda e, tg=tg, mc=mc, fc=fc, cw=cw, wu=wu, pUt=pUt: e.matmul(
                            pUt[:, 0:512],
                            wu[:, fc * cw + mc * 128: fc * cw + (mc + 1) * 128],
                            xn_fm[:, fc * 2048 + tg * 512: fc * 2048 + tg * 512 + 512],
                            start=(fc == 0), stop=(fc == 7)),
                            reads=[wu.name + ":u"] + [f"{xn_fm.name}:{tg * 4 + j}" for j in range(4)],
                            writes=[pUt.name])
                    p.add("scalar", lambda e, pGt=pGt, ssb=ssb: e.activation(
                        ssb[:], pGt[:, 0:512], AF.Silu),
                        reads=[pGt.name], writes=[ssb.name])
                    p.add("vector", lambda e, pUt=pUt, usb=usb: e.tensor_copy(
                        usb[:], pUt[:, 0:512]),
                        reads=[pUt.name], writes=[usb.name])
                    p.add("vector", lambda e, mc=mc, ssb=ssb, usb=usb, hsb=hsb: e.tensor_tensor(
                        hsb[:, mc * 512:(mc + 1) * 512], ssb[:], usb[:], ALU.mult),
                        reads=[ssb.name, usb.name], writes=[f"{hsb.name}:{mc}"])

            def down(tg):
                hsb = hsb2[tg % 2]
                for tl2 in range(4):
                    tl = tg * 4 + tl2
                    for mc in range(nmc):
                        for hf in range(2):
                            p.add("tensor", lambda e, mc=mc, hf=hf, tl2=tl2, nmc=nmc, wd=wd, hsb=hsb: e.matmul(
                                pO[hf][:, 0:512],
                                hsb[:, mc * 512 + tl2 * 128: mc * 512 + (tl2 + 1) * 128],
                                wd[:, mc * 1024 + hf * 512: mc * 1024 + hf * 512 + 512],
                                start=(mc == 0), stop=(mc == nmc - 1)),
                                reads=[f"{hsb.name}:{mc}", wd.name + ":d"], writes=[pO[hf].name])
                    for hf in range(2):
                        csl = slice(tl * 1024 + hf * 512, tl * 1024 + hf * 512 + 512)
                        if kind == "s":
                            p.add("vector", lambda e, csl=csl, hf=hf: e.tensor_add(
                                acc[:, csl], acc[:, csl], pO[hf][:, 0:512]),
                                reads=[pO[hf].name, f"b_acc:{tl}"], writes=[f"b_acc:{tl}"])
                        else:
                            p.add("vector", lambda e, csl=csl, hf=hf, tl=tl, ei=ei: e.scalar_tensor_tensor(
                                acc[:, csl], pO[hf][:, 0:512],
                                comb[:, tl * 16 + ei: tl * 16 + ei + 1],
                                acc[:, csl], ALU.mult, ALU.add),
                                reads=[pO[hf].name, comb.name, f"b_acc:{tl}"],
                                writes=[f"b_acc:{tl}"])

            gu_h(0)
            for tg in range(4):
                if tg + 1 < 4:
                    gu_h(tg + 1)
                down(tg)
        for tl in range(16):
            tt = tb + tl
            p.add("gpsimd", lambda e, tt=tt, tl=tl: e.dma_start(
                Xo.ap()[tt * 128:(tt + 1) * 128, :], acc[:, tl * 1024:(tl + 1) * 1024]),
                reads=[f"b_acc:{tl}"], writes=[Xo.name], dma="b_store")



def build_block_c(nc, p, al, Xi, Xmid, Xo, WqT, WkT, WvT, WpT, Bq, Bk, Bv, Bp,
                  CosT, SinT, XnH, IlS, env):
    sq = al.alloc_sbuf_tensor("c_sq", [128, 1024], F32)
    stats = al.alloc_sbuf_tensor("c_st", [128, 8], F32)

    K_fm = al.alloc_sbuf_tensor("c_K", [128, 4 * T], BF16)
    V_tm = al.alloc_sbuf_tensor("c_V", [128, NT * 512], BF16)
    cos_sb = al.alloc_sbuf_tensor("c_cos", [128, 1024], BF16)
    sin_sb = al.alloc_sbuf_tensor("c_sin", [128, 1024], BF16)
    xnt_s = [al.alloc_sbuf_tensor(f"c_xns{i}", [128, 1024], BF16) for i in range(2)]
    xnfm_t2 = [al.alloc_sbuf_tensor(f"c_xnfm{i}", [128, 1024], BF16) for i in range(2)]
    wq_bf = al.alloc_sbuf_tensor("c_wq", [128, 4096], BF16)
    wp_bf = al.alloc_sbuf_tensor("c_wp", [128, 4096], BF16)
    kraw = al.alloc_sbuf_tensor("c_kraw", [128, 512], F32)
    krot2 = [al.alloc_sbuf_tensor(f"c_krot{i}", [128, 512], BF16) for i in range(2)]
    tmpa = al.alloc_sbuf_tensor("c_tma", [128, 256], F32)
    tmpb = al.alloc_sbuf_tensor("c_tmb", [128, 256], F32)
    abf = [al.alloc_sbuf_tensor(f"c_abf{i}", [128, T], BF16) for i in range(4)]
    at_sb = [al.alloc_sbuf_tensor(f"c_at{i}", [128, 512], BF16) for i in range(2)]
    qfm = al.alloc_sbuf_tensor("c_qfm", [128, 2048], BF16)
    yfm = al.alloc_sbuf_tensor("c_yfm", [128, 2048], F32)
    ybf = al.alloc_sbuf_tensor("c_ybf", [128, 2048], BF16)
    il_all = al.alloc_sbuf_tensor("c_il", [128, 32], F32)
    ilT = sq
    smx = al.alloc_sbuf_tensor("c_smx", [128, 80], F32)
    rhl = al.alloc_sbuf_tensor("c_rhl", [1, 6 * 512], BF16)
    bphl = al.alloc_sbuf_tensor("c_bp", [1, 2 * 1024], BF16)
    xnew = [al.alloc_sbuf_tensor(f"c_xn{i}", [128, 1024], F32) for i in range(2)]
    wstage = [al.alloc_sbuf_tensor("c_wst", [128, 1024], F32)]
    rows = sq
    rtmp = sq
    ar1 = sbuf_arena(al, "c_ar1", 8192)
    ar2 = sbuf_arena(al, "c_ar2", 16384)
    xin = [al.alloc_sbuf_tensor_at(f"c_xin{i}", [128, 1024], F32, offset=ar1 + 4096 * i)
           for i in range(2)]
    xold = [al.alloc_sbuf_tensor_at(f"c_xo{i}", [128, 1024], F32, offset=ar1 + 4096 * i)
            for i in range(2)]
    wk_bf = al.alloc_sbuf_tensor_at("c_wk", [128, 4096], BF16, offset=ar2)
    wv_bf = al.alloc_sbuf_tensor_at("c_wv", [128, 4096], BF16, offset=ar2 + 8192)
    s_row = al.alloc_sbuf_tensor_at("c_srow", [128, T], F32, offset=ar2)
    xn_t2 = [al.alloc_sbuf_tensor_at(f"c_xnt{i}", [128, 1024], BF16,
                                     offset=ar2 + 2048 * i) for i in range(2)]

    q4 = env["psum_q"]
    pP, pYH, pS = q4[0], q4[1], [q4[2], q4[3]]
    pT = env["psum_t"]
    pO = env["psum_o"]

    env = dict(env)
    env["sq"] = sq
    env["stats"] = stats
    ones_bf = env["ones_bf"]
    onesf = env["ones_f"]
    ident = env["ident_bf"]

    def hilo_into(src_hbm, c0, n, dsth, dstl, keyh, keyl):
        r0 = sq[0:1, 0:n]
        r1 = sq[0:1, 512:512 + n]
        p.add("sync", lambda e: e.dma_start(r0, src_hbm.ap()[0:1, c0:c0 + n]),
              reads=[src_hbm.name], writes=[sq.name], dma="c_rowld")
        p.add("vector", lambda e: e.tensor_copy(dsth, r0),
              reads=[sq.name], writes=[keyh])
        p.add("vector", lambda e: e.tensor_copy(r1, dsth),
              reads=[keyh], writes=[sq.name])
        p.add("vector", lambda e: e.tensor_sub(r1, r0, r1),
              reads=[sq.name], writes=[sq.name])
        p.add("vector", lambda e: e.tensor_copy(dstl, r1),
              reads=[sq.name], writes=[keyl])

    def hilo(src_hbm, c0, n, slot):
        assert n <= 512
        dsth = rhl[:, slot * 512: slot * 512 + n]
        dstl = rhl[:, (slot + 1) * 512: (slot + 1) * 512 + n]
        keyh = rhl.name + f":{slot}h"
        keyl = rhl.name + f":{slot}l"
        hilo_into(src_hbm, c0, n, dsth, dstl, keyh, keyl)
        return dsth, dstl, [keyh, keyl]


    for nmsb, hbm in ((cos_sb, CosT), (sin_sb, SinT)):
        p.add("sync", lambda e, nmsb=nmsb, hbm=hbm: e.dma_start(
            wstage[0][:].rearrange("p (a b) -> p a b", b=32),
            hbm.ap().rearrange("(a p) b -> p a b", p=128)),
            reads=[hbm.name], writes=[wstage[0].name], dma=wstage[0].name)
        p.add("vector", lambda e, nmsb=nmsb: e.tensor_copy(nmsb[:], wstage[0][:]),
              reads=[wstage[0].name], writes=[nmsb.name])

    def rope(src_f32, dst_bf, tile):
        va = lambda t: t[:].rearrange("p (h x) -> p h x", x=64)[:, :, 0:32]
        vb = lambda t: t[:].rearrange("p (h x) -> p h x", x=64)[:, :, 32:64]
        cosb = cos_sb[:, tile * 32:(tile + 1) * 32].unsqueeze(1).broadcast_to([128, 8, 32])
        sinb = sin_sb[:, tile * 32:(tile + 1) * 32].unsqueeze(1).broadcast_to([128, 8, 32])
        t3 = lambda t: t[:].rearrange("p (h x) -> p h x", x=32)
        ks, kd = src_f32.name, dst_bf.name
        p.add("vector", lambda e: e.tensor_tensor(t3(tmpa), va(src_f32), cosb, ALU.mult),
              reads=[ks, cos_sb.name], writes=[tmpa.name])
        p.add("vector", lambda e: e.tensor_tensor(t3(tmpb), vb(src_f32), sinb, ALU.mult),
              reads=[ks, sin_sb.name], writes=[tmpb.name])
        p.add("vector", lambda e: e.tensor_tensor(va(dst_bf), t3(tmpa), t3(tmpb), ALU.subtract),
              reads=[tmpa.name, tmpb.name], writes=[kd])
        p.add("vector", lambda e: e.tensor_tensor(t3(tmpa), va(src_f32), sinb, ALU.mult),
              reads=[ks, sin_sb.name], writes=[tmpa.name])
        p.add("vector", lambda e: e.tensor_tensor(t3(tmpb), vb(src_f32), cosb, ALU.mult),
              reads=[ks, cos_sb.name], writes=[tmpb.name])
        p.add("vector", lambda e: e.tensor_tensor(vb(dst_bf), t3(tmpa), t3(tmpb), ALU.add),
              reads=[tmpa.name, tmpb.name], writes=[kd])

    def xn_tile_to_fm(tile, stream_slot):
        xs = xnt_s[stream_slot]
        xnfm_t = xnfm_t2[stream_slot]
        p.add("sync", lambda e, tile=tile, xs=xs: e.dma_start(
            xs[:], XnH.ap()[tile * 128:(tile + 1) * 128, :]),
            reads=[XnH.name], writes=[xs.name], dma=xs.name)
        for c in range(FC):
            emit_transpose_to(p, env, xs[:, c * 128:(c + 1) * 128], xs.name,
                              xnfm_t[:, c * 128:(c + 1) * 128], xnfm_t.name,
                              pT[c % 2],
                              evac_engine=("vector" if c % 2 == 0 else "scalar"))
        return xnfm_t

    GQ = 8
    for H in range(2):
        c0 = H * 512
        if H == 0:
            pass
        else:
            p.region_switch(s_row.name, wk_bf.name)
            p.region_switch(s_row.name, wv_bf.name)
        emit_load_cast(p, env, lambda i, c0=c0: WkT.ap()[i * 128:(i + 1) * 128, c0:c0 + 512],
                       WkT.name, 8, 512, wk_bf, 0, wstage)
        emit_load_cast(p, env, lambda i: WvT.ap()[i * 128:(i + 1) * 128, c0:c0 + 512],
                       WvT.name, 8, 512, wv_bf, 0, wstage)
        bk_h, bk_l, bk_keys = hilo(Bk, c0, 512, 0)
        bv_h, bv_l, bv_keys = hilo(Bv, c0, 512, 2)
        for kt in range(NT):
            if H == 0:
                xn_t = xnt_s[kt % 2]
                def consume(p, kt=kt, xn_t=xn_t):
                    p.add("gpsimd", lambda e, kt=kt, xn_t=xn_t: e.dma_start(
                        XnH.ap()[kt * 128:(kt + 1) * 128, :], xn_t[:]),
                        reads=[xn_t.name], writes=[XnH.name], dma="c_xnst")
                emit_ln_tile(p, env, Xi, kt, xin[kt % 2], xn_t, None, consume)
                xnfm_t = xnfm_t2[kt % 2]
                for c in range(FC):
                    emit_transpose_to(p, env, xn_t[:, c * 128:(c + 1) * 128], xn_t.name,
                                      xnfm_t[:, c * 128:(c + 1) * 128], xnfm_t.name,
                                      pT[c % 2],
                                      evac_engine=("vector" if c % 2 == 0 else "scalar"))
            else:
                xnfm_t = xn_tile_to_fm(kt, kt % 2)
            krot = krot2[kt % 2]
            for fc in range(FC):
                p.add("tensor", lambda e, fc=fc, xnfm_t=xnfm_t: e.matmul(
                    pP[:, 0:512], xnfm_t[:, fc * 128:(fc + 1) * 128],
                    wk_bf[:, fc * 512:(fc + 1) * 512],
                    start=(fc == 0), stop=False),
                    reads=[xnfm_t.name, wk_bf.name], writes=[pP.name])
            p.add("tensor", lambda e: e.matmul(pP[:, 0:512], ones_bf[:], bk_h,
                                               start=False, stop=False),
                  reads=[ones_bf.name] + bk_keys, writes=[pP.name])
            p.add("tensor", lambda e: e.matmul(pP[:, 0:512], ones_bf[:], bk_l,
                                               start=False, stop=True),
                  reads=[ones_bf.name] + bk_keys, writes=[pP.name])
            p.add("vector", lambda e, kraw=kraw: e.tensor_copy(kraw[:], pP[:, 0:512]),
                  reads=[pP.name], writes=[kraw.name])
            rope(kraw, krot, kt)
            for c in range(4):
                emit_transpose_to(p, env, krot[:, c * 128:(c + 1) * 128], krot.name,
                                  K_fm[:, c * T + kt * 128: c * T + (kt + 1) * 128],
                                  f"{K_fm.name}:{kt}", pT[c % 2],
                                  evac_engine=("vector" if c % 2 == 0 else "scalar"))
            for fc in range(FC):
                p.add("tensor", lambda e, fc=fc, xnfm_t=xnfm_t: e.matmul(
                    pP[:, 0:512], xnfm_t[:, fc * 128:(fc + 1) * 128],
                    wv_bf[:, fc * 512:(fc + 1) * 512],
                    start=(fc == 0), stop=False),
                    reads=[xnfm_t.name, wv_bf.name], writes=[pP.name])
            p.add("tensor", lambda e: e.matmul(pP[:, 0:512], ones_bf[:], bv_h,
                                               start=False, stop=False),
                  reads=[ones_bf.name] + bv_keys, writes=[pP.name])
            p.add("tensor", lambda e: e.matmul(pP[:, 0:512], ones_bf[:], bv_l,
                                               start=False, stop=True),
                  reads=[ones_bf.name] + bv_keys, writes=[pP.name])
            p.add("scalar", lambda e, kt=kt: e.copy(
                V_tm[:, kt * 512:(kt + 1) * 512], pP[:, 0:512]),
                reads=[pP.name], writes=[f"{V_tm.name}:{kt}"])

        if H == 0:
            p.region_switch(xin[0].name, xold[0].name)
            p.region_switch(xin[1].name, xold[1].name)
        p.region_switch(wk_bf.name, s_row.name)
        p.region_switch(wv_bf.name, s_row.name)
        emit_load_cast(p, env, lambda i, c0=c0: WqT.ap()[i * 128:(i + 1) * 128, c0:c0 + 512],
                       WqT.name, 8, 512, wq_bf, 0, wstage)
        emit_load_cast(p, env, lambda i, c0=c0: WpT.ap()[c0 + i * 128: c0 + (i + 1) * 128, :],
                       WpT.name, 4, 1024, wp_bf, 0, wstage)
        bq_h, bq_l, bq_keys = hilo(Bq, c0, 512, 4)
        if H == 0:
            for hseg in range(2):
                hilo_into(Bp, hseg * 512, 512,
                          bphl[:, hseg * 512: hseg * 512 + 512],
                          bphl[:, 1024 + hseg * 512: 1024 + hseg * 512 + 512],
                          bphl.name + ":h", bphl.name + ":l")
        Xsrc = Xi if H == 0 else Xmid
        Xdst = Xmid if H == 0 else Xo
        for g in range(GQ):
            for qtl in range(4):
                qt = g * 4 + qtl
                xnfm_q = xn_tile_to_fm(qt, qtl % 2)
                for fc in range(FC):
                    p.add("tensor", lambda e, fc=fc, xnfm_q=xnfm_q: e.matmul(
                        pP[:, 0:512], xnfm_q[:, fc * 128:(fc + 1) * 128],
                        wq_bf[:, fc * 512:(fc + 1) * 512],
                        start=(fc == 0), stop=False),
                        reads=[xnfm_q.name, wq_bf.name], writes=[pP.name])
                p.add("tensor", lambda e: e.matmul(pP[:, 0:512], ones_bf[:], bq_h,
                                                   start=False, stop=False),
                      reads=[ones_bf.name] + bq_keys, writes=[pP.name])
                p.add("tensor", lambda e: e.matmul(pP[:, 0:512], ones_bf[:], bq_l,
                                                   start=False, stop=True),
                      reads=[ones_bf.name] + bq_keys, writes=[pP.name])
                p.add("vector", lambda e, kraw=kraw: e.tensor_copy(kraw[:], pP[:, 0:512]),
                      reads=[pP.name], writes=[kraw.name])
                krot = krot2[qtl % 2]
                rope(kraw, krot, qt)
                for c in range(4):
                    emit_transpose_to(
                        p, env, krot[:, c * 128:(c + 1) * 128], krot.name,
                        qfm[:, qtl * 512 + c * 128: qtl * 512 + (c + 1) * 128],
                        f"{qfm.name}:{qtl}", pT[c % 2],
                        evac_engine=("vector" if c % 2 == 0 else "scalar"))
            ng = g + 1
            L = ng * 512
            mask4 = env["mask4"]
            slots = [pS[0], pS[1], pP, pO[0], pO[1]]
            for hh in range(8):
                c = hh // 2
                pb = (hh % 2) * 64
                for qtl in range(4):
                    qt = g * 4 + qtl
                    v = qt % 4
                    sb = qtl * 20
                    kgm = f"{smx.name}:{qtl}:gm"
                    klsp = f"{smx.name}:{qtl}:lsp"
                    kbm = f"{smx.name}:{qtl}:bm"
                    kls = f"{smx.name}:{qtl}:ls"
                    km = f"{smx.name}:{qtl}:m"
                    direct = ng <= NDIR
                    for g2 in range(ng):
                        ps = slots[(qtl * ng + g2) % len(slots)] if direct else slots[g2 % 3]
                        kk = [f"{K_fm.name}:{4 * g2 + j}" for j in range(4)]
                        p.add("tensor", lambda e, g2=g2, qtl=qtl, c=c, pb=pb, ps=ps: e.matmul(
                            ps[:, 0:512],
                            qfm[pb:pb + 64, qtl * 512 + c * 128: qtl * 512 + (c + 1) * 128],
                            K_fm[pb:pb + 64, c * T + g2 * 512: c * T + g2 * 512 + 512],
                            start=True, stop=True),
                            reads=[f"{qfm.name}:{qtl}"] + kk, writes=[ps.name])
                        if direct:
                            if g2 == ng - 1:
                                p.add("vector", lambda e, ps=ps, v=v: e.tensor_tensor(
                                    ps[:, 0:512], ps[:, 0:512],
                                    mask4[:, v * 512:(v + 1) * 512], ALU.add),
                                    reads=[ps.name, mask4.name], writes=[ps.name])
                            p.add("vector", lambda e, ps=ps, g2=g2, sb=sb: e.tensor_reduce(
                                smx[:, sb + g2: sb + g2 + 1], ps[:, 0:512], AX.X, ALU.max),
                                reads=[ps.name], writes=[kgm])
                        else:
                            if g2 == ng - 1:
                                p.add("vector", lambda e, ps=ps, g2=g2, v=v: e.tensor_tensor(
                                    s_row[:, g2 * 512: g2 * 512 + 512], ps[:, 0:512],
                                    mask4[:, v * 512:(v + 1) * 512], ALU.add),
                                    reads=[ps.name, mask4.name], writes=[s_row.name])
                            else:
                                eng = "vector" if g2 % 2 == 0 else "scalar"
                                p.add(eng, lambda e, g2=g2, ps=ps, eng=eng: (
                                    e.tensor_copy(s_row[:, g2 * 512:(g2 + 1) * 512], ps[:, 0:512])
                                    if eng == "vector" else
                                    e.copy(s_row[:, g2 * 512:(g2 + 1) * 512], ps[:, 0:512])),
                                    reads=[ps.name], writes=[s_row.name])
                            p.add("vector", lambda e, g2=g2, sb=sb: e.tensor_reduce(
                                smx[:, sb + g2: sb + g2 + 1],
                                s_row[:, g2 * 512:(g2 + 1) * 512], AX.X, ALU.max),
                                reads=[s_row.name], writes=[kgm])
                    p.add("vector", lambda e, sb=sb, ng=ng: e.tensor_reduce(
                        smx[:, sb + 18: sb + 19], smx[:, sb: sb + ng], AX.X, ALU.max),
                        reads=[kgm], writes=[km])
                    p.add("vector", lambda e, sb=sb: e.tensor_scalar(
                        smx[:, sb + 16: sb + 17], smx[:, sb + 18: sb + 19], -0.125,
                        None, ALU.mult),
                        reads=[km], writes=[kbm])
                    if direct:
                        for g2 in range(ng):
                            ps = slots[(qtl * ng + g2) % len(slots)]
                            p.add("scalar", lambda e, ps=ps, g2=g2, qtl=qtl, sb=sb: e.activation(
                                abf[qtl][:, g2 * 512:(g2 + 1) * 512], ps[:, 0:512],
                                AF.Exp, bias=smx[:, sb + 16: sb + 17], scale=0.125,
                                accum_out=smx[:, sb + 8 + g2: sb + 9 + g2]),
                                reads=[ps.name, kbm],
                                writes=[f"{abf[qtl].name}:{g2}", klsp])
                        p.add("vector", lambda e, sb=sb, ng=ng: e.tensor_reduce(
                            smx[:, sb + 17: sb + 18], smx[:, sb + 8: sb + 8 + ng],
                            AX.X, ALU.add),
                            reads=[klsp], writes=[kls])
                    else:
                        p.add("scalar", lambda e, L=L, qtl=qtl, sb=sb: e.activation(
                            abf[qtl][:, 0:L], s_row[:, 0:L], AF.Exp,
                            bias=smx[:, sb + 16: sb + 17], scale=0.125,
                            accum_out=smx[:, sb + 17: sb + 18]),
                            reads=[s_row.name, kbm],
                            writes=[kls] + [f"{abf[qtl].name}:{gg2}" for gg2 in range(ng)])
                    p.add("vector", lambda e, qtl=qtl, hh=hh, sb=sb: e.reciprocal(
                        il_all[:, qtl * 8 + hh: qtl * 8 + hh + 1],
                        smx[:, sb + 17: sb + 18]),
                        reads=[kls], writes=[il_all.name])
                nkb = 4 * ng
                def emit_at(kb):
                    pt = pT[kb % 2]
                    ab = at_sb[kb % 2]
                    for qtl in range(4):
                        p.add("tensor", lambda e, kb=kb, qtl=qtl, pt=pt: e.transpose(
                            pt[:, qtl * 128:(qtl + 1) * 128],
                            abf[qtl][:, kb * 128:(kb + 1) * 128], ident[:]),
                            reads=[f"{abf[qtl].name}:{kb // 4}", ident.name],
                            writes=[pt.name])
                    eng = "vector" if kb % 2 == 0 else "scalar"
                    p.add(eng, lambda e, pt=pt, ab=ab, eng=eng: (
                        e.tensor_copy(ab[:], pt[:, 0:512]) if eng == "vector"
                        else e.copy(ab[:], pt[:, 0:512])),
                        reads=[pt.name], writes=[ab.name])
                emit_at(0)
                for kb in range(nkb):
                    if kb + 1 < nkb:
                        emit_at(kb + 1)
                    ab = at_sb[kb % 2]
                    p.add("tensor", lambda e, kb=kb, hh=hh, ab=ab, nkb=nkb: e.matmul(
                        pYH[0:64, 0:512],
                        V_tm[:, kb * 512 + hh * 64: kb * 512 + (hh + 1) * 64],
                        ab[:], start=(kb == 0), stop=(kb == nkb - 1)),
                        reads=[f"{V_tm.name}:{kb}", ab.name], writes=[pYH.name])
                p.add("vector" if hh % 2 == 0 else "scalar",
                      lambda e, hh=hh, c=c, pb=pb: (
                          e.tensor_copy(yfm[pb:pb + 64, c * 512:(c + 1) * 512],
                                        pYH[0:64, 0:512])
                          if hh % 2 == 0 else
                          e.copy(yfm[pb:pb + 64, c * 512:(c + 1) * 512],
                                 pYH[0:64, 0:512])),
                      reads=[pYH.name], writes=[yfm.name])
            identf = env["ident_f"]
            p.add("tensor", lambda e, identf=identf: e.transpose(
                pS[1][0:32, 0:128], il_all[:], identf[:]),
                reads=[il_all.name, identf.name], writes=[pS[1].name])
            p.add("vector", lambda e: e.tensor_copy(sq[64:96, 0:128], pS[1][0:32, 0:128]),
                  reads=[pS[1].name], writes=[sq.name])
            p.add("gpsimd", lambda e: e.dma_start(IlS.ap(), sq[64:96, 0:128]),
                  reads=[sq.name], writes=[IlS.name], dma="c_ils")
            for c in range(4):
                src = IlS.ap().rearrange("(q hh) c -> q hh c", hh=8)[:, 2 * c:2 * c + 2, :]
                dst = sq[0:1, 0:1024].rearrange("o (q s c) -> o q s c", q=4, s=2)
                p.add("sync", lambda e, src=src, dst=dst: e.dma_start(dst, src),
                      reads=[IlS.name], writes=[sq.name], dma="c_ilr")
                for sub in range(2):
                    for qtl in range(4):
                        p.add("tensor", lambda e, qtl=qtl, sub=sub: e.matmul(
                            pYH[sub * 64:(sub + 1) * 64, qtl * 128:(qtl + 1) * 128],
                            onesf[0:1, 0:64],
                            sq[0:1, qtl * 256 + sub * 128: qtl * 256 + (sub + 1) * 128],
                            start=True, stop=True),
                            reads=[sq.name, onesf.name], writes=[pYH.name])
                p.add("vector", lambda e, c=c: e.tensor_tensor(
                    ybf[:, c * 512:(c + 1) * 512], yfm[:, c * 512:(c + 1) * 512],
                    pYH[:, 0:512], ALU.mult),
                    reads=[yfm.name, pYH.name], writes=[ybf.name])
            for qtl in range(4):
                qt = g * 4 + qtl
                p.add("gpsimd", lambda e, qt=qt, qtl=qtl, Xsrc=Xsrc: e.dma_start(
                    xold[qtl % 2][:], Xsrc.ap()[qt * 128:(qt + 1) * 128, :]),
                    reads=[Xsrc.name], writes=[xold[qtl % 2].name],
                    dma=xold[qtl % 2].name)
                for c in range(4):
                    for hf in range(2):
                        p.add("tensor", lambda e, c=c, hf=hf, qtl=qtl: e.matmul(
                            pO[hf][:, 0:512],
                            ybf[:, c * 512 + qtl * 128: c * 512 + (qtl + 1) * 128],
                            wp_bf[:, c * 1024 + hf * 512: c * 1024 + hf * 512 + 512],
                            start=(c == 0), stop=(c == 3 and H == 1)),
                            reads=[ybf.name, wp_bf.name], writes=[pO[hf].name])
                if H == 0:
                    for hf in range(2):
                        p.add("tensor", lambda e, hf=hf: e.matmul(
                            pO[hf][:, 0:512], ones_bf[:],
                            bphl[:, hf * 512: hf * 512 + 512],
                            start=False, stop=False),
                            reads=[ones_bf.name, bphl.name + ":h"], writes=[pO[hf].name])
                        p.add("tensor", lambda e, hf=hf: e.matmul(
                            pO[hf][:, 0:512], ones_bf[:],
                            bphl[:, 1024 + hf * 512: 1024 + hf * 512 + 512],
                            start=False, stop=True),
                            reads=[ones_bf.name, bphl.name + ":l"], writes=[pO[hf].name])
                kxn = xnew[qtl % 2].name
                for hf in range(2):
                    p.add("vector", lambda e, qtl=qtl, hf=hf: e.tensor_add(
                        xnew[qtl % 2][:, hf * 512: hf * 512 + 512], pO[hf][:, 0:512],
                        xold[qtl % 2][:, hf * 512: hf * 512 + 512]),
                        reads=[pO[hf].name, xold[qtl % 2].name], writes=[kxn])
                p.add("gpsimd", lambda e, qt=qt, qtl=qtl, Xdst=Xdst: e.dma_start(
                    Xdst.ap()[qt * 128:(qt + 1) * 128, :], xnew[qtl % 2][:]),
                    reads=[kxn], writes=[Xdst.name], dma="c_st" + Xdst.name)



def build_block_d(nc, p, al, Xi, Xo, W1, B1c, W2, B2, WrT, Br, env):
    rxin = al.suballoc(8192)
    xin = [al.alloc_sbuf_tensor_at(f"d_xin{i}", [128, 1024], F32, offset=rxin + 4096 * i)
           for i in range(2)]
    stats = al.alloc_sbuf_tensor("d_st", [128, 8], F32)
    xn_fm = al.alloc_sbuf_tensor("d_xnfm", [128, FC * T], BF16)
    ract = al.suballoc(65536)
    act = al.alloc_sbuf_tensor_at("d_act", [128, 16 * 2048], BF16, offset=ract)
    sq = al.alloc_sbuf_tensor_at("d_sq", [128, 1024], F32, offset=ract)
    xn_t2 = [al.alloc_sbuf_tensor_at(f"d_xnt{i}", [128, 1024], BF16,
                                     offset=ract + 4096 + 2048 * i) for i in range(2)]
    w2_bf = al.alloc_sbuf_tensor("d_w2", [128, 16 * 1024], BF16)
    w1a = [al.alloc_sbuf_tensor(f"d_w1a{i}", [128, 1024], BF16) for i in range(2)]
    w1g = [al.alloc_sbuf_tensor(f"d_w1g{i}", [128, 1024], BF16) for i in range(2)]
    w1stage = [al.alloc_sbuf_tensor(f"d_w1s{i}", [128, 1024], F32) for i in range(2)]
    w1stage += [al.alloc_sbuf_tensor_at(f"d_w1s{i+2}", [128, 1024], F32,
                                        offset=rxin + 4096 * i) for i in range(2)]
    wstage = w1stage[0:2]
    rw = al.alloc_sbuf_tensor("d_rw", [128, 64], BF16)
    r_rt = al.suballoc(4096)
    lg = al.alloc_sbuf_tensor_at("d_lg", [128, 256], F32, offset=r_rt)
    ev = al.alloc_sbuf_tensor_at("d_ev", [128, 256], F32, offset=r_rt + 1024)
    t1 = al.alloc_sbuf_tensor_at("d_t1", [128, 256], F32, offset=r_rt + 2048)
    t2 = al.alloc_sbuf_tensor_at("d_t2", [128, 256], F32, offset=r_rt + 3072)
    comb = al.alloc_sbuf_tensor("d_comb", [128, 256], F32)
    combb = al.alloc_sbuf_tensor("d_combb", [128, 256], BF16)
    st2 = al.alloc_sbuf_tensor("d_st2", [128, 112], F32)
    b1c = al.alloc_sbuf_tensor("d_b1", [128, 32], F32)
    brrow = al.alloc_sbuf_tensor("d_br", [1, 8], F32)
    brhl = al.alloc_sbuf_tensor("d_brhl", [1, 16], BF16)
    brt = al.alloc_sbuf_tensor("d_brt", [1, 8], F32)
    combT = al.alloc_sbuf_tensor("d_cT", [8, 128], BF16)
    r_sg = al.suballoc(2048)
    b2bf = al.alloc_sbuf_tensor_at("d_b2b", [8, 1024], BF16, offset=r_sg)
    ssb = [al.alloc_sbuf_tensor_at(f"d_s{i}", [128, 512], BF16, offset=r_sg + 1024 * i)
           for i in range(2)]
    gsb = [al.alloc_sbuf_tensor(f"d_g{i}", [128, 512], BF16) for i in range(2)]
    obuf = [al.alloc_sbuf_tensor_at("d_ob0", [128, 1024], F32, offset=r_rt),
            al.alloc_sbuf_tensor("d_ob1", [128, 1024], F32)]

    q4 = env["psum_q"]
    pR, pA, pG, pC = q4[0], q4[1], q4[2], q4[3]
    pO = env["psum_o"]
    pT = env["psum_t"]

    env = dict(env)
    env["sq"] = sq
    env["stats"] = stats
    ones_bf = env["ones_bf"]
    ident = env["ident_bf"]

    for tt in range(NT):
        xn_t = xn_t2[tt % 2]
        def consume(p, tt=tt, xn_t=xn_t):
            for c in range(FC):
                emit_transpose_to(
                    p, env, xn_t[:, c * 128:(c + 1) * 128], xn_t.name,
                    xn_fm[:, c * T + tt * 128: c * T + (tt + 1) * 128],
                    f"{xn_fm.name}:{tt}", pT[c % 2],
                    evac_engine=("vector" if c % 2 == 0 else "scalar"))
        emit_ln_tile(p, env, Xi, tt, xin[tt % 2], xn_t, None, consume)

    emit_load_cast(p, env, lambda i: WrT.ap()[i * 128:(i + 1) * 128, :],
                   WrT.name, 8, 8, rw, 0, wstage)
    p.add("sync", lambda e: e.dma_start(brrow[:], Br.ap()[0:1, :]),
          reads=[Br.name], writes=[brrow.name], dma=brrow.name)
    p.add("vector", lambda e: e.tensor_copy(brhl[:, 0:8], brrow[:]),
          reads=[brrow.name], writes=[brhl.name + ":h"])
    p.add("vector", lambda e: e.tensor_copy(brt[:], brhl[:, 0:8]),
          reads=[brhl.name + ":h"], writes=[brt.name])
    p.add("vector", lambda e: e.tensor_sub(brt[:], brrow[:], brt[:]),
          reads=[brrow.name, brt.name], writes=[brt.name])
    p.add("vector", lambda e: e.tensor_copy(brhl[:, 8:16], brt[:]),
          reads=[brt.name], writes=[brhl.name + ":l"])
    for tt in range(NT):
        for fc in range(FC):
            p.add("tensor", lambda e, tt=tt, fc=fc: e.matmul(
                pR[:, tt * 8:(tt + 1) * 8],
                xn_fm[:, fc * T + tt * 128: fc * T + (tt + 1) * 128],
                rw[:, fc * 8:(fc + 1) * 8], start=(fc == 0), stop=False),
                reads=[f"{xn_fm.name}:{tt}", rw.name], writes=[pR.name])
        p.add("tensor", lambda e, tt=tt: e.matmul(
            pR[:, tt * 8:(tt + 1) * 8], ones_bf[:], brhl[:, 0:8],
            start=False, stop=False),
            reads=[ones_bf.name, brhl.name + ":h"], writes=[pR.name])
        p.add("tensor", lambda e, tt=tt: e.matmul(
            pR[:, tt * 8:(tt + 1) * 8], ones_bf[:], brhl[:, 8:16],
            start=False, stop=True),
            reads=[ones_bf.name, brhl.name + ":l"], writes=[pR.name])
    p.add("vector", lambda e: e.tensor_copy(lg[:], pR[:, 0:256]),
          reads=[pR.name], writes=[lg.name])
    e3 = lambda t: t[:].rearrange("p (a b) -> p a b", b=8)
    bc = lambda t, col0: t[:, col0:col0 + 32].unsqueeze(2).broadcast_to([128, 32, 8])
    p.add("vector", lambda e: e.tensor_reduce(
        st2[:, 0:32].rearrange("p (a b) -> p a b", b=1), e3(lg), AX.X, ALU.max),
        reads=[lg.name], writes=[st2.name + ":m1"])
    p.add("vector", lambda e: e.tensor_tensor(e3(t1), e3(lg), bc(st2, 0), ALU.is_ge),
          reads=[lg.name, st2.name + ":m1"], writes=[t1.name])
    p.add("vector", lambda e: e.scalar_tensor_tensor(
        e3(t2), e3(t1), NEG_BIG, e3(lg), ALU.mult, ALU.add),
        reads=[t1.name, lg.name], writes=[t2.name])
    p.add("vector", lambda e: e.tensor_reduce(
        st2[:, 32:64].rearrange("p (a b) -> p a b", b=1), e3(t2), AX.X, ALU.max),
        reads=[t2.name], writes=[st2.name + ":m2"])
    p.add("vector", lambda e: e.tensor_tensor(e3(t1), e3(lg), bc(st2, 0), ALU.subtract),
          reads=[lg.name, st2.name + ":m1"], writes=[t1.name])
    p.add("scalar", lambda e: e.activation(ev[:], t1[:], AF.Exp),
          reads=[t1.name], writes=[ev.name])
    p.add("vector", lambda e: e.tensor_sub(st2[:, 64:96], st2[:, 32:64], st2[:, 0:32]),
          reads=[st2.name + ":m2", st2.name + ":m1"], writes=[st2.name + ":dm"])
    p.add("scalar", lambda e: e.activation(st2[:, 64:96], st2[:, 64:96], AF.Exp),
          reads=[st2.name + ":dm"], writes=[st2.name + ":ed"])
    p.add("vector", lambda e: e.tensor_scalar(st2[:, 64:96], st2[:, 64:96], 1.0,
                                              None, ALU.add),
          reads=[st2.name + ":ed"], writes=[st2.name + ":dn"])
    p.add("vector", lambda e: e.reciprocal(st2[:, 64:96], st2[:, 64:96]),
          reads=[st2.name + ":dn"], writes=[st2.name + ":idn"])
    p.add("vector", lambda e: e.tensor_tensor(e3(t2), e3(lg), bc(st2, 32), ALU.is_ge),
          reads=[lg.name, st2.name + ":m2"], writes=[t2.name])
    p.add("vector", lambda e: e.tensor_tensor(e3(t1), e3(ev), e3(t2), ALU.mult),
          reads=[ev.name, t2.name], writes=[t1.name])
    p.add("vector", lambda e: e.tensor_tensor(e3(comb), e3(t1), bc(st2, 64), ALU.mult),
          reads=[t1.name, st2.name + ":idn"], writes=[comb.name])
    p.add("vector", lambda e: e.tensor_copy(combb[:], comb[:]),
          reads=[comb.name], writes=[combb.name])

    p.add("sync", lambda e: e.dma_start(wstage[0][0:8, 0:1024], B2.ap()),
          reads=[B2.name], writes=[wstage[0].name], dma=wstage[0].name)
    p.add("vector", lambda e: e.tensor_copy(b2bf[:], wstage[0][0:8, 0:1024]),
          reads=[wstage[0].name], writes=[b2bf.name])
    p.region_switch(lg.name, obuf[0].name)
    p.region_switch(ev.name, obuf[0].name)
    p.region_switch(t1.name, obuf[0].name)
    p.region_switch(t2.name, obuf[0].name)
    for tt in range(NT):
        p.add("tensor", lambda e, tt=tt: e.transpose(
            pT[0][0:8, 0:128], combb[:, tt * 8:(tt + 1) * 8], ident[:]),
            reads=[combb.name, ident.name], writes=[pT[0].name])
        p.add("vector", lambda e: e.tensor_copy(combT[:], pT[0][0:8, 0:128]),
              reads=[pT[0].name], writes=[combT.name])
        p.add("sync", lambda e, tt=tt: e.dma_start(
            xin[tt % 2][:], Xi.ap()[tt * 128:(tt + 1) * 128, :]),
            reads=[Xi.name], writes=[xin[tt % 2].name], dma=xin[tt % 2].name)
        for hf in range(2):
            p.add("tensor", lambda e, hf=hf: e.matmul(
                pC[:, 0:512] if hf == 0 else pC[:, 0:512], combT[:],
                b2bf[:, hf * 512: hf * 512 + 512], start=True, stop=True),
                reads=[combT.name, b2bf.name], writes=[pC.name])
            p.add("vector", lambda e, tt=tt, hf=hf: e.tensor_add(
                obuf[tt % 2][:, hf * 512: hf * 512 + 512], pC[:, 0:512],
                xin[tt % 2][:, hf * 512: hf * 512 + 512]),
                reads=[pC.name, xin[tt % 2].name], writes=[obuf[tt % 2].name])
        p.add("gpsimd", lambda e, tt=tt: e.dma_start(
            Xo.ap()[tt * 128:(tt + 1) * 128, :], obuf[tt % 2][:]),
            reads=[obuf[tt % 2].name, f"d_Xo:{tt}"], writes=[f"d_Xo:{tt}"],
            dma=f"d_ac{tt % 8}")

    p.region_switch(sq.name, act.name)
    p.region_switch(xn_t2[0].name, act.name)
    p.region_switch(xn_t2[1].name, act.name)
    p.region_switch(xin[0].name, w1stage[2].name)
    p.region_switch(xin[1].name, w1stage[3].name)
    p.region_switch(b2bf.name, ssb[0].name)
    p.region_switch(b2bf.name, ssb[1].name)
    for ei in range(8):
        def load_w1(mc, ei=ei):
            sa, sg = w1stage[2 * (mc % 2)], w1stage[2 * (mc % 2) + 1]
            wa, wg1 = w1a[mc % 2], w1g[mc % 2]
            p.add("sync", lambda e, mc=mc, sa=sa, ei=ei: e.dma_start(
                sa[:].rearrange("p (c j) -> p c j", j=128),
                W1.ap()[ei, :, mc * 128:(mc + 1) * 128]
                .rearrange("(c p) j -> p c j", p=128)),
                reads=[W1.name], writes=[sa.name], dma=sa.name)
            p.add("vector", lambda e, sa=sa, wa=wa: e.tensor_copy(wa[:], sa[:]),
                  reads=[sa.name], writes=[wa.name])
            p.add("sync", lambda e, mc=mc, sg=sg, ei=ei: e.dma_start(
                sg[:].rearrange("p (c j) -> p c j", j=128),
                W1.ap()[ei, :, 2048 + mc * 128: 2048 + (mc + 1) * 128]
                .rearrange("(c p) j -> p c j", p=128)),
                reads=[W1.name], writes=[sg.name], dma=sg.name)
            p.add("vector", lambda e, sg=sg, wg1=wg1: e.tensor_copy(wg1[:], sg[:]),
                  reads=[sg.name], writes=[wg1.name])

        load_w1(0)
        p.add("sync", lambda e, ei=ei: e.dma_start(b1c[:], B1c.ap()[ei]),
              reads=[B1c.name], writes=[b1c.name], dma=b1c.name)
        for i in range(16):
            stg = wstage[i % 2]
            p.add("gpsimd", lambda e, i=i, stg=stg, ei=ei: e.dma_start(
                stg[:, 0:1024], W2.ap()[ei, i * 128:(i + 1) * 128, :]),
                reads=[W2.name], writes=[stg.name], dma="d_w2g")
            p.add("vector", lambda e, i=i, stg=stg: e.tensor_copy(
                w2_bf[:, i * 1024:(i + 1) * 1024], stg[:, 0:1024]),
                reads=[stg.name], writes=[w2_bf.name])

        for half in range(2):
            for mc in range(16):
                if D_W1DB:
                    if mc < 15:
                        load_w1(mc + 1)
                    elif half == 0:
                        load_w1(0)
                else:
                    if not (half == 0 and mc == 0):
                        load_w1(mc)
                wa, wg1 = w1a[mc % 2], w1g[mc % 2]
                for tg in range(4):
                    tcol = half * 2048 + tg * 512
                    pAt = q4[D_PAR and tg % 2]
                    pGt = q4[2 + (D_PAR and tg % 2)]
                    sst = ssb[D_PAR and tg % 2]
                    gst = gsb[D_PAR and tg % 2]
                    for fc in range(FC):
                        p.add("tensor", lambda e, fc=fc, tcol=tcol, wa=wa, pAt=pAt: e.matmul(
                            pAt[:, 0:512], wa[:, fc * 128:(fc + 1) * 128],
                            xn_fm[:, fc * T + tcol: fc * T + tcol + 512],
                            start=(fc == 0), stop=(fc == 7)),
                            reads=[wa.name] + [f"{xn_fm.name}:{tcol // 128 + j}" for j in range(4)],
                            writes=[pAt.name])
                    for fc in range(FC):
                        p.add("tensor", lambda e, fc=fc, tcol=tcol, wg1=wg1, pGt=pGt: e.matmul(
                            pGt[:, 0:512], wg1[:, fc * 128:(fc + 1) * 128],
                            xn_fm[:, fc * T + tcol: fc * T + tcol + 512],
                            start=(fc == 0), stop=(fc == 7)),
                            reads=[wg1.name] + [f"{xn_fm.name}:{tcol // 128 + j}" for j in range(4)],
                            writes=[pGt.name])
                    p.add("scalar", lambda e, mc=mc, pAt=pAt, sst=sst: e.activation(
                        sst[:], pAt[:, 0:512], AF.Silu, bias=b1c[:, mc:mc + 1]),
                        reads=[pAt.name, b1c.name], writes=[sst.name])
                    p.add("vector", lambda e, mc=mc, pGt=pGt, gst=gst: e.tensor_scalar(
                        gst[:], pGt[:, 0:512], b1c[:, 16 + mc:17 + mc], None, ALU.add),
                        reads=[pGt.name, b1c.name], writes=[gst.name])
                    p.add("vector", lambda e, mc=mc, tg=tg, sst=sst, gst=gst: e.tensor_tensor(
                        act[:, mc * 2048 + tg * 512: mc * 2048 + (tg + 1) * 512],
                        sst[:], gst[:], ALU.mult),
                        reads=[sst.name, gst.name], writes=[act.name])
            if DEBUG_D and ei == 0 and half == 0:
                dbgd = env["dbgd"]
                p.add("gpsimd", lambda e: e.dma_start(dbgd.ap()[0:128, :], act[:]),
                      reads=[act.name], writes=[dbgd.name], dma="dbgd")
                dbgx = env["dbgx"]
                p.add("gpsimd", lambda e: e.dma_start(dbgx.ap()[0:128, :], xn_fm[:]),
                      reads=[xn_fm.name], writes=[dbgx.name], dma="dbgd")
            for tl in range(16):
                gt = half * 16 + tl
                pOt = pO if tl % 2 == 0 else [q4[0], q4[1]]
                for mc in range(16):
                    for hf in range(2):
                        p.add("tensor", lambda e, mc=mc, hf=hf, tl=tl, pOt=pOt: e.matmul(
                            pOt[hf][:, 0:512],
                            act[:, mc * 2048 + tl * 128: mc * 2048 + (tl + 1) * 128],
                            w2_bf[:, mc * 1024 + hf * 512: mc * 1024 + hf * 512 + 512],
                            start=(mc == 0), stop=(mc == 15)),
                            reads=[act.name, w2_bf.name], writes=[pOt[hf].name])
                ob = obuf[tl % 2]
                for hf in range(2):
                    p.add("vector", lambda e, gt=gt, ei=ei, ob=ob, hf=hf, pOt=pOt: e.tensor_scalar(
                        ob[:, hf * 512: hf * 512 + 512], pOt[hf][:, 0:512],
                        comb[:, gt * 8 + ei: gt * 8 + ei + 1],
                        None, ALU.mult),
                        reads=[pOt[hf].name, comb.name], writes=[ob.name])
                p.add("gpsimd", lambda e, gt=gt, ob=ob: e.dma_start(
                    Xo.ap()[gt * 128:(gt + 1) * 128, :], ob[:], accum_op=ALU.add),
                    reads=[ob.name, f"d_Xo:{gt}"], writes=[f"d_Xo:{gt}"],
                    dma=f"d_ac{gt % 8}")



def make_env(nc, need_mask=True):
    ident_f = nc.dram_tensor("c_ident", [128, 128], F32, kind="ExternalInput")
    ones_f = nc.dram_tensor("c_ones", [1, 128], F32, kind="ExternalInput")
    st_id = nc.alloc_sbuf_tensor("s_identf", [128, 128], F32)
    st_on = nc.alloc_sbuf_tensor("s_onesf", [1, 128], F32)
    s_ident = nc.alloc_sbuf_tensor("s_ident", [128, 128], BF16)
    s_ones = nc.alloc_sbuf_tensor("s_ones", [1, 128], BF16)
    eps_col = nc.alloc_sbuf_tensor("s_eps", [128, 1], F32)
    env = {"nc": nc, "ident_bf": s_ident, "ones_bf": s_ones, "eps_col": eps_col,
           "ident_f": st_id, "ones_f": st_on}
    if need_mask:
        env["mask4"] = nc.alloc_sbuf_tensor("s_mask4", [128, 2048], BF16)
        env["z512"] = nc.alloc_sbuf_tensor("s_z512", [128, 512], BF16)
    env["psum_q"] = [nc.alloc_psum_tensor(f"ps_q{i}", [128, 512], F32) for i in range(4)]
    env["psum_t"] = [nc.alloc_psum_tensor(f"ps_t{i}", [128, 1024], BF16) for i in range(2)]
    env["psum_o"] = [nc.alloc_psum_tensor(f"ps_o{i}", [128, 512], F32) for i in range(2)]
    return env, [(st_id, ident_f, s_ident), (st_on, ones_f, s_ones)]


def load_consts(p, env, pairs):
    for stg, dr, sb in pairs:
        p.add("sync", lambda e, stg=stg, dr=dr: e.dma_start(stg[:], dr.ap()),
              reads=[dr.name], writes=[stg.name], dma=stg.name)
        p.add("vector", lambda e, stg=stg, sb=sb: e.tensor_copy(sb[:], stg[:]),
              reads=[stg.name], writes=[sb.name])
    eps = env["eps_col"]
    p.add("vector", lambda e: e.memset(eps[:], 1e-5), writes=[eps.name])
    if "mask4" in env:
        mask4 = env["mask4"]
        z512 = env["z512"]
        p.add("vector", lambda e: e.memset(z512[:], 0.0), writes=[z512.name])
        for v in range(4):
            p.add("gpsimd", lambda e, v=v: e.affine_select(
                mask4[:, v * 512:(v + 1) * 512], z512[:], [[-1, 512]],
                mybir.AluOpType.is_ge, NEG_BIG, base=v * 128, channel_multiplier=1),
                reads=[z512.name], writes=[mask4.name])


def const_inputs():
    return {
        "c_ident": np.eye(128, dtype=np.float32),
        "c_ones": np.ones((1, 128), dtype=np.float32),
    }


def mk_arena(nc, reserve=1248):
    base = (SBUF_TOP - nc.sbuf_bytes_remaining + 31) & ~31
    size = SBUF_TOP - reserve - base
    nc.alloc_sbuf_tensor("arena", [128, SBUF_TOP - (SBUF_TOP - nc.sbuf_bytes_remaining) - reserve],
                         mybir.dt.uint8)
    return BAlloc(nc, base, size)


def build_prog_fused():
    nc = bass.Bass("TRN2", target_bir_lowering=False)
    Xi = nc.dram_tensor("Xi", [T, D], F32, kind="ExternalInput")
    Xo = nc.dram_tensor("Xo", [T, D], F32, kind="ExternalOutput")
    X1 = nc.dram_tensor("X1", [T, D], F32)
    X2 = nc.dram_tensor("X2", [T, D], F32)
    X3 = nc.dram_tensor("X3", [T, D], F32)
    Xmid = nc.dram_tensor("Xmid", [T, D], F32)
    XnH = nc.dram_tensor("XnH", [T, D], BF16)
    IlS = nc.dram_tensor("IlS", [32, 128], F32)
    aWqT = nc.dram_tensor("aWqT", [D, D], F32, kind="ExternalInput")
    aWkT = nc.dram_tensor("aWkT", [D, D], F32, kind="ExternalInput")
    aWv = nc.dram_tensor("aWv", [D, D], F32, kind="ExternalInput")
    bWr = nc.dram_tensor("bWr", [D, 16], F32, kind="ExternalInput")
    bWgs = nc.dram_tensor("bWgs", [2, D, 512], F32, kind="ExternalInput")
    bWus = nc.dram_tensor("bWus", [2, D, 512], F32, kind="ExternalInput")
    bWds = nc.dram_tensor("bWds", [2, 512, D], F32, kind="ExternalInput")
    bWgr = nc.dram_tensor("bWgr", [16, D, 256], F32, kind="ExternalInput")
    bWur = nc.dram_tensor("bWur", [16, D, 256], F32, kind="ExternalInput")
    bWdr = nc.dram_tensor("bWdr", [16, 256, D], F32, kind="ExternalInput")
    cWqT = nc.dram_tensor("cWqT", [D, D], F32, kind="ExternalInput")
    cWkT = nc.dram_tensor("cWkT", [D, D], F32, kind="ExternalInput")
    cWvT = nc.dram_tensor("cWvT", [D, D], F32, kind="ExternalInput")
    cWpT = nc.dram_tensor("cWpT", [D, D], F32, kind="ExternalInput")
    cBq = nc.dram_tensor("cBq", [1, D], F32, kind="ExternalInput")
    cBk = nc.dram_tensor("cBk", [1, D], F32, kind="ExternalInput")
    cBv = nc.dram_tensor("cBv", [1, D], F32, kind="ExternalInput")
    cBp = nc.dram_tensor("cBp", [1, D], F32, kind="ExternalInput")
    CosT = nc.dram_tensor("CosT", [T, 32], F32, kind="ExternalInput")
    SinT = nc.dram_tensor("SinT", [T, 32], F32, kind="ExternalInput")
    dW1 = nc.dram_tensor("dW1", [8, D, 4096], F32, kind="ExternalInput")
    dB1c = nc.dram_tensor("dB1c", [8, 128, 32], F32, kind="ExternalInput")
    dW2 = nc.dram_tensor("dW2", [8, 2048, D], F32, kind="ExternalInput")
    dB2 = nc.dram_tensor("dB2", [8, D], F32, kind="ExternalInput")
    dWrT = nc.dram_tensor("dWrT", [D, 8], F32, kind="ExternalInput")
    dBr = nc.dram_tensor("dBr", [1, 8], F32, kind="ExternalInput")

    env, pairs = make_env(nc)
    al = mk_arena(nc)
    p = Prog(nc)
    load_consts(p, env, pairs)
    build_block_a(nc, p, al, Xi, X1, aWqT, aWkT, aWv, env)
    p.barrier(); al.reset()
    build_block_b(nc, p, al, X1, X2, bWr, bWgs, bWus, bWds, bWgr, bWur, bWdr, env)
    p.barrier(); al.reset()
    build_block_c(nc, p, al, X2, Xmid, X3, cWqT, cWkT, cWvT, cWpT, cBq, cBk, cBv,
                  cBp, CosT, SinT, XnH, IlS, env)
    p.barrier(); al.reset()
    build_block_d(nc, p, al, X3, Xo, dW1, dB1c, dW2, dB2, dWrT, dBr, env)
    p.barrier()
    p.finalize_and_emit()
    return nc


def build_prog_a():
    nc = bass.Bass("TRN2", target_bir_lowering=False)
    Xi = nc.dram_tensor("Xi", [T, D], F32, kind="ExternalInput")
    Xo = nc.dram_tensor("Xo", [T, D], F32, kind="ExternalOutput")
    WqT = nc.dram_tensor("WqT", [D, D], F32, kind="ExternalInput")
    WkT = nc.dram_tensor("WkT", [D, D], F32, kind="ExternalInput")
    Wv = nc.dram_tensor("Wv", [D, D], F32, kind="ExternalInput")
    env, pairs = make_env(nc)
    al = mk_arena(nc)
    p = Prog(nc)
    load_consts(p, env, pairs)
    build_block_a(nc, p, al, Xi, Xo, WqT, WkT, Wv, env)
    p.barrier()
    p.finalize_and_emit()
    return nc


def build_prog_b():
    nc = bass.Bass("TRN2", target_bir_lowering=False)
    Xi = nc.dram_tensor("Xi", [T, D], F32, kind="ExternalInput")
    Xo = nc.dram_tensor("Xo", [T, D], F32, kind="ExternalOutput")
    Wr = nc.dram_tensor("Wr", [D, 16], F32, kind="ExternalInput")
    Wgs = nc.dram_tensor("Wgs", [2, D, 512], F32, kind="ExternalInput")
    Wus = nc.dram_tensor("Wus", [2, D, 512], F32, kind="ExternalInput")
    Wds = nc.dram_tensor("Wds", [2, 512, D], F32, kind="ExternalInput")
    Wgr = nc.dram_tensor("Wgr", [16, D, 256], F32, kind="ExternalInput")
    Wur = nc.dram_tensor("Wur", [16, D, 256], F32, kind="ExternalInput")
    Wdr = nc.dram_tensor("Wdr", [16, 256, D], F32, kind="ExternalInput")
    env, pairs = make_env(nc, need_mask=False)
    al = mk_arena(nc)
    p = Prog(nc)
    load_consts(p, env, pairs)
    build_block_b(nc, p, al, Xi, Xo, Wr, Wgs, Wus, Wds, Wgr, Wur, Wdr, env)
    p.barrier()
    p.finalize_and_emit()
    return nc


def build_prog_c():
    nc = bass.Bass("TRN2", target_bir_lowering=False)
    Xi = nc.dram_tensor("Xi", [T, D], F32, kind="ExternalInput")
    Xo = nc.dram_tensor("Xo", [T, D], F32, kind="ExternalOutput")
    Xmid = nc.dram_tensor("Xmid", [T, D], F32,
                          kind=("ExternalOutput" if DEBUG_C else "Internal"))
    XnH = nc.dram_tensor("XnH", [T, D], BF16)
    IlS = nc.dram_tensor("IlS", [32, 128], F32)
    WqT = nc.dram_tensor("WqT", [D, D], F32, kind="ExternalInput")
    WkT = nc.dram_tensor("WkT", [D, D], F32, kind="ExternalInput")
    WvT = nc.dram_tensor("WvT", [D, D], F32, kind="ExternalInput")
    WpT = nc.dram_tensor("WpT", [D, D], F32, kind="ExternalInput")
    Bq = nc.dram_tensor("Bq", [1, D], F32, kind="ExternalInput")
    Bk = nc.dram_tensor("Bk", [1, D], F32, kind="ExternalInput")
    Bv = nc.dram_tensor("Bv", [1, D], F32, kind="ExternalInput")
    Bp = nc.dram_tensor("Bp", [1, D], F32, kind="ExternalInput")
    CosT = nc.dram_tensor("CosT", [T, 32], F32, kind="ExternalInput")
    SinT = nc.dram_tensor("SinT", [T, 32], F32, kind="ExternalInput")
    env, pairs = make_env(nc)
    if DEBUG_C:
        env["dbg"] = nc.dram_tensor("Dbg", [1024, 2048], F32, kind="ExternalOutput")
    al = mk_arena(nc)
    p = Prog(nc)
    load_consts(p, env, pairs)
    build_block_c(nc, p, al, Xi, Xmid, Xo, WqT, WkT, WvT, WpT, Bq, Bk, Bv, Bp,
                  CosT, SinT, XnH, IlS, env)
    p.barrier()
    p.finalize_and_emit()
    return nc


def build_prog_d():
    nc = bass.Bass("TRN2", target_bir_lowering=False)
    Xi = nc.dram_tensor("Xi", [T, D], F32, kind="ExternalInput")
    Xo = nc.dram_tensor("Xo", [T, D], F32, kind="ExternalOutput")
    W1 = nc.dram_tensor("W1", [8, D, 4096], F32, kind="ExternalInput")
    B1c = nc.dram_tensor("B1c", [8, 128, 32], F32, kind="ExternalInput")
    W2 = nc.dram_tensor("W2", [8, 2048, D], F32, kind="ExternalInput")
    B2 = nc.dram_tensor("B2", [8, D], F32, kind="ExternalInput")
    WrT = nc.dram_tensor("WrT", [D, 8], F32, kind="ExternalInput")
    Br = nc.dram_tensor("Br", [1, 8], F32, kind="ExternalInput")
    env, pairs = make_env(nc, need_mask=False)
    if DEBUG_D:
        env["dbgd"] = nc.dram_tensor("DbgD", [128, 32768], mybir.dt.bfloat16,
                                     kind="ExternalOutput")
        env["dbgx"] = nc.dram_tensor("DbgX", [128, 32768], mybir.dt.bfloat16,
                                     kind="ExternalOutput")
    al = mk_arena(nc)
    p = Prog(nc)
    load_consts(p, env, pairs)
    build_block_d(nc, p, al, Xi, Xo, W1, B1c, W2, B2, WrT, Br, env)
    p.barrier()
    p.finalize_and_emit()
    return nc


def rope_tables():
    half = 32
    i = np.arange(half, dtype=np.float64)
    alpha = 1.0 / 10000.0 ** (2.0 * i / 64.0)
    ang = np.arange(T, dtype=np.float64)[:, None] * alpha[None, :]
    return (np.cos(ang).astype(np.float32), np.sin(ang).astype(np.float32))


_PROG_CACHE = {}


def _prog(name, builder):
    if name not in _PROG_CACHE:
        _PROG_CACHE[name] = builder()
    return _PROG_CACHE[name]


def _run(nc, per_core, shared):
    in_maps = []
    for b in range(8):
        m = dict(shared)
        m.update({k: v[b] for k, v in per_core.items()})
        in_maps.append(m)
    res = run_bass_kernel_spmd(nc, in_maps, core_ids=list(range(8)))
    return np.stack([np.asarray(r["Xo"], dtype=np.float32) for r in res.results])


def kernel(**inputs):
    f32c = lambda a: np.ascontiguousarray(np.asarray(a), dtype=np.float32)
    x = f32c(inputs["L0_attn_x"])
    cos_t, sin_t = rope_tables()
    b1 = f32c(inputs["L1_ffn_b1"])
    b1c = np.concatenate(
        [b1[:, :2048].reshape(8, 16, 128).transpose(0, 2, 1),
         b1[:, 2048:].reshape(8, 16, 128).transpose(0, 2, 1)], axis=2)
    sh = dict(const_inputs())
    sh["aWqT"] = f32c(inputs["L0_attn_w_q"].T)
    sh["aWkT"] = f32c(inputs["L0_attn_w_k"].T)
    sh["aWv"] = f32c(inputs["L0_attn_w_v"])
    sh["bWr"] = f32c(inputs["L0_ffn_router_w"])
    sh["bWgs"] = f32c(inputs["L0_ffn_wg_s"])
    sh["bWus"] = f32c(inputs["L0_ffn_wu_s"])
    sh["bWds"] = f32c(inputs["L0_ffn_wd_s"])
    sh["bWgr"] = f32c(inputs["L0_ffn_wg_r"])
    sh["bWur"] = f32c(inputs["L0_ffn_wu_r"])
    sh["bWdr"] = f32c(inputs["L0_ffn_wd_r"])
    sh["cWqT"] = f32c(inputs["L1_attn_wq"].T)
    sh["cWkT"] = f32c(inputs["L1_attn_wk"].T)
    sh["cWvT"] = f32c(inputs["L1_attn_wv"].T)
    sh["cWpT"] = f32c(inputs["L1_attn_wp"].T)
    sh["cBq"] = f32c(inputs["L1_attn_bq"]).reshape(1, D)
    sh["cBk"] = f32c(inputs["L1_attn_bk"]).reshape(1, D)
    sh["cBv"] = f32c(inputs["L1_attn_bv"]).reshape(1, D)
    sh["cBp"] = f32c(inputs["L1_attn_bp"]).reshape(1, D)
    sh["CosT"] = cos_t
    sh["SinT"] = sin_t
    sh["dW1"] = f32c(inputs["L1_ffn_W1"])
    sh["dB1c"] = np.ascontiguousarray(b1c)
    sh["dW2"] = f32c(inputs["L1_ffn_W2"])
    sh["dB2"] = f32c(inputs["L1_ffn_b2"])
    sh["dWrT"] = f32c(inputs["L1_ffn_Wr"])
    sh["dBr"] = f32c(inputs["L1_ffn_br"]).reshape(1, 8)
    return _run(_prog("fused", build_prog_fused), {"Xi": x}, sh)
